# version 11
# baseline (speedup 1.0000x reference)
import contextlib
import numpy as np
import concourse.bass as bass
import concourse.mybir as mybir
from concourse.bass_utils import run_bass_kernel_spmd

F32 = mybir.dt.float32
BF16 = mybir.dt.float16
AF = mybir.ActivationFunctionType
ALU = mybir.AluOpType

T = 4096
D = 1024
NCORE = 8
TCH = 512

ENGINES = ("tensor", "vector", "scalar", "gpsimd", "sync")


class _Op:
    __slots__ = ("oid", "engine", "fn", "deps", "is_dma", "inc_num", "vc", "tag", "dma_slot")

    def __init__(self, oid, engine, fn, is_dma, tag):
        self.oid = oid
        self.engine = engine
        self.fn = fn
        self.deps = []
        self.is_dma = is_dma
        self.inc_num = None
        self.vc = {}
        self.tag = tag


class Prog:
    def __init__(self):
        self.ops = {e: [] for e in ENGINES}
        self.all_ops = []
        self.last_writer = {}
        self.readers = {}
        self.pending_barrier = {}

    def emit(self, engine, fn, reads=(), writes=(), dma=False, tag=""):
        op = _Op(len(self.all_ops), engine, fn, dma, tag)
        deps = {}
        if engine in self.pending_barrier:
            for d in self.pending_barrier.pop(engine):
                deps[d.oid] = d
        for k in reads:
            w = self.last_writer.get(k)
            if w is not None:
                deps[w.oid] = w
        for k in writes:
            w = self.last_writer.get(k)
            if w is not None:
                deps[w.oid] = w
            for r in self.readers.get(k, ()):
                deps[r.oid] = r
        for d in deps.values():
            if d is op:
                continue
            if (d.engine == engine and d.engine == "tensor"
                    and not d.is_dma and not dma):
                continue
            op.deps.append(d)
        self.all_ops.append(op)
        self.ops[engine].append(op)
        for k in reads:
            self.readers.setdefault(k, []).append(op)
        for k in writes:
            self.last_writer[k] = op
            self.readers[k] = [op]
        return op

    def barrier(self):
        lasts = [self.ops[e][-1] for e in ENGINES if self.ops[e]]
        self.pending_barrier = {e: list(lasts) for e in ENGINES}

    NSLOT = 32

    def finalize_and_emit(self, nc):
        for op in self.all_ops:
            for d in op.deps:
                d.inc_num = -1
        counters = {}
        slot_uses = [0] * self.NSLOT
        next_slot = [0]
        for op in self.all_ops:
            if op.is_dma:
                sl = next_slot[0]
                next_slot[0] = (sl + 1) % self.NSLOT
                slot_uses[sl] += 1
                op.dma_slot = sl
                op.inc_num = slot_uses[sl]
            elif op.inc_num is not None:
                key = (op.engine, False)
                counters[key] = counters.get(key, 0) + 1
                op.inc_num = counters[key]

        def skey(op):
            return ("slot", op.dma_slot) if op.is_dma else (op.engine, False)

        def merge(a, b):
            for k, v in b.items():
                if a.get(k, 0) < v:
                    a[k] = v
        waits = {}
        know = {e: {} for e in ENGINES}
        n_waits = 0
        for op in self.all_ops:
            k = know[op.engine]
            need = {}
            for d in op.deps:
                merge(need, d.vc)
                key = skey(d)
                if d.inc_num is not None and need.get(key, 0) < d.inc_num:
                    need[key] = d.inc_num
            if op.is_dma and op.inc_num > 1:
                key = ("slot", op.dma_slot)
                if need.get(key, 0) < op.inc_num - 1:
                    need[key] = op.inc_num - 1
            wl = [(key, val) for key, val in need.items() if k.get(key, 0) < val]
            merge(k, need)
            vc = dict(need)
            if op.inc_num is not None and op.inc_num > 0:
                key = skey(op)
                if vc.get(key, 0) < op.inc_num:
                    vc[key] = op.inc_num
            op.vc = vc
            waits[op.oid] = wl
            n_waits += len(wl)
        n_incs = sum(1 for op in self.all_ops
                     if op.inc_num is not None and not op.is_dma)
        n_dma = sum(1 for op in self.all_ops if op.is_dma)
        print(f"Prog: {len(self.all_ops)} ops ({n_dma} dma), {n_incs} incs, "
              f"{n_waits} waits")

        with contextlib.ExitStack() as st:
            sems = {}
            for e in ENGINES:
                sems[(e, False)] = st.enter_context(nc.semaphore(f"p_{e}"))
            for i in range(self.NSLOT):
                sems[("slot", i)] = st.enter_context(nc.semaphore(f"ds{i}"))
            block = st.enter_context(nc.Block())
            prog = self

            def run_engine(engobj, ename):
                for op in prog.ops[ename]:
                    for (key, val) in waits[op.oid]:
                        engobj.wait_ge(sems[key], val * (16 if key[0] == "slot" else 1))
                    inst = op.fn(engobj)
                    if op.is_dma:
                        inst.then_inc(sems[("slot", op.dma_slot)], 16)
                    elif op.inc_num is not None:
                        if inst is None:
                            raise RuntimeError(f"op '{op.tag}' needs inc, fn returned None")
                        inst.then_inc(sems[(op.engine, False)], 1)

            block.tensor(lambda t: run_engine(t, "tensor"))
            block.vector(lambda v: run_engine(v, "vector"))
            block.scalar(lambda s: run_engine(s, "scalar"))
            block.gpsimd(lambda g: run_engine(g, "gpsimd"))
            block.sync(lambda s: run_engine(s, "sync"))



def _cc(a):
    return np.ascontiguousarray(a)


def host_prep(inputs):
    d = {}
    H, HD = 16, 64
    perm = np.concatenate([np.arange(h * HD, (h + 1) * HD, 2) for h in range(H)]
                          + [np.arange(h * HD + 1, (h + 1) * HD, 2) for h in range(H)])
    def _pi(h):
        return (h % 4) % 2
    def _cb(h):
        return h // 4 + 4 * ((h % 4) // 2)
    sigma = [0] * 16
    for h in range(16):
        sigma[_pi(h) * 8 + _cb(h)] = h
    d["aWq_t"] = _cc(np.asarray(inputs["L0_attn_a_Wq"])[perm, :].T)
    d["aWk_t"] = _cc(np.asarray(inputs["L0_attn_a_Wk"])[perm, :].T)
    d["aWv_t"] = _cc(np.asarray(inputs["L0_attn_a_Wv"]).T)
    d["aWg_t"] = _cc(np.asarray(inputs["L0_attn_a_Wg"]).T)
    d["aWo_t"] = _cc(np.asarray(inputs["L0_attn_a_Wo"]).T)
    d["aWb_t"] = _cc(np.asarray(inputs["L0_attn_a_Wb"]).T[:, sigma])
    d["aWgk_t"] = _cc(np.asarray(inputs["L0_attn_a_Wgk"]).T[:, sigma])
    cw = np.asarray(inputs["L0_attn_a_cqw"])[perm, 0, :]
    d["a_cqw"] = _cc(cw.reshape(8, 128, 4).transpose(1, 0, 2).reshape(128, 32))
    d["a_cqb"] = _cc(np.asarray(inputs["L0_attn_a_cqb"])[perm].reshape(8, 128).T)
    cw = np.asarray(inputs["L0_attn_a_ckw"])[perm, 0, :]
    d["a_ckw"] = _cc(cw.reshape(8, 128, 4).transpose(1, 0, 2).reshape(128, 32))
    d["a_ckb"] = _cc(np.asarray(inputs["L0_attn_a_ckb"])[perm].reshape(8, 128).T)
    cw = np.asarray(inputs["L0_attn_a_cvw"])[:, 0, :]
    d["a_cvw"] = _cc(cw.reshape(8, 128, 4).transpose(1, 0, 2).reshape(128, 32))
    d["a_cvb"] = _cc(np.asarray(inputs["L0_attn_a_cvb"]).reshape(8, 128).T)
    d["a_bb"] = _cc(np.asarray(inputs["L0_attn_a_bb"])[sigma].reshape(2, 8).T)
    d["a_bgk_dt"] = _cc((np.asarray(inputs["L0_attn_a_bgk"])
                         + np.asarray(inputs["L0_attn_a_dt_bias"]))[sigma].reshape(2, 8).T)
    d["a_Alog"] = _cc(np.asarray(inputs["L0_attn_a_A_log"])[sigma].reshape(2, 8).T)
    C = 64
    i_ar = np.arange(C)
    mlow = np.where(i_ar[:, None] > i_ar[None, :], 0.0, -1e30).astype(np.float32)
    mup = np.where(i_ar[:, None] < i_ar[None, :], 0.0, -1e30).astype(np.float32)
    d["c_maskhpL"] = _cc(np.tile(mlow, (2, 8)))
    d["c_maskhpU"] = _cc(np.tile(mup, (2, 8)))
    dphp = np.zeros((128, 8), np.float32)
    onhp = np.zeros((128, 1), np.float32)
    on_w = np.asarray(inputs["L0_attn_a_onorm_w"])
    for h in range(16):
        dphp[64 * _pi(h):64 * _pi(h) + 64, _cb(h)] = np.asarray(inputs["L0_attn_a_Dp"])[h]
    onhp[0:64, 0] = on_w
    onhp[64:128, 0] = on_w
    d["a_Dphp"] = _cc(dphp)
    d["a_onormc"] = _cc(onhp)
    d["c_blk8"] = _cc(np.kron(np.eye(8, dtype=np.float32), np.ones((1, 64), np.float32)))
    d["c_eyehp"] = _cc(np.tile(np.eye(64, dtype=np.float32), (2, 8)))
    d["c_lind4"] = _cc(np.eye(4, dtype=np.float32).repeat(32, axis=0))
    d["c_lind4T"] = _cc(np.eye(4, dtype=np.float32).repeat(32, axis=1))
    d["c_lind2"] = _cc(np.eye(2, dtype=np.float32).repeat(64, axis=0))
    d["c_lind2T"] = _cc(np.eye(2, dtype=np.float32).repeat(64, axis=1))
    inv_freq = 1.0 / 10000.0 ** (np.arange(0, HD, 2, dtype=np.float64) / HD)
    fsel = inv_freq[(2 * np.arange(32)) % 32]
    fr = np.outer(fsel, np.arange(T, dtype=np.float64))
    d["rope_cos"] = _cc(np.tile(np.cos(fr).astype(np.float32), (4, 1)))
    d["rope_sin"] = _cc(np.tile(np.sin(fr).astype(np.float32), (4, 1)))

    d["bwq_t"] = _cc(np.asarray(inputs["L0_attn_b_wq"]).T)
    d["bwk_t"] = _cc(np.asarray(inputs["L0_attn_b_wk"]).T)
    d["bwv_t"] = _cc(np.asarray(inputs["L0_attn_b_wv"]).T)
    d["bwo_t"] = _cc(np.asarray(inputs["L0_attn_b_wo"]).T)
    d["b_bqc"] = _cc(np.asarray(inputs["L0_attn_b_bq"]).reshape(8, 128).T)
    d["b_bkc"] = _cc(np.asarray(inputs["L0_attn_b_bk"]).reshape(8, 128).T)
    d["b_bvr"] = _cc(np.asarray(inputs["L0_attn_b_bv"]).reshape(1, 1024))
    d["b_boc"] = _cc(np.asarray(inputs["L0_attn_b_bo"]).reshape(8, 128).T)
    d["b_pw1_t"] = _cc(np.transpose(np.asarray(inputs["L0_attn_b_pw1"]), (0, 2, 1)))
    d["b_pb1_t"] = _cc(np.asarray(inputs["L0_attn_b_pb1"]).reshape(3, 4, 128)
                       .transpose(2, 0, 1).reshape(128, 12))
    d["b_pw2_t"] = _cc(np.transpose(np.asarray(inputs["L0_attn_b_pw2"]), (0, 2, 1)))
    d["b_pb2_t"] = _cc(np.asarray(inputs["L0_attn_b_pb2"]).T)
    d["b_sw1_t"] = _cc(np.asarray(inputs["L0_attn_b_sw1"]).T)
    d["b_sb1_t"] = _cc(np.asarray(inputs["L0_attn_b_sb1"]).reshape(4, 128).T)
    d["b_sw2_t"] = _cc(np.asarray(inputs["L0_attn_b_sw2"]).T)
    d["b_sb2_t"] = _cc(np.asarray(inputs["L0_attn_b_sb2"]).reshape(3, 1))
    km = np.arange(128)
    qm = np.arange(512)
    d["c_att_mask"] = np.stack([
        np.where((r * 128 + km)[:, None] > qm[None, :], -1e30, 0.0).astype(np.float32)
        for r in range(4)])

    d["fWg"] = _cc(np.asarray(inputs["L0_ffn_Wg"]))
    d["fbg"] = _cc(np.asarray(inputs["L0_ffn_bg"]).reshape(32, 128).T)
    d["fWu"] = _cc(np.asarray(inputs["L0_ffn_Wu"]))
    d["fbu"] = _cc(np.asarray(inputs["L0_ffn_bu"]).reshape(32, 128).T)
    d["fWd"] = _cc(np.asarray(inputs["L0_ffn_Wd"]))
    d["fbd"] = _cc(np.asarray(inputs["L0_ffn_bd"]).reshape(8, 128).T)
    d["fWq"] = _cc(np.asarray(inputs["L0_ffn_Wq"]))
    d["fkeys_t"] = _cc(np.transpose(np.asarray(inputs["L0_ffn_keys"]), (2, 0, 3, 1)))
    d["f_de_t"] = _cc(np.asarray(inputs["L0_ffn_down_embed"]).T)
    d["f_ue"] = _cc(np.asarray(inputs["L0_ffn_up_embed"]))

    d["c_ident"] = np.eye(128, dtype=np.float32)
    ind_split = np.zeros((8, 128, 16), dtype=np.float32)
    for i in range(4):
        for r in range(128):
            ind_split[i, r, 4 * i + r // 32] = 1.0
            ind_split[4 + i, r, 4 * i + r // 32] = 1.0
    d["c_ind_split"] = ind_split
    ind_hc = np.zeros((8, 128, 16), dtype=np.float32)
    for j in range(8):
        for r in range(128):
            ind_hc[j, r, 2 * j + r // 64] = 1.0
    d["c_ind_hc"] = ind_hc
    d["c_ind_hc_T"] = _cc(np.transpose(ind_hc, (0, 2, 1)))
    C = 64
    i_ar = np.arange(C)
    d["c_mask_strictlow"] = np.where(i_ar[:, None] > i_ar[None, :], 0.0, -1e30).astype(np.float32)
    d["c_mask_strictup"] = np.where(i_ar[:, None] < i_ar[None, :], 0.0, -1e30).astype(np.float32)
    d["c_eye64"] = np.eye(64, dtype=np.float32)
    return d



class Ctx:
    def __init__(self, nc, P, stack):
        self.nc = nc
        self.P = P
        self.stack = stack
        self.dram = {}
        self.sb = {}
        self.uid = 0

    def u(self, base):
        self.uid += 1
        return f"{base}{self.uid}"

    def sbuf(self, name, shape, dtype, stack=None):
        t = (stack or self.stack).enter_context(self.nc.sbuf_tensor(name, shape, dtype))
        self.sb[name] = t
        return t

    def load_const(self, name, dtype=BF16, stack=None, shape=None):
        ap = self.dram[name]
        shp = list(shape if shape is not None else ap.shape)
        if len(shp) == 1:
            shp = [1] + shp
        t = self.sbuf(f"sb_{name}", shp, dtype, stack=stack)
        eng = "gpsimd" if dtype != F32 else "sync"
        src = ap if len(ap.shape) > 1 else ap.rearrange("n -> 1 n")
        self.P.emit(eng, lambda e: e.dma_start(out=t[:], in_=src),
                    writes=[f"sb_{name}"], dma=True, tag=f"ld_{name}")
        return t



class LNHelper:

    def __init__(self, cx, cols, name, stack=None):
        self.cx = cx
        self.cols = cols
        self.n = name
        cx.sbuf(f"{name}_xb", [128, 8 * cols], BF16, stack=stack)
        cx.sbuf(f"{name}_st", [1, 2 * cols], F32, stack=stack)
        cx.sbuf(f"{name}_tmp", [128, cols], F32, stack=stack)


    def emit(self, x_keys, xsb, xn_dst_ap, xn_keys, psA="pb6", psB="pb7"):
        cx, P, sb, cols, n = self.cx, self.cx.P, self.cx.sb, self.cols, self.n
        xb, st, tmp = sb[f"{n}_xb"], sb[f"{n}_st"], sb[f"{n}_tmp"]
        xnb = xb
        x2b = xb
        rk = list(x_keys)
        P.emit("vector", lambda v: v.tensor_copy(xb[:], xsb),
               reads=rk, writes=[f"{n}_xb"], tag=f"{n}cast")

        def mk_mm(src, bank):
            def f(t):
                last = None
                for i in range(8):
                    last = t.matmul(sb[bank][0:1, 0:cols], sb["onescol"][:],
                                    src[:, i * cols:(i + 1) * cols],
                                    start=(i == 0), stop=(i == 7))
                return last
            return f
        P.emit("tensor", mk_mm(xb, psA), reads=[f"{n}_xb", "onescol"],
               writes=[psA], tag=f"{n}mm1")
        P.emit("scalar", lambda s: s.activation(x2b[:], xsb, AF.Square),
               reads=rk + [f"{n}_xb"], writes=[f"{n}_xb"], tag=f"{n}sq")
        P.emit("tensor", mk_mm(x2b, psB), reads=[f"{n}_xb", "onescol"],
               writes=[psB], tag=f"{n}mm2")

        def f_mu(v):
            v.tensor_scalar(st[0:1, 0:cols], sb[psA][0:1, 0:cols], 1.0 / D, None, ALU.mult)
            return v.tensor_scalar(st[0:1, cols:], sb[psB][0:1, 0:cols],
                                   1.0 / D, None, ALU.mult)
        P.emit("vector", f_mu, reads=[psA, psB], writes=[f"{n}_st"], tag=f"{n}mu")

        P.emit("vector", lambda v: v.tensor_mul(tmp[0:1, 0:cols], st[0:1, 0:cols],
                                                st[0:1, 0:cols]),
               reads=[f"{n}_st"], writes=[f"{n}_tmp"], tag=f"{n}musq")
        P.emit("vector", lambda v: v.tensor_sub(st[0:1, cols:], st[0:1, cols:],
                                                tmp[0:1, 0:cols]),
               reads=[f"{n}_tmp"], writes=[f"{n}_st"], tag=f"{n}var")

        P.emit("scalar", lambda s: s.activation(st[0:1, cols:], st[0:1, cols:],
                                                AF.Ln, bias=sb["epscol"][0:1, 0:1]),
               reads=[f"{n}_st", "epscol"], writes=[f"{n}_st"], tag=f"{n}ln")
        P.emit("scalar", lambda s: s.activation(st[0:1, cols:], st[0:1, cols:],
                                                AF.Exp, scale=-0.5),
               reads=[f"{n}_st"], writes=[f"{n}_st"], tag=f"{n}rstd")

        def f_bc(t):
            t.matmul(sb[psA][:, 0:cols], sb["ones_row"][0:1, :], st[0:1, 0:cols],
                     start=True, stop=True)
            return t.matmul(sb[psB][:, 0:cols], sb["ones_row"][0:1, :],
                            st[0:1, cols:], start=True, stop=True)
        P.emit("tensor", f_bc, reads=[f"{n}_st", "ones_row"], writes=[psA, psB],
               tag=f"{n}bc")

        for i in range(8):
            P.emit("vector", lambda v, i=i: v.tensor_sub(
                tmp[:, 0:cols], xsb[:, i * cols:(i + 1) * cols], sb[psA][:, 0:cols]),
                reads=rk + [psA], writes=[f"{n}_tmp"], tag=f"{n}ns{i}")
            P.emit("vector", lambda v, i=i: v.tensor_mul(
                xnb[:, i * cols:(i + 1) * cols], tmp[:, 0:cols],
                sb[psB][:, 0:cols]),
                reads=[f"{n}_tmp", psB], writes=[f"{n}_xb"], tag=f"{n}nm{i}")
        if xn_dst_ap is not None:
            P.emit("sync", lambda e: e.dma_start(
                out=xn_dst_ap, in_=xnb[:].rearrange("p (a n) -> p a n", a=8)),
                reads=[f"{n}_xb"], writes=list(xn_keys), dma=True, tag=f"{n}st")
        return f"{n}_xb"



def emit_ln_input(cx):
    P = cx.P
    xin, xn0 = cx.dram["x_in"], cx.dram["xn0"]
    with contextlib.ExitStack() as stk:
        xbuf = [cx.sbuf(f"ph0_x{i}", [128, 8 * TCH], F32, stack=stk) for i in range(2)]
        ln = LNHelper(cx, TCH, "ph0ln", stack=stk)
        for c in range(T // TCH):
            b = c % 2
            sl = np.s_[c * TCH:(c + 1) * TCH]
            key = f"ph0x{b}"
            P.emit("sync", lambda e, sl=sl, b=b: e.dma_start(
                out=xbuf[b][:].rearrange("p (a n) -> p a n", a=8),
                in_=xin[:, sl].rearrange("(a p) n -> p a n", p=128)),
                writes=[key], dma=True, tag=f"ph0ld{c}")
            ln.emit([key], xbuf[b][:],
                    xn0[:, sl].rearrange("(a p) n -> p a n", p=128), [f"xn0:{c}"])



def emit_ffn(cx, x_in, xn_in, x_out, xn_out, idx):
    nc, P = cx.nc, cx.P
    pref = f"f{idx}"
    xnd = cx.dram[xn_in]
    xind = cx.dram[x_in]
    xoutd = cx.dram[x_out]
    xtmpd = cx.dram["xtmp"]
    wtdd = cx.dram["wtd"]
    NTCH = T // TCH

    def _part_a1():
        with contextlib.ExitStack() as stk:
            sbuf = lambda nm, shp, dt: cx.sbuf(f"{pref}{nm}", shp, dt, stack=stk)
            sb = cx.sb
            de_t = sbuf("de", [128, 8 * 4096], BF16)
            wq = sbuf("wq", [128, 8 * 512], BF16)
            keys = sbuf("keys", [128, 8 * 64], BF16)
            identb = cx.sb["identb"]
            for j in range(4):
                P.emit("gpsimd", lambda e, j=j: e.dma_start(
                    out=de_t[:, j * 8192:(j + 1) * 8192].rearrange("p (a n) -> p a n", a=2),
                    in_=cx.dram["f_de_t"][:, j * 1024:(j + 1) * 1024]
                        .rearrange("(a p) n -> p a n", p=128)),
                    writes=[f"{pref}de:{j}"], dma=True, tag=f"{pref}ldde{j}")
            P.emit("gpsimd", lambda e: e.dma_start(
                out=wq[:].rearrange("p (a n) -> p a n", a=8),
                in_=cx.dram["fWq"].rearrange("(a p) n -> p a n", p=128)),
                writes=[f"{pref}wq"], dma=True, tag=f"{pref}ldwq")
            P.emit("gpsimd", lambda e: e.dma_start(
                out=keys[0:64, :].rearrange("p (a n) -> p a n", a=8),
                in_=cx.dram["fkeys_t"].rearrange("p h n k -> n (p h) k")),
                writes=[f"{pref}keys"], dma=True, tag=f"{pref}ldk1")
            P.emit("gpsimd", lambda e: e.dma_start(
                out=keys[64:128, :].rearrange("p (a n) -> p a n", a=8),
                in_=cx.dram["fkeys_t"].rearrange("p h n k -> n (p h) k")),
                writes=[f"{pref}keys"], dma=True, tag=f"{pref}ldk2")

            xnc = [sbuf(f"xnc{i}", [128, 8 * TCH], BF16) for i in range(2)]
            q = sbuf("q", [128, 4 * TCH], BF16)
            sims = sbuf("sims", [128, 8 * 64], F32)
            top8 = sbuf("top8", [128, 8 * 8], F32)
            cmb = sbuf("cmb", [128, 64], F32)
            c8 = sbuf("c8", [128, 8], F32)
            colv = sbuf("colv", [128, 8], F32)
            zcol = sbuf("zcol", [128, 4], F32)
            efx = sbuf("efx", [128, 8 * 64], F32)
            smw = sbuf("smw", [128, 4096], BF16)
            hbuf = sbuf("hbuf", [128, 4096], BF16)
            dsil = sbuf("dsil", [128, 4096], BF16)
            wt = sbuf("wt", [128, 32 * 128], BF16)

            for c in range(NTCH):
                bi = c % 2
                sl = np.s_[c * TCH:(c + 1) * TCH]
                kxn = f"{pref}xnc{bi}"
                P.emit("sync", lambda e, sl=sl, bi=bi: e.dma_start(
                    out=xnc[bi][:].rearrange("p (a n) -> p a n", a=8),
                    in_=xnd[:, sl].rearrange("(a p) n -> p a n", p=128)),
                    reads=[f"{xn_in}:{c}"], writes=[kxn], dma=True, tag=f"{pref}ldxn{c}")
                for m in range(4):
                    def f_q(t, m=m, bi=bi):
                        last = None
                        for k in range(8):
                            last = t.matmul(sb["pb0"][:, 0:TCH],
                                            wq[:, k * 512 + m * 128: k * 512 + (m + 1) * 128],
                                            xnc[bi][:, k * TCH:(k + 1) * TCH],
                                            start=(k == 0), stop=(k == 7))
                        return last
                    P.emit("tensor", f_q, reads=[kxn, f"{pref}wq"], writes=["pb0"],
                           tag=f"{pref}qmm{c}_{m}")
                    P.emit("vector", lambda v, m=m: v.tensor_copy(
                        q[:, m * TCH:(m + 1) * TCH], sb["pb0"][:, 0:TCH]),
                        reads=["pb0"], writes=[f"{pref}q:{m}"], tag=f"{pref}qe{c}_{m}")
                for tb in range(TCH // 128):
                    def f_sims(t, tb=tb):
                        last = None
                        for p_ in range(2):
                            for h_ in range(4):
                                g = p_ * 4 + h_
                                slot = SLOT[(p_, h_)]
                                tile = 2 * p_ + h_ // 2
                                r0 = 64 * (h_ % 2)
                                bank = "pb1" if h_ % 2 == 0 else "pb2"
                                col = (slot % 4) * 64
                                last = t.matmul(
                                    sb[bank][:, col:col + 64],
                                    q[r0:r0 + 64,
                                      tile * TCH + tb * 128: tile * TCH + (tb + 1) * 128],
                                    keys[r0:r0 + 64, g * 64:(g + 1) * 64],
                                    start=True, stop=True,
                                    tile_position=(r0, 0))
                        return last
                    P.emit("tensor", f_sims, reads=[f"{pref}q:{m}" for m in range(4)]
                           + [f"{pref}keys"], writes=["pb1", "pb2"], tag=f"{pref}sims{c}_{tb}")
                    def f_sime(sc_):
                        sc_.activation(sims[:, 0:256], sb["pb1"][:, 0:256], AF.Copy)
                        return sc_.activation(sims[:, 256:512], sb["pb2"][:, 0:256],
                                              AF.Copy)
                    P.emit("scalar", f_sime, reads=["pb1", "pb2"], writes=[f"{pref}sims"],
                           tag=f"{pref}sime{c}_{tb}")
                    for s_ in range(8):
                        P.emit("vector", lambda v, s_=s_: v.max(
                            out=top8[:, s_ * 8:(s_ + 1) * 8], in_=sims[:, s_ * 64:(s_ + 1) * 64]),
                            reads=[f"{pref}sims"], writes=[f"{pref}top8"],
                            tag=f"{pref}max{c}_{tb}_{s_}")
                    def f_negth(v):
                        return v.tensor_scalar(
                            colv[:, 0:8],
                            top8[:].rearrange("p (s e) -> p s e", s=8)[:, :, 7],
                            -1.0, None, ALU.mult)
                    P.emit("vector", f_negth, reads=[f"{pref}top8"], writes=[f"{pref}colv"],
                           tag=f"{pref}negth{c}_{tb}")
                    for s_ in range(8):
                        def f_exp(sc, s_=s_):
                            return sc.activation(efx[:, s_ * 64:(s_ + 1) * 64],
                                                 sims[:, s_ * 64:(s_ + 1) * 64], AF.Exp,
                                                 bias=colv[:, s_:s_ + 1])
                        P.emit("scalar", f_exp, reads=[f"{pref}sims", f"{pref}colv"],
                               writes=[f"{pref}efx"], tag=f"{pref}fex{c}_{tb}_{s_}")
                        def f_mask(v, s_=s_):
                            return v.scalar_tensor_tensor(
                                efx[:, s_ * 64:(s_ + 1) * 64], sims[:, s_ * 64:(s_ + 1) * 64],
                                top8[:, s_ * 8 + 7: s_ * 8 + 8], efx[:, s_ * 64:(s_ + 1) * 64],
                                ALU.is_ge, ALU.mult)
                        P.emit("vector", f_mask, reads=[f"{pref}sims", f"{pref}top8",
                                                        f"{pref}efx"],
                               writes=[f"{pref}efx"], tag=f"{pref}fm{c}_{tb}_{s_}")
                    for h in range(4):
                        s_x = SLOT[(0, h)]
                        s_y = SLOT[(1, h)]
                        def f_cmb(v, s_x=s_x, s_y=s_y):
                            return v.tensor_tensor(
                                cmb[:].rearrange("p (a b) -> p a b", a=8),
                                top8[:, s_x * 8:(s_x + 1) * 8].unsqueeze(2)
                                    .to_broadcast([128, 8, 8]),
                                top8[:, s_y * 8:(s_y + 1) * 8].unsqueeze(1)
                                    .to_broadcast([128, 8, 8]),
                                ALU.add)
                        P.emit("vector", f_cmb, reads=[f"{pref}top8"], writes=[f"{pref}cmb"],
                               tag=f"{pref}cmb{c}_{tb}_{h}")
                        P.emit("vector", lambda v: v.max(out=c8[:], in_=cmb[:]),
                               reads=[f"{pref}cmb"], writes=[f"{pref}c8"],
                               tag=f"{pref}c8{c}_{tb}_{h}")
                        P.emit("vector", lambda v: v.tensor_scalar(
                            zcol[:, 0:1], c8[:, 7:8], -1.0, None, ALU.mult),
                            reads=[f"{pref}c8"], writes=[f"{pref}zcol"],
                            tag=f"{pref}nthc{c}_{tb}_{h}")
                        def f_ec8(sc):
                            return sc.activation(c8[:], c8[:], AF.Exp, bias=zcol[:, 0:1],
                                                 accum_out=zcol[:, 1:2])
                        P.emit("scalar", f_ec8, reads=[f"{pref}c8", f"{pref}zcol"],
                               writes=[f"{pref}c8", f"{pref}zcol"],
                               tag=f"{pref}ec8{c}_{tb}_{h}")
                        P.emit("vector", lambda v: v.reciprocal(zcol[:, 2:3], zcol[:, 1:2]),
                               reads=[f"{pref}zcol"], writes=[f"{pref}zcol"],
                               tag=f"{pref}rz{c}_{tb}_{h}")
                        def f_sumth(v, s_x=s_x, s_y=s_y):
                            return v.tensor_add(zcol[:, 3:4], colv[:, s_x:s_x + 1],
                                                colv[:, s_y:s_y + 1])
                        P.emit("vector", f_sumth, reads=[f"{pref}colv"],
                               writes=[f"{pref}zcol"], tag=f"{pref}sth{c}_{tb}_{h}")
                        def f_es(sc):
                            return sc.activation(zcol[:, 3:4], zcol[:, 3:4], AF.Exp,
                                                 scale=-1.0, bias=zcol[:, 0:1])
                        P.emit("scalar", f_es, reads=[f"{pref}zcol"], writes=[f"{pref}zcol"],
                               tag=f"{pref}esc{c}_{tb}_{h}")
                        def f_stau(v):
                            v.tensor_mul(zcol[:, 3:4], zcol[:, 3:4], zcol[:, 2:3])
                            return v.tensor_scalar(zcol[:, 1:2], zcol[:, 2:3],
                                                   1.0 - 1e-6, None, ALU.mult)
                        P.emit("vector", f_stau, reads=[f"{pref}zcol"],
                               writes=[f"{pref}zcol"], tag=f"{pref}stau{c}_{tb}_{h}")
                        def f_scale(v, s_x=s_x):
                            return v.tensor_scalar(efx[:, s_x * 64:(s_x + 1) * 64],
                                                   efx[:, s_x * 64:(s_x + 1) * 64],
                                                   zcol[:, 3:4], None, ALU.mult)
                        P.emit("vector", f_scale, reads=[f"{pref}zcol", f"{pref}efx"],
                               writes=[f"{pref}efx"], tag=f"{pref}fsc{c}_{tb}_{h}")
                        dst = smw if h == 0 else hbuf
                        dk = f"{pref}smw" if h == 0 else f"{pref}hbuf"
                        def f_outer(v, dst=dst, s_x=s_x, s_y=s_y):
                            return v.tensor_tensor(
                                dst[:].rearrange("p (a b) -> p a b", a=64),
                                efx[:, s_x * 64:(s_x + 1) * 64].unsqueeze(2)
                                    .to_broadcast([128, 64, 64]),
                                efx[:, s_y * 64:(s_y + 1) * 64].unsqueeze(1)
                                    .to_broadcast([128, 64, 64]),
                                ALU.mult)
                        P.emit("vector", f_outer, reads=[f"{pref}efx"], writes=[dk],
                               tag=f"{pref}out{c}_{tb}_{h}")
                        def f_thr(v, dst=dst):
                            return v.scalar_tensor_tensor(dst[:], dst[:], zcol[:, 1:2],
                                                          dst[:], ALU.is_ge, ALU.mult)
                        P.emit("vector", f_thr, reads=[f"{pref}zcol", dk], writes=[dk],
                               tag=f"{pref}thr{c}_{tb}_{h}")
                        if h > 0:
                            P.emit("vector", lambda v: v.tensor_add(smw[:], smw[:], hbuf[:]),
                                   reads=[f"{pref}hbuf", f"{pref}smw"], writes=[f"{pref}smw"],
                                   tag=f"{pref}acch{c}_{tb}_{h}")
                    for half in range(2):
                        def f_dots(t, tb=tb, half=half, bi=bi):
                            last = None
                            for nn in range(4):
                                for k in range(8):
                                    last = t.matmul(
                                        sb[f"pb{nn}"][:, 0:512],
                                        xnc[bi][:, k * TCH + tb * 128: k * TCH + (tb + 1) * 128],
                                        de_t[:, k * 4096 + half * 2048 + nn * 512:
                                             k * 4096 + half * 2048 + (nn + 1) * 512],
                                        start=(k == 0), stop=(k == 7))
                            return last
                        P.emit("tensor", f_dots, reads=[kxn] + [f"{pref}de:{j}" for j in range(4)],
                               writes=["pb0", "pb1", "pb2", "pb3"],
                               tag=f"{pref}dmm{c}_{tb}_{half}")
                        for nn in range(4):
                            def f_sil(s, nn=nn, half=half):
                                return s.activation(
                                    dsil[:, half * 2048 + nn * 512: half * 2048 + (nn + 1) * 512],
                                    sb[f"pb{nn}"][:, 0:512], AF.Silu)
                            P.emit("scalar", f_sil, reads=[f"pb{nn}"], writes=[f"{pref}dsil"],
                                   tag=f"{pref}sil{c}_{tb}_{half}_{nn}")
                    P.emit("vector", lambda v: v.tensor_mul(hbuf[:], dsil[:], smw[:]),
                           reads=[f"{pref}dsil", f"{pref}smw"], writes=[f"{pref}hbuf"],
                           tag=f"{pref}wt{c}_{tb}")
                    for grp in range(8):
                        def f_tr(t, grp=grp):
                            last = None
                            for j4 in range(4):
                                j = grp * 4 + j4
                                last = t.matmul(sb["pb4"][:, j4 * 128:(j4 + 1) * 128],
                                                hbuf[:, j * 128:(j + 1) * 128], identb[:],
                                                start=True, stop=True)
                            return last
                        P.emit("tensor", f_tr, reads=[f"{pref}hbuf", "identb"],
                               writes=["pb4"], tag=f"{pref}tr{c}_{tb}_{grp}")
                        P.emit("scalar", lambda sc_, grp=grp: sc_.activation(
                            wt[:, grp * 512:(grp + 1) * 512], sb["pb4"][:, 0:512],
                            AF.Copy),
                            reads=["pb4"], writes=[f"{pref}wtb"], tag=f"{pref}tre{c}_{tb}_{grp}")
                    def f_wst(e, c=c, tb=tb):
                        return e.dma_start(
                            out=wtdd[:, :, c * TCH + tb * 128: c * TCH + (tb + 1) * 128]
                                .rearrange("j p n -> p j n"),
                            in_=wt[:].rearrange("p (j n) -> p j n", j=32))
                    P.emit("sync", f_wst, reads=[f"{pref}wtb"],
                           writes=[f"wtd:{c}_{tb}"], dma=True, tag=f"{pref}wst{c}_{tb}")
            P.barrier()
    _part_a1()

    def _part_a2():
        with contextlib.ExitStack() as stk:
            sbuf = lambda nm, shp, dt: cx.sbuf(f"{pref}{nm}", shp, dt, stack=stk)
            sb = cx.sb
            ue = sbuf("ue", [128, 32 * 1024], BF16)
            wtl2 = [sbuf(f"wtl{i}", [128, 32 * 512], BF16) for i in range(2)]
            esacc = sbuf("esacc", [128, 8 * TCH], F32)
            xchunk = sbuf("xchunk", [128, 8 * TCH], F32)
            for j in range(4):
                P.emit("gpsimd", lambda e, j=j: e.dma_start(
                    out=ue[:, j * 8192:(j + 1) * 8192].rearrange("p (a n) -> p a n", a=8),
                    in_=cx.dram["f_ue"][j * 1024:(j + 1) * 1024, :]
                        .rearrange("(a p) n -> p a n", p=128)),
                    writes=[f"{pref}ue:{j}"], dma=True, tag=f"{pref}ldue{j}")
            for c in range(NTCH):
                sl = np.s_[c * TCH:(c + 1) * TCH]
                wtl = wtl2[c % 2]
                P.emit("sync", lambda e, c=c, wtl=wtl: e.dma_start(
                    out=wtl[:].rearrange("p (j n) -> p j n", j=32),
                    in_=wtdd[:, :, c * TCH:(c + 1) * TCH].rearrange("j p n -> p j n")),
                    reads=[f"wtd:{c}_{tb}" for tb in range(4)],
                    writes=[f"{pref}wtl{c % 2}"],
                    dma=True, tag=f"{pref}wld{c}")
                P.emit("sync", lambda e, sl=sl: e.dma_start(
                    out=xchunk[:].rearrange("p (a n) -> p a n", a=8),
                    in_=xind[:, sl].rearrange("(a p) n -> p a n", p=128)),
                    reads=[f"{x_in}:{c}"], writes=[f"{pref}xch"], dma=True,
                    tag=f"{pref}ldx{c}")
                for tb in range(4):
                    for mg in range(2):
                        def f_es(t, mg=mg, tb=tb, wtl=wtl):
                            last = None
                            for m4 in range(4):
                                m = mg * 4 + m4
                                for j in range(32):
                                    last = t.matmul(
                                        sb["pb5"][:, m4 * 128:(m4 + 1) * 128],
                                        ue[:, j * 1024 + m * 128: j * 1024 + (m + 1) * 128],
                                        wtl[:, j * 512 + tb * 128: j * 512 + (tb + 1) * 128],
                                        start=(j == 0), stop=(j == 31))
                            return last
                        P.emit("tensor", f_es, reads=[f"{pref}wtl{c % 2}"]
                               + [f"{pref}ue:{j}" for j in range(4)],
                               writes=["pb5"], tag=f"{pref}es{c}_{tb}_{mg}")
                        def f_ese(v, mg=mg, tb=tb):
                            last = None
                            for m4 in range(4):
                                m = mg * 4 + m4
                                last = v.tensor_add(
                                    xchunk[:, m * TCH + tb * 128: m * TCH + (tb + 1) * 128],
                                    xchunk[:, m * TCH + tb * 128: m * TCH + (tb + 1) * 128],
                                    sb["pb5"][:, m4 * 128:(m4 + 1) * 128])
                            return last
                        P.emit("vector", f_ese, reads=["pb5", f"{pref}xch"],
                               writes=[f"{pref}xch"], tag=f"{pref}ese{c}_{tb}_{mg}")
                P.emit("sync", lambda e, sl=sl: e.dma_start(
                    out=xtmpd[:, sl].rearrange("(a p) n -> p a n", p=128),
                    in_=xchunk[:].rearrange("p (a n) -> p a n", a=8)),
                    reads=[f"{pref}xch"], writes=[f"xtmp:{c}"], dma=True,
                    tag=f"{pref}stx{c}")
            P.barrier()
    _part_a2()

    NBLK = 4
    BW = 1024
    def _part_b():
        with contextlib.ExitStack() as stk:
            sbuf = lambda nm, shp, dt: cx.sbuf(f"{pref}{nm}", shp, dt, stack=stk)
            sb = cx.sb
            acc = sbuf("macc", [128, 8 * 512], F32)
            wgh = [sbuf(f"mwg{i}", [128, 8 * 512], BF16) for i in range(3)]
            wuh = [sbuf(f"mwu{i}", [128, 8 * 512], BF16) for i in range(3)]
            wdh = [sbuf(f"mwd{i}", [128, 8 * 512], BF16) for i in range(3)]
            xnc = [sbuf(f"mxn{i}", [128, 8 * TCH], BF16) for i in range(2)]
            hg2 = [sbuf(f"mhg{i}", [128, 8 * 512], BF16) for i in range(2)]
            hh2 = [sbuf(f"mh{i}", [128, 8 * 512], BF16) for i in range(2)]
            bgs = sbuf("mbg", [128, 32], F32)
            bus = sbuf("mbu", [128, 32], F32)
            bds = sbuf("mbd", [128, 8], F32)
            xck = sbuf("mxck", [128, 8 * TCH], F32)
            P.emit("sync", lambda e: e.dma_start(
                out=bgs[:], in_=cx.dram["fbg"]),
                writes=[f"{pref}mbg"], dma=True, tag=f"{pref}ldbg")
            P.emit("sync", lambda e: e.dma_start(
                out=bus[:], in_=cx.dram["fbu"]),
                writes=[f"{pref}mbu"], dma=True, tag=f"{pref}ldbu")
            P.emit("sync", lambda e: e.dma_start(
                out=bds[:], in_=cx.dram["fbd"]),
                writes=[f"{pref}mbd"], dma=True, tag=f"{pref}ldbd")
            ln = LNHelper(cx, TCH, f"{pref}mln", stack=stk) if xn_out is not None else None

            units = [(tq, blk) for tq in range(8) for blk in range(NBLK)]

            def emit_wloads(u, hfs=(0, 1)):
                blk = units[u][1]
                for hf in hfs:
                    par = (u * 2 + hf) % 3
                    c0 = blk * BW + hf * 512
                    P.emit("gpsimd", lambda e, c0=c0, par=par: e.dma_start(
                        out=wgh[par][:].rearrange("p (a n) -> p a n", a=8),
                        in_=cx.dram["fWg"][:, c0:c0 + 512]
                        .rearrange("(a p) n -> p a n", p=128)),
                        writes=[f"{pref}mwg{par}"], dma=True, tag=f"{pref}lwg{u}_{hf}")
                    P.emit("gpsimd", lambda e, c0=c0, par=par: e.dma_start(
                        out=wuh[par][:].rearrange("p (a n) -> p a n", a=8),
                        in_=cx.dram["fWu"][:, c0:c0 + 512]
                        .rearrange("(a p) n -> p a n", p=128)),
                        writes=[f"{pref}mwu{par}"], dma=True, tag=f"{pref}lwu{u}_{hf}")
                    P.emit("gpsimd", lambda e, blk=blk, hf=hf, par=par: e.dma_start(
                        out=wdh[par][:].rearrange("p (a n) -> p a n", a=8),
                        in_=cx.dram["fWd"][blk * BW:(blk + 1) * BW,
                                           hf * 512:(hf + 1) * 512]
                        .rearrange("(a p) n -> p a n", p=128)),
                        writes=[f"{pref}mwd{par}"], dma=True, tag=f"{pref}lwd{u}_{hf}")

            emit_wloads(0)
            for u, (tq, blk) in enumerate(units):
                bi = tq % 2
                hb = blk % 2
                kxn = f"{pref}mxn{bi}"
                if u == 0 or (blk == 1 and tq + 1 < 8):
                    ltq = tq if u == 0 else tq + 1
                    sl = np.s_[ltq * TCH:(ltq + 1) * TCH]
                    P.emit("sync", lambda e, sl=sl, ltq=ltq: e.dma_start(
                        out=xnc[ltq % 2][:].rearrange("p (a n) -> p a n", a=8),
                        in_=xnd[:, sl].rearrange("(a p) n -> p a n", p=128)),
                        reads=[f"{xn_in}:{ltq}"], writes=[f"{pref}mxn{ltq % 2}"],
                        dma=True, tag=f"{pref}mxl{ltq}")
                if u + 1 < len(units):
                    emit_wloads(u + 1, hfs=(0,))
                for m in range(8):
                    par = (u * 2 + m // 4) % 3
                    m4 = m % 4
                    pg = f"pb{m % 2}"
                    pu = f"pb{2 + m % 2}"

                    def f_g(t, m4=m4, par=par, pg=pg, bi=bi):
                        last = None
                        for k in range(8):
                            last = t.matmul(
                                sb[pg][:, 0:512],
                                wgh[par][:, k * 512 + m4 * 128: k * 512 + (m4 + 1) * 128],
                                xnc[bi][:, k * TCH:(k + 1) * TCH],
                                start=(k == 0), stop=(k == 7))
                        return last
                    P.emit("tensor", f_g, reads=[kxn, f"{pref}mwg{par}"], writes=[pg],
                           tag=f"{pref}gmm{u}_{m}")

                    def f_u(t, m4=m4, par=par, pu=pu, bi=bi):
                        last = None
                        for k in range(8):
                            last = t.matmul(
                                sb[pu][:, 0:512],
                                wuh[par][:, k * 512 + m4 * 128: k * 512 + (m4 + 1) * 128],
                                xnc[bi][:, k * TCH:(k + 1) * TCH],
                                start=(k == 0), stop=(k == 7))
                        return last
                    P.emit("tensor", f_u, reads=[kxn, f"{pref}mwu{par}"], writes=[pu],
                           tag=f"{pref}umm{u}_{m}")

                    def f_sg(s, m=m, pg=pg, blk=blk, hb=hb):
                        return s.activation(hg2[hb][:, m * 512:(m + 1) * 512],
                                            sb[pg][:, 0:512], AF.Silu,
                                            bias=bgs[:, blk * 8 + m: blk * 8 + m + 1])
                    P.emit("scalar", f_sg, reads=[pg, f"{pref}mbg"],
                           writes=[f"{pref}mhg{hb}"], tag=f"{pref}sg{u}_{m}")

                    def f_h(v, m=m, pu=pu, blk=blk, hb=hb):
                        return v.scalar_tensor_tensor(
                            hh2[hb][:, m * 512:(m + 1) * 512], sb[pu][:, 0:512],
                            bus[:, blk * 8 + m: blk * 8 + m + 1],
                            hg2[hb][:, m * 512:(m + 1) * 512], ALU.add, ALU.mult)
                    P.emit("vector", f_h, reads=[pu, f"{pref}mbu", f"{pref}mhg{hb}"],
                           writes=[f"{pref}mh{hb}"], tag=f"{pref}h{u}_{m}")
                for m in range(8):
                    par = (u * 2 + m // 4) % 3
                    m4 = m % 4
                    pd = f"pb{4 + m % 2}"

                    def f_d(t, m4=m4, par=par, pd=pd, hb=hb):
                        last = None
                        for k in range(8):
                            last = t.matmul(
                                sb[pd][:, 0:512],
                                wdh[par][:, k * 512 + m4 * 128: k * 512 + (m4 + 1) * 128],
                                hh2[hb][:, k * 512:(k + 1) * 512],
                                start=(k == 0), stop=(k == 7))
                        return last
                    P.emit("tensor", f_d, reads=[f"{pref}mh{hb}", f"{pref}mwd{par}"],
                           writes=[pd], tag=f"{pref}dmm{u}_{m}")
                    if blk == 0:
                        def f_a(v, m=m, pd=pd):
                            return v.tensor_copy(acc[:, m * 512:(m + 1) * 512],
                                                 sb[pd][:, 0:512])
                    else:
                        def f_a(v, m=m, pd=pd):
                            return v.tensor_add(acc[:, m * 512:(m + 1) * 512],
                                                acc[:, m * 512:(m + 1) * 512],
                                                sb[pd][:, 0:512])
                    P.emit("vector", f_a, reads=[pd, f"{pref}macc"],
                           writes=[f"{pref}macc"], tag=f"{pref}macc{u}_{m}")
                    if m == 3 and u + 1 < len(units):
                        emit_wloads(u + 1, hfs=(1,))
                if blk == NBLK - 1:
                    sl = np.s_[tq * TCH:(tq + 1) * TCH]
                    P.emit("sync", lambda e, sl=sl: e.dma_start(
                        out=xck[:].rearrange("p (a n) -> p a n", a=8),
                        in_=xtmpd[:, sl].rearrange("(a p) n -> p a n", p=128)),
                        reads=[f"xtmp:{tq}"], writes=[f"{pref}mxck"], dma=True,
                        tag=f"{pref}eplx{tq}")

                    def f_fin(v):
                        last = None
                        for m in range(8):
                            last = v.scalar_tensor_tensor(
                                xck[:, m * TCH:(m + 1) * TCH],
                                acc[:, m * 512:(m + 1) * 512],
                                bds[:, m:m + 1], xck[:, m * TCH:(m + 1) * TCH],
                                ALU.add, ALU.add)
                        return last
                    P.emit("vector", f_fin, reads=[f"{pref}macc", f"{pref}mbd",
                                                   f"{pref}mxck"],
                           writes=[f"{pref}mxck"], tag=f"{pref}fin{tq}")
                    P.emit("sync", lambda e, sl=sl: e.dma_start(
                        out=xoutd[:, sl].rearrange("(a p) n -> p a n", p=128),
                        in_=xck[:].rearrange("p (a n) -> p a n", a=8)),
                        reads=[f"{pref}mxck"], writes=[f"{x_out}:{tq}"], dma=True,
                        tag=f"{pref}stxo{tq}")
                    if xn_out is not None:
                        ln.emit([f"{pref}mxck"], xck[:],
                                cx.dram[xn_out][:, sl].rearrange("(a p) n -> p a n", p=128),
                                [f"{xn_out}:{tq}"])
            P.barrier()
    _part_b()


SLOT = {}
_s0 = 0
for _p in range(2):
    for _h in range(4):
        pass
def _fix_slots():
    for p in range(2):
        for h in range(4):
            if h % 2 == 0:
                SLOT[(p, h)] = p * 2 + h // 2
            else:
                SLOT[(p, h)] = 4 + p * 2 + (h - 1) // 2
_fix_slots()


def _api(h):
    return h % 2


def _acb(h):
    return h // 4 + 4 * ((h % 4) // 2)


def emit_attn_a(cx, x_in, xn_in, x_out, xn_out):
    nc, P = cx.nc, cx.P
    sb = cx.sb
    xnd = cx.dram[xn_in]
    xind = cx.dram[x_in]
    xoutd = cx.dram[x_out]
    identb = cx.sb["identb"]
    SC = 256
    NSC = T // SC

    def _sec_a():
      with contextlib.ExitStack() as stk:
        sbuf = lambda nm, shp, dt: cx.sbuf(f"a{nm}", shp, dt, stack=stk)

        for nm, key in (("wq", "aWq_t"), ("wk", "aWk_t"), ("wv", "aWv_t"),
                        ("wg", "aWg_t"), ("wo", "aWo_t")):
            t = sbuf(nm, [128, 8 * 1024], BF16)
            P.emit("gpsimd", lambda e, t=t, key=key: e.dma_start(
                out=t[:].rearrange("p (a n) -> p a n", a=8),
                in_=cx.dram[key].rearrange("(a p) n -> p a n", p=128)),
                writes=[f"a_{nm}"], dma=True, tag=f"ald{nm}")
        wbt = sbuf("wbt", [128, 8 * 16], BF16)
        P.emit("gpsimd", lambda e: e.dma_start(
            out=wbt[:].rearrange("p (a n) -> p a n", a=8),
            in_=cx.dram["aWb_t"].rearrange("(a p) n -> p a n", p=128)),
            writes=["a_wbt"], dma=True, tag="aldwbt")
        wgkt = sbuf("wgkt", [128, 8 * 16], BF16)
        P.emit("gpsimd", lambda e: e.dma_start(
            out=wgkt[:].rearrange("p (a n) -> p a n", a=8),
            in_=cx.dram["aWgk_t"].rearrange("(a p) n -> p a n", p=128)),
            writes=["a_wgkt"], dma=True, tag="aldwgkt")
        consts = {}
        for nm, key, dt_ in (("cqw", "a_cqw", F32), ("cqb", "a_cqb", F32),
                             ("ckw", "a_ckw", F32), ("ckb", "a_ckb", F32),
                             ("cvw", "a_cvw", F32), ("cvb", "a_cvb", F32),
                             ("bbv", "a_bb", F32), ("bgkdt", "a_bgk_dt", F32),
                             ("alog", "a_Alog", F32), ("dphp", "a_Dphp", F32),
                             ("onrm", "a_onormc", F32),
                             ("blk8", "c_blk8", BF16),
                             ("eyehp", "c_eyehp", BF16),
                             ("mskL", "c_maskhpL", BF16),
                             ("mskU", "c_maskhpU", BF16),
                             ("li4", "c_lind4", BF16), ("li4T", "c_lind4T", BF16)):
            tt = sbuf(nm, list(cx.dram[key].shape), dt_)
            consts[nm] = tt
            eng = "gpsimd" if dt_ != F32 else "sync"
            P.emit(eng, lambda e, tt=tt, key=key: e.dma_start(out=tt[:], in_=cx.dram[key]),
                   writes=[f"a_{nm}"], dma=True, tag=f"aldc{nm}")
        wq, wk, wv, wg, wo = (cx.sb["awq"], cx.sb["awk"], cx.sb["awv"],
                              cx.sb["awg"], cx.sb["awo"])
        cc_ = consts

        neal = sbuf("neal", [8, 2], F32)
        idphp = sbuf("idphp", [128, 512], BF16)
        ones8 = sbuf("ones8", [8, 64], BF16)
        negones8 = sbuf("negones8", [8, 64], BF16)
        onef = sbuf("onef", [128, 1], F32)
        e12 = sbuf("e12", [128, 1], F32)
        e6 = sbuf("e6", [128, 1], F32)
        P.emit("vector", lambda v: v.memset(ones8[:], 1.0), writes=["a_ones8"], tag="ao8")
        P.emit("vector", lambda v: v.memset(negones8[:], -1.0), writes=["a_no8"], tag="an8")
        P.emit("vector", lambda v: v.memset(onef[:], 1.0), writes=["a_onef"], tag="aof")
        P.emit("vector", lambda v: v.memset(e12[:], 1e-12), writes=["a_e12"], tag="ae12")
        P.emit("vector", lambda v: v.memset(e6[:], 1e-6), writes=["a_e6"], tag="ae6")
        P.emit("scalar", lambda s: s.activation(neal[:], cc_["alog"][:], AF.Exp),
               reads=["a_alog"], writes=["a_neal"], tag="aneal1")
        P.emit("vector", lambda v: v.tensor_scalar(neal[:], neal[:], -1.0, None, ALU.mult),
               reads=["a_neal"], writes=["a_neal"], tag="aneal2")
        P.emit("vector", lambda v: v.tensor_tensor(
            idphp[:].rearrange("p (a n) -> p a n", a=8),
            cc_["eyehp"][:].rearrange("p (a n) -> p a n", a=8),
            cc_["dphp"][:].unsqueeze(2).to_broadcast([128, 8, 64]), ALU.mult),
            reads=["a_eyehp", "a_dphp"], writes=["a_idphp"], tag="aidp")

        xnc = [sbuf(f"xnc{i}", [128, 8 * SC], BF16) for i in range(2)]
        csq = sbuf("csq", [128, 8 * SC], BF16)
        csk = sbuf("csk", [128, 8 * SC], BF16)
        qs = sbuf("qs", [128, 8 * SC], BF16)
        ks = sbuf("ks", [128, 8 * SC], BF16)
        vs = sbuf("vs", [128, 8 * SC], BF16)
        gbuf = sbuf("gbuf", [128, 8 * SC], BF16)
        halo = {q: sbuf(f"halo{q}", [128, 24], F32) for q in "qkv"}
        for q in "qkv":
            P.emit("vector", lambda v, q=q: v.memset(halo[q][:], 0.0),
                   writes=[f"a_halo{q}"], tag=f"ahz{q}")
        ctmp = sbuf("ctmp", [128, SC], F32)
        rc = sbuf("rc", [128, SC], F32)
        rs = sbuf("rs", [128, SC], F32)
        rt1 = sbuf("rt1", [128, SC], F32)
        rt2 = sbuf("rt2", [128, SC], F32)
        rt3 = sbuf("rt3", [128, SC], F32)
        rt4 = sbuf("rt4", [128, SC], F32)
        sqt = sbuf("sqt", [128, SC], BF16)
        rstd4 = sbuf("rstd4", [4, SC], F32)
        rstd4b = sbuf("rstd4b", [4, SC], BF16)
        bsc = sbuf("bsc", [8, 2 * SC], F32)
        lal = sbuf("lal", [8, 2 * SC], F32)
        t8a = sbuf("t8a", [8, SC], F32)
        t8b = sbuf("t8b", [8, SC], F32)
        zs8 = sbuf("zs8", [8, 64], F32)
        P.emit("vector", lambda v: v.memset(zs8[:], 0.0), writes=["a_zs8"], tag="azs8")
        gcum = sbuf("gcum", [8, 2 * 64], F32)
        gprev = sbuf("gprev", [8, 2 * 64], F32)
        gcb = sbuf("gcb", [8, 2 * 64], BF16)
        gpb = sbuf("gpb", [8, 2 * 64], BF16)
        negg = sbuf("negg", [8, 2 * 64], BF16)
        bcb = sbuf("bcb", [8, 2 * 64], BF16)
        gpe = sbuf("gpe", [8, 2 * 64], BF16)
        ggr = sbuf("ggr", [8, 2 * 64], F32)
        ggrb = sbuf("ggrb", [8, 2 * 64], BF16)
        gcc = sbuf("gcc", [8, 2], F32)
        gccb = sbuf("gccb", [8, 2], BF16)
        gcrow = sbuf("gcrow", [1, 2 * 8], BF16)
        blkx = sbuf("blkx", [8, 512], BF16)
        gphp = sbuf("gphp", [128, 8], F32)
        bhp = sbuf("bhp", [128, 8], F32)
        ghp = sbuf("ghp", [128, 8], F32)
        bghp = sbuf("bghp", [128, 8], F32)
        gcghp = sbuf("gcghp", [128, 8], F32)
        gchp = sbuf("gchp", [128, 8], BF16)
        gcim = sbuf("gcim", [128, 512], BF16)
        mats = {}
        for nm in ("kda", "qda", "qsa", "Aa", "HTa", "HTd", "B0", "B1", "B2",
                   "B3", "B4", "X1", "X2", "X3", "X4", "X5", "PP", "KTl",
                   "GRr", "VTs", "VTu", "UVm", "GGm", "E2m", "Sb"):
            mats[nm] = sbuf(nm, [128, 512], BF16)
        tmpw = sbuf("tmpw", [128, 512], F32)
        rf32 = tmpw
        sf32 = sbuf("sf32", [128, 512], F32)
        P.emit("vector", lambda v: v.memset(sf32[:], 0.0), writes=["a_sf32"],
               tag="aSfz")
        P.emit("vector", lambda v: v.memset(mats["Sb"][:], 0.0), writes=["a_Sb"],
               tag="aSz")

        ofm = sbuf("ofm", [128, 8 * SC], F32)
        og8 = csq
        tmps = sbuf("tmps", [128, SC], F32)
        tmpsb = sbuf("tmpsb", [128, SC], BF16)
        rstd2 = sbuf("rstd2", [2, 8 * SC], F32)
        rstd2b = sbuf("rstd2b", [2, 8 * SC], BF16)
        li2 = sbuf("li2", [128, 2], BF16)
        li2T = sbuf("li2T", [2, 128], BF16)
        P.emit("gpsimd", lambda e: e.dma_start(out=li2[:], in_=cx.dram["c_lind2"]),
               writes=["a_li2"], dma=True, tag="aldli2")
        P.emit("gpsimd", lambda e: e.dma_start(out=li2T[:], in_=cx.dram["c_lind2T"]),
               writes=["a_li2T"], dma=True, tag="aldli2T")
        xck = sbuf("xck", [128, 8 * SC], F32)
        ln = LNHelper(cx, SC, "alna", stack=stk) if xn_out is not None else None

        def slot(mat, h):
            p_, c_ = _api(h), _acb(h)
            return mat[64 * p_:64 * p_ + 64, 64 * c_:64 * c_ + 64]

        def idslice(h):
            p_ = _api(h)
            return identb[64 * p_:64 * p_ + 64, 64 * p_:64 * p_ + 64]

        for sc in range(NSC):
            bi = sc % 2
            sl = np.s_[sc * SC:(sc + 1) * SC]
            kxn = f"a_xnc{bi}"
            P.emit("sync", lambda e, sl=sl, bi=bi: e.dma_start(
                out=xnc[bi][:].rearrange("p (a n) -> p a n", a=8),
                in_=xnd[:, sl].rearrange("(a p) n -> p a n", p=128)),
                reads=[f"{xn_in}:{sc // 2}"], writes=[kxn], dma=True, tag=f"alx{sc}")
            P.emit("sync", lambda e, sl=sl: e.dma_start(out=rc[:], in_=cx.dram["rope_cos"][:, sl]),
                   writes=["a_rc"], dma=True, tag=f"alrc{sc}")
            P.emit("sync", lambda e, sl=sl: e.dma_start(out=rs[:], in_=cx.dram["rope_sin"][:, sl]),
                   writes=["a_rs"], dma=True, tag=f"alrs{sc}")
            for qty, wmat, wkey, cwn, cbn, dstb, dstk in (
                    ("q", wq, "a_wq", "cqw", "cqb", csq, "a_csq"),
                    ("k", wk, "a_wk", "ckw", "ckb", csk, "a_csk"),
                    ("v", wv, "a_wv", "cvw", "cvb", vs, "a_vs")):
                for m in range(8):
                    bank = f"pb{m % 4}"
                    def f_p(t, m=m, bi=bi, wmat=wmat, bank=bank):
                        last = None
                        for k in range(8):
                            last = t.matmul(
                                sb[bank][:, 0:SC],
                                wmat[:, k * 1024 + m * 128: k * 1024 + (m + 1) * 128],
                                xnc[bi][:, k * SC:(k + 1) * SC],
                                start=(k == 0), stop=(k == 7))
                        return last
                    P.emit("tensor", f_p, reads=[kxn, wkey], writes=[bank],
                           tag=f"apm{sc}{qty}{m}")
                    cw = cc_[cwn]
                    cb = cc_[cbn]
                    hb = halo[qty]
                    def f_c0(v, m=m, bank=bank, cw=cw):
                        return v.tensor_scalar(ctmp[:], sb[bank][:, 0:SC],
                                               cw[:, m * 4 + 3: m * 4 + 4], None,
                                               ALU.mult)
                    P.emit("vector", f_c0, reads=[bank, f"a_{cwn}"], writes=["a_ctmp"],
                           tag=f"ac0{sc}{qty}{m}")
                    for s_ in range(3):
                        w_i = 2 - s_
                        sh = 3 - w_i
                        def f_cs(v, m=m, bank=bank, cw=cw, w_i=w_i, sh=sh):
                            return v.scalar_tensor_tensor(
                                ctmp[:, sh:SC], sb[bank][:, 0:SC - sh],
                                cw[:, m * 4 + w_i: m * 4 + w_i + 1],
                                ctmp[:, sh:SC], ALU.mult, ALU.add)
                        P.emit("vector", f_cs, reads=[bank, f"a_{cwn}", "a_ctmp"],
                               writes=["a_ctmp"], tag=f"acs{sc}{qty}{m}_{s_}")
                        def f_cb(v, m=m, cw=cw, w_i=w_i, sh=sh, hb=hb):
                            return v.scalar_tensor_tensor(
                                ctmp[:, 0:sh], hb[:, m * 3 + (3 - sh):m * 3 + 3],
                                cw[:, m * 4 + w_i: m * 4 + w_i + 1],
                                ctmp[:, 0:sh], ALU.mult, ALU.add)
                        P.emit("vector", f_cb, reads=[f"a_halo{qty}", f"a_{cwn}",
                                                      "a_ctmp"],
                               writes=["a_ctmp"], tag=f"acb{sc}{qty}{m}_{s_}")
                    P.emit("vector", lambda v, m=m, bank=bank, hb=hb: v.tensor_copy(
                        hb[:, m * 3:m * 3 + 3], sb[bank][:, SC - 3:SC]),
                        reads=[bank], writes=[f"a_halo{qty}"], tag=f"ahs{sc}{qty}{m}")
                    def f_si(s_, m=m, dstb=dstb, cb=cb):
                        return s_.activation(dstb[:, m * SC:(m + 1) * SC], ctmp[:],
                                             AF.Silu, bias=cb[:, m:m + 1])
                    P.emit("scalar", f_si, reads=["a_ctmp", f"a_{cbn}"], writes=[dstk],
                           tag=f"asi{sc}{qty}{m}")
            for m in range(8):
                bank = f"pb{m % 4}"
                def f_g(t, m=m, bi=bi, bank=bank):
                    last = None
                    for k in range(8):
                        last = t.matmul(
                            sb[bank][:, 0:SC],
                            wg[:, k * 1024 + m * 128: k * 1024 + (m + 1) * 128],
                            xnc[bi][:, k * SC:(k + 1) * SC],
                            start=(k == 0), stop=(k == 7))
                    return last
                P.emit("tensor", f_g, reads=[kxn, "a_wg"], writes=[bank],
                       tag=f"agm{sc}{m}")
                P.emit("vector", lambda v, m=m, bank=bank: v.tensor_copy(
                    gbuf[:, m * SC:(m + 1) * SC], sb[bank][:, 0:SC]),
                    reads=[bank], writes=["a_gbuf"], tag=f"age{sc}{m}")
            for pi_ in range(2):
                def f_bm(t, pi_=pi_, bi=bi):
                    last = None
                    for k in range(8):
                        last = t.matmul(
                            sb["pb4"][0:8, 0:SC],
                            wbt[:, k * 16 + 8 * pi_: k * 16 + 8 * pi_ + 8],
                            xnc[bi][:, k * SC:(k + 1) * SC],
                            start=(k == 0), stop=(k == 7))
                    return last
                P.emit("tensor", f_bm, reads=[kxn, "a_wbt"], writes=["pb4"],
                       tag=f"abm{sc}_{pi_}")
                def f_bs(s_, pi_=pi_):
                    return s_.activation(bsc[:, pi_ * SC:(pi_ + 1) * SC],
                                         sb["pb4"][0:8, 0:SC], AF.Sigmoid,
                                         bias=cc_["bbv"][0:8, pi_:pi_ + 1])
                P.emit("scalar", f_bs, reads=["pb4", "a_bbv"], writes=["a_bsc"],
                       tag=f"abs{sc}_{pi_}")
                def f_am(t, pi_=pi_, bi=bi):
                    last = None
                    for k in range(8):
                        last = t.matmul(
                            sb["pb5"][0:8, 0:SC],
                            wgkt[:, k * 16 + 8 * pi_: k * 16 + 8 * pi_ + 8],
                            xnc[bi][:, k * SC:(k + 1) * SC],
                            start=(k == 0), stop=(k == 7))
                    return last
                P.emit("tensor", f_am, reads=[kxn, "a_wgkt"], writes=["pb5"],
                       tag=f"aam{sc}_{pi_}")
                def f_a1(s_, pi_=pi_):
                    return s_.activation(t8a[:], sb["pb5"][0:8, 0:SC], AF.Abs,
                                         bias=cc_["bgkdt"][0:8, pi_:pi_ + 1])
                P.emit("scalar", f_a1, reads=["pb5", "a_bgkdt"], writes=["a_t8a"],
                       tag=f"aa1{sc}_{pi_}")
                P.emit("scalar", lambda s_: s_.activation(t8b[:], t8a[:], AF.Exp,
                                                          scale=-1.0),
                       reads=["a_t8a"], writes=["a_t8b"], tag=f"aa2{sc}_{pi_}")
                P.emit("scalar", lambda s_: s_.activation(t8b[:], t8b[:], AF.Ln,
                                                          bias=onef[0:8, 0:1]),
                       reads=["a_t8b", "a_onef"], writes=["a_t8b"], tag=f"aa3{sc}_{pi_}")
                def f_a4(v, pi_=pi_):
                    return v.tensor_scalar(t8a[:], sb["pb5"][0:8, 0:SC],
                                           cc_["bgkdt"][0:8, pi_:pi_ + 1],
                                           0.0, ALU.add, ALU.max)
                P.emit("vector", f_a4, reads=["pb5", "a_bgkdt"], writes=["a_t8a"],
                       tag=f"aa4{sc}_{pi_}")
                P.emit("vector", lambda v: v.tensor_add(t8a[:], t8a[:], t8b[:]),
                       reads=["a_t8a", "a_t8b"], writes=["a_t8a"], tag=f"aa5{sc}_{pi_}")
                def f_a6(v, pi_=pi_):
                    return v.tensor_scalar(lal[:, pi_ * SC:(pi_ + 1) * SC], t8a[:],
                                           neal[0:8, pi_:pi_ + 1], None,
                                           ALU.mult)
                P.emit("vector", f_a6, reads=["a_t8a", "a_neal"], writes=["a_lal"],
                       tag=f"aa6{sc}_{pi_}")
            for qty, src, dstb, dkey, skey_ in (("q", csq, qs, "a_qs", "a_csq"),
                                                ("k", csk, ks, "a_ks", "a_csk")):
                for i in range(4):
                    A1 = src[:, i * SC:(i + 1) * SC]
                    A2 = src[:, (4 + i) * SC:(5 + i) * SC]
                    P.emit("vector", lambda v, A1=A1: v.tensor_mul(rt1[:], A1, rc[:]),
                           reads=[skey_, "a_rc"], writes=["a_rt1"], tag=f"ar1{sc}{qty}{i}")
                    P.emit("vector", lambda v, A2=A2: v.tensor_mul(rt2[:], A2, rs[:]),
                           reads=[skey_, "a_rs"], writes=["a_rt2"], tag=f"ar2{sc}{qty}{i}")
                    P.emit("vector", lambda v: v.tensor_sub(rt3[:], rt1[:], rt2[:]),
                           reads=["a_rt1", "a_rt2"], writes=["a_rt3"], tag=f"ar3{sc}{qty}{i}")
                    P.emit("vector", lambda v, A1=A1: v.tensor_mul(rt1[:], A1, rs[:]),
                           reads=[skey_, "a_rs"], writes=["a_rt1"], tag=f"ar4{sc}{qty}{i}")
                    P.emit("vector", lambda v, A2=A2: v.tensor_mul(rt2[:], A2, rc[:]),
                           reads=[skey_, "a_rc"], writes=["a_rt2"], tag=f"ar5{sc}{qty}{i}")
                    P.emit("vector", lambda v: v.tensor_add(rt4[:], rt1[:], rt2[:]),
                           reads=["a_rt1", "a_rt2"], writes=["a_rt4"], tag=f"ar6{sc}{qty}{i}")
                    P.emit("scalar", lambda s_: s_.activation(sqt[:], rt3[:], AF.Square),
                           reads=["a_rt3"], writes=["a_sqt"], tag=f"al1{sc}{qty}{i}")
                    P.emit("tensor", lambda t: t.matmul(sb["pb6"][0:4, 0:SC],
                                                        cc_["li4"][:], sqt[:],
                                                        start=True, stop=False),
                           reads=["a_sqt", "a_li4"], writes=["pb6"], tag=f"al2{sc}{qty}{i}")
                    P.emit("scalar", lambda s_: s_.activation(sqt[:], rt4[:], AF.Square),
                           reads=["a_rt4"], writes=["a_sqt"], tag=f"al3{sc}{qty}{i}")
                    P.emit("tensor", lambda t: t.matmul(sb["pb6"][0:4, 0:SC],
                                                        cc_["li4"][:], sqt[:],
                                                        start=False, stop=True),
                           reads=["a_sqt", "a_li4"], writes=["pb6"], tag=f"al4{sc}{qty}{i}")
                    P.emit("scalar", lambda s_: s_.activation(rstd4[:], sb["pb6"][0:4, 0:SC],
                                                              AF.Ln, bias=e12[0:4, 0:1]),
                           reads=["pb6", "a_e12"], writes=["a_rstd4"], tag=f"al5{sc}{qty}{i}")
                    P.emit("scalar", lambda s_: s_.activation(rstd4[:], rstd4[:], AF.Exp,
                                                              scale=-0.5),
                           reads=["a_rstd4"], writes=["a_rstd4"], tag=f"al6{sc}{qty}{i}")
                    P.emit("vector", lambda v: v.tensor_copy(rstd4b[:], rstd4[:]),
                           reads=["a_rstd4"], writes=["a_rstd4b"], tag=f"al7{sc}{qty}{i}")
                    P.emit("tensor", lambda t: t.matmul(sb["pb7"][:, 0:SC],
                                                        cc_["li4T"][:], rstd4b[:],
                                                        start=True, stop=True),
                           reads=["a_rstd4b", "a_li4T"], writes=["pb7"], tag=f"al8{sc}{qty}{i}")
                    def f_q1(v, i=i, dstb=dstb):
                        return v.tensor_mul(dstb[:, i * SC:(i + 1) * SC], rt3[:],
                                            sb["pb7"][:, 0:SC])
                    P.emit("vector", f_q1, reads=["a_rt3", "pb7"], writes=[dkey],
                           tag=f"al9{sc}{qty}{i}")
                    def f_q2(v, i=i, dstb=dstb):
                        return v.tensor_mul(dstb[:, (4 + i) * SC:(5 + i) * SC], rt4[:],
                                            sb["pb7"][:, 0:SC])
                    P.emit("vector", f_q2, reads=["a_rt4", "pb7"], writes=[dkey],
                           tag=f"alA{sc}{qty}{i}")

            if sc == 0 and cx.dram.get("DBGTAP") is not None:
                dd = cx.dram["dbgt"]
                for nm, buf, row in (("qs", qs, 0), ("ks", ks, 128), ("vs", vs, 256),
                                     ("gb", gbuf, 384)):
                    P.emit("gpsimd", lambda e, buf=buf, row=row: e.dma_start(
                        out=dd[row:row + 128, 0:2048], in_=buf[:]),
                        reads=[f"a_{nm if nm != 'gb' else 'gbuf'}"], writes=["DBGT"],
                        dma=True, tag=f"dtap{nm}")
                P.emit("sync", lambda e: e.dma_start(
                    out=dd[512:520, 0:512], in_=bsc[:]),
                    reads=["a_bsc"], writes=["DBGT"], dma=True, tag="dtapb")
                P.emit("sync", lambda e: e.dma_start(
                    out=dd[520:528, 0:512], in_=lal[:]),
                    reads=["a_lal"], writes=["DBGT"], dma=True, tag="dtapl")
            for ci in range(SC // 64):
                o = ci * 64
                emit_scan_chunk(cx, sc, ci, o, mats, locals())
            for m in range(8):
                P.emit("scalar", lambda s_, m=m: s_.activation(
                    sqt[:], ofm[:, m * SC:(m + 1) * SC], AF.Square),
                    reads=["a_ofm"], writes=["a_sqt"], tag=f"aq1{sc}{m}")
                P.emit("tensor", lambda t, m=m: t.matmul(
                    sb["pb6"][0:2, m * 64:m * 64 + SC // 4] if False else
                    sb["pb6"][0:2, 0:SC],
                    li2[:], sqt[:], start=True, stop=True),
                    reads=["a_sqt", "a_li2"], writes=["pb6"], tag=f"aq2{sc}{m}")
                P.emit("scalar", lambda s_, m=m: s_.activation(
                    rstd2[:, m * SC:(m + 1) * SC], sb["pb6"][0:2, 0:SC], AF.Ln,
                    bias=e6[0:2, 0:1], scale=1.0 / 64),
                    reads=["pb6", "a_e6"], writes=["a_rstd2"], tag=f"aq3{sc}{m}")
            P.emit("scalar", lambda s_: s_.activation(rstd2[:], rstd2[:], AF.Exp,
                                                      scale=-0.5),
                   reads=["a_rstd2"], writes=["a_rstd2"], tag=f"aq4{sc}")
            P.emit("vector", lambda v: v.tensor_copy(rstd2b[:], rstd2[:]),
                   reads=["a_rstd2"], writes=["a_rstd2b"], tag=f"aq5{sc}")
            for m in range(8):
                P.emit("tensor", lambda t, m=m: t.matmul(
                    sb["pb7"][:, 0:SC], li2T[:], rstd2b[:, m * SC:(m + 1) * SC],
                    start=True, stop=True),
                    reads=["a_rstd2b", "a_li2T"], writes=["pb7"], tag=f"aq6{sc}{m}")
                def f_on(v, m=m):
                    return v.scalar_tensor_tensor(
                        tmps[:], ofm[:, m * SC:(m + 1) * SC], cc_["onrm"][:, 0:1],
                        sb["pb7"][:, 0:SC], ALU.mult, ALU.mult)
                P.emit("vector", f_on, reads=["a_ofm", "a_onrm", "pb7"],
                       writes=["a_tmps"], tag=f"aq7{sc}{m}")
                P.emit("scalar", lambda s_: s_.activation(tmpsb[:], tmps[:], AF.Silu),
                       reads=["a_tmps"], writes=["a_tmpsb"], tag=f"aq8{sc}{m}")
                P.emit("vector", lambda v, m=m: v.tensor_mul(
                    og8[:, m * SC:(m + 1) * SC], gbuf[:, m * SC:(m + 1) * SC],
                    tmpsb[:]),
                    reads=["a_tmpsb", "a_gbuf"], writes=["a_csq"], tag=f"aq9{sc}{m}")
            if sc == 0 and cx.dram.get("DBGTAP") is not None:
                P.emit("sync", lambda e: e.dma_start(
                    out=cx.dram["dbgt"][640:768, 0:2048], in_=ofm[:]),
                    reads=["a_ofm"], writes=["DBGT"], dma=True, tag="dtapo")
                P.emit("gpsimd", lambda e: e.dma_start(
                    out=cx.dram["dbgt"][768:896, 0:2048], in_=og8[:]),
                    reads=["a_csq"], writes=["DBGT"], dma=True, tag="dtapog")
            P.emit("sync", lambda e, sl=sl: e.dma_start(
                out=xck[:].rearrange("p (a n) -> p a n", a=8),
                in_=xind[:, sl].rearrange("(a p) n -> p a n", p=128)),
                reads=[f"{x_in}:{sc // 2}"], writes=["a_xck"], dma=True,
                tag=f"axl{sc}")
            for m in range(8):
                def f_wo(t, m=m):
                    last = None
                    for k in range(8):
                        last = t.matmul(
                            sb["pb0"][:, 0:SC],
                            wo[:, k * 1024 + m * 128: k * 1024 + (m + 1) * 128],
                            og8[:, k * SC:(k + 1) * SC],
                            start=(k == 0), stop=(k == 7))
                    return last
                P.emit("tensor", f_wo, reads=["a_csq", "a_wo"], writes=["pb0"],
                       tag=f"awo{sc}{m}")
                P.emit("vector", lambda v, m=m: v.tensor_add(
                    xck[:, m * SC:(m + 1) * SC], xck[:, m * SC:(m + 1) * SC],
                    sb["pb0"][:, 0:SC]),
                    reads=["pb0", "a_xck"], writes=["a_xck"], tag=f"awoe{sc}{m}")
            P.emit("sync", lambda e, sl=sl: e.dma_start(
                out=xoutd[:, sl].rearrange("(a p) n -> p a n", p=128),
                in_=xck[:].rearrange("p (a n) -> p a n", a=8)),
                reads=["a_xck"], writes=[f"{x_out}:{sc // 2}"], dma=True,
                tag=f"axs{sc}")
            if xn_out is not None:
                ln.emit(["a_xck"], xck[:],
                        cx.dram[xn_out][:, sl].rearrange("(a p) n -> p a n", p=128),
                        [f"{xn_out}:{sc // 2}"])
        P.barrier()
    _sec_a()


def emit_scan_chunk(cx, sc, ci, o, mats, env):
    P = cx.P
    sb = cx.sb
    SC = 256
    cc_ = env["cc_"]
    identb = cx.sb["identb"]
    (gcum, gprev, gcb, gpb, negg, bcb, gpe, ggr, ggrb, gcc, gccb, gcrow, blkx,
     gphp, bhp, ghp, bghp, gcghp, gchp, gcim, tmpw, rf32, sf32, ofm, lal, bsc,
     zs8, qs, ks, vs) = (env[k] for k in (
        "gcum", "gprev", "gcb", "gpb", "negg", "bcb", "gpe", "ggr", "ggrb",
        "gcc", "gccb", "gcrow", "blkx", "gphp", "bhp", "ghp", "bghp", "gcghp",
        "gchp", "gcim", "tmpw", "rf32", "sf32", "ofm", "lal", "bsc", "zs8",
        "qs", "ks", "vs"))
    tg = f"S{sc}_{ci}"

    def slot(mat, h):
        p_, c_ = _api(h), _acb(h)
        return mat[64 * p_:64 * p_ + 64, 64 * c_:64 * c_ + 64]

    def idslice(h):
        p_ = _api(h)
        return identb[64 * p_:64 * p_ + 64, 64 * p_:64 * p_ + 64]

    _HS = ([h for h in range(16) if _api(h) == 0],
           [h for h in range(16) if _api(h) == 1])

    for pi_ in range(2):
        po = pi_ * 64
        def f_cum(v, pi_=pi_, po=po):
            return v.tensor_tensor_scan(
                gcum[:, po:po + 64], lal[:, pi_ * SC + o: pi_ * SC + o + 64],
                zs8[:], 0.0, ALU.add, ALU.add)
        P.emit("vector", f_cum, reads=["a_lal", "a_zs8"], writes=["a_gcum"],
               tag=f"{tg}cum{pi_}")
        P.emit("vector", lambda v, po=po: v.tensor_copy(
            gprev[:, po + 1:po + 64], gcum[:, po:po + 63]),
            reads=["a_gcum"], writes=["a_gprev"], tag=f"{tg}gp{pi_}")
        P.emit("vector", lambda v, po=po: v.memset(gprev[:, po:po + 1], 0.0),
               writes=["a_gprev"], tag=f"{tg}gpz{pi_}")
        P.emit("vector", lambda v, po=po: v.tensor_copy(gcb[:, po:po + 64],
                                                        gcum[:, po:po + 64]),
               reads=["a_gcum"], writes=["a_gcb"], tag=f"{tg}c1{pi_}")
        P.emit("vector", lambda v, po=po: v.tensor_copy(gpb[:, po:po + 64],
                                                        gprev[:, po:po + 64]),
               reads=["a_gprev"], writes=["a_gpb"], tag=f"{tg}c2{pi_}")
        P.emit("vector", lambda v, po=po: v.tensor_scalar(
            negg[:, po:po + 64], gcum[:, po:po + 64], -1.0, None, ALU.mult),
            reads=["a_gcum"], writes=["a_negg"], tag=f"{tg}c3{pi_}")
        def f_bc(v, pi_=pi_, po=po):
            return v.tensor_copy(bcb[:, po:po + 64],
                                 bsc[:, pi_ * SC + o: pi_ * SC + o + 64])
        P.emit("vector", f_bc, reads=["a_bsc"], writes=["a_bcb"], tag=f"{tg}c4{pi_}")
        P.emit("scalar", lambda s_, po=po: s_.activation(gpe[:, po:po + 64],
                                                         gprev[:, po:po + 64], AF.Exp),
               reads=["a_gprev"], writes=["a_gpe"], tag=f"{tg}c5{pi_}")
        def f_ggr(s_, po=po):
            return s_.activation(ggr[:, po:po + 64], gcum[:, po:po + 64], AF.Exp,
                                 scale=-1.0, bias=gcum[:, po + 63:po + 64])
        P.emit("scalar", f_ggr, reads=["a_gcum"], writes=["a_ggr"], tag=f"{tg}c6{pi_}")
        P.emit("vector", lambda v, po=po: v.tensor_copy(ggrb[:, po:po + 64],
                                                        ggr[:, po:po + 64]),
               reads=["a_ggr"], writes=["a_ggrb"], tag=f"{tg}c7{pi_}")
        def f_gcc(s_, pi_=pi_, po=po):
            return s_.activation(gcc[:, pi_:pi_ + 1], gcum[:, po + 63:po + 64], AF.Exp)
        P.emit("scalar", f_gcc, reads=["a_gcum"], writes=["a_gcc"], tag=f"{tg}c8{pi_}")
        P.emit("vector", lambda v, pi_=pi_: v.tensor_copy(gccb[:, pi_:pi_ + 1],
                                                          gcc[:, pi_:pi_ + 1]),
               reads=["a_gcc"], writes=["a_gccb"], tag=f"{tg}c9{pi_}")
        for j, (srcb, srck, dsthp, dk, expit) in enumerate((
                (gcb, "a_gcb", ghp, "a_ghp", True),
                (gpb, "a_gpb", gphp, "a_gphp", True),
                (bcb, "a_bcb", bhp, "a_bhp", False),
                (ggrb, "a_ggrb", gcghp, "a_gcghp", False))):
            def f_tr(t, srcb=srcb, po=po, j=j):
                return t.matmul(sb["pb6"][0:64, j * 8:(j + 1) * 8],
                                srcb[:, po:po + 64], identb[0:8, 0:8],
                                start=True, stop=True)
            P.emit("tensor", f_tr, reads=[srck, "identb"], writes=["pb6"],
                   tag=f"{tg}tr{pi_}{j}")
            if expit:
                def f_ev(s_, dsthp=dsthp, pi_=pi_, j=j):
                    return s_.activation(dsthp[64 * pi_:64 * pi_ + 64, 0:8],
                                         sb["pb6"][0:64, j * 8:(j + 1) * 8], AF.Exp)
                P.emit("scalar", f_ev, reads=["pb6"], writes=[dk], tag=f"{tg}te{pi_}{j}")
            else:
                def f_ev(v, dsthp=dsthp, pi_=pi_, j=j):
                    return v.tensor_copy(dsthp[64 * pi_:64 * pi_ + 64, 0:8],
                                         sb["pb6"][0:64, j * 8:(j + 1) * 8])
                P.emit("vector", f_ev, reads=["pb6"], writes=[dk], tag=f"{tg}te{pi_}{j}")
    P.emit("vector", lambda v: v.tensor_mul(bghp[:], bhp[:], ghp[:]),
           reads=["a_bhp", "a_ghp"], writes=["a_bghp"], tag=f"{tg}bg")
    def f_gcr(t):
        last = None
        for pi_ in range(2):
            last = t.matmul(sb["pb6"][0:1, 64 + pi_ * 8: 64 + pi_ * 8 + 8],
                            gccb[:, pi_:pi_ + 1], identb[0:8, 0:8],
                            start=True, stop=True)
        return last
    P.emit("tensor", f_gcr, reads=["a_gccb", "identb"], writes=["pb6"], tag=f"{tg}gcr")
    P.emit("vector", lambda v: v.tensor_copy(gcrow[0:1, :], sb["pb6"][0:1, 64:80]),
           reads=["pb6"], writes=["a_gcrow"], tag=f"{tg}gcre")
    def f_gco2(t):
        t.matmul(sb["pb7"][:, 0:8], cx.sb["ones_row_b"][0:1, 0:128],
                 gcrow[0:1, 0:8], start=True, stop=True)
        return t.matmul(sb["pb7"][:, 8:16], cx.sb["ones_row_b"][0:1, 0:128],
                        gcrow[0:1, 8:16], start=True, stop=True)
    P.emit("tensor", f_gco2, reads=["a_gcrow", "ones_row_b"], writes=["pb7"],
           tag=f"{tg}gco")
    P.emit("vector", lambda v: v.tensor_copy(gchp[0:64, 0:8], sb["pb7"][0:64, 0:8]),
           reads=["pb7"], writes=["a_gchp"], tag=f"{tg}gce1")
    P.emit("vector", lambda v: v.tensor_copy(gchp[64:128, 0:8], sb["pb7"][64:128, 8:16]),
           reads=["pb7"], writes=["a_gchp"], tag=f"{tg}gce2")
    def f_gcim(v):
        return v.tensor_tensor(
            gcim[:].rearrange("p (a n) -> p a n", a=8),
            cc_["eyehp"][:].rearrange("p (a n) -> p a n", a=8),
            gchp[:].unsqueeze(2).to_broadcast([128, 8, 64]), ALU.mult)
    P.emit("vector", f_gcim, reads=["a_eyehp", "a_gchp"], writes=["a_gcim"],
           tag=f"{tg}gcim")

    for qty, srcb, srck, dstm, dk in (("k", ks, "a_ks", "kda", "a_kda"),
                                      ("q", qs, "a_qs", "qda", "a_qda")):
        for b in range(4):
            for half in range(2):
                def f_cp(v, b=b, half=half, srcb=srcb, dstm=dstm):
                    sap = srcb[32 * b:32 * b + 32, :]                         .rearrange("p (a n) -> p a n", a=8)[:, half * 4:half * 4 + 4,
                                                           o:o + 64]
                    dap = mats[dstm][64 * (b % 2) + 32 * half:
                                     64 * (b % 2) + 32 * half + 32,
                                     256 * (b // 2):256 * (b // 2) + 256]                         .rearrange("p (a n) -> p a n", a=4)
                    return v.tensor_copy(dap, sap)
                P.emit("vector", f_cp, reads=[srck], writes=[dk],
                       tag=f"{tg}as{qty}{b}{half}")
    for pi_ in range(2):
        def f_blk(v, pi_=pi_):
            po = pi_ * 64
            return v.tensor_tensor(
                blkx[:].rearrange("p (a n) -> p a n", a=8),
                gpe[:, po:po + 64].unsqueeze(1).to_broadcast([8, 8, 64]),
                cc_["blk8"][:].rearrange("p (a n) -> p a n", a=8), ALU.mult)
        P.emit("vector", f_blk, reads=["a_gpe", "a_blk8"], writes=["a_blkx"],
               tag=f"{tg}qb{pi_}")
        def f_gpm(t, pi_=pi_):
            return t.matmul(sb["pb7"][64 * pi_:64 * pi_ + 64, 0:512],
                            env["ones8"][:], blkx[:],
                            start=True, stop=True, tile_position=(0, 64 * pi_))
        P.emit("tensor", f_gpm, reads=["a_blkx", "a_ones8"], writes=["pb7"],
               tag=f"{tg}qm{pi_}")
    P.emit("vector", lambda v: v.tensor_mul(mats["qsa"][:], mats["qda"][:],
                                            sb["pb7"][:, 0:512]),
           reads=["a_qda", "pb7"], writes=["a_qsa"], tag=f"{tg}qsc")

    for pi_ in range(2):
        bank = "pb0" if pi_ == 0 else "pb1"
        def f_kk(t, pi_=pi_, bank=bank):
            last = None
            for h in _HS[pi_]:
                last = t.matmul(sb[bank][64 * _api(h):64 * _api(h) + 64,
                                         64 * _acb(h):64 * _acb(h) + 64],
                                slot(mats["kda"], h), slot(mats["kda"], h),
                                start=True, stop=True,
                                tile_position=(64 * _api(h), 64 * _api(h)))
            return last
        P.emit("tensor", f_kk, reads=["a_kda"], writes=[bank], tag=f"{tg}kk{pi_}")
        bank2 = "pb2" if pi_ == 0 else "pb3"
        def f_kq(t, pi_=pi_, bank2=bank2):
            last = None
            for h in _HS[pi_]:
                last = t.matmul(sb[bank2][64 * _api(h):64 * _api(h) + 64,
                                          64 * _acb(h):64 * _acb(h) + 64],
                                slot(mats["kda"], h), slot(mats["qda"], h),
                                start=True, stop=True,
                                tile_position=(64 * _api(h), 64 * _api(h)))
            return last
        P.emit("tensor", f_kq, reads=["a_kda", "a_qda"], writes=[bank2],
               tag=f"{tg}kq{pi_}")

    for pi_ in range(2):
        po = pi_ * 64
        def f_gblk(v, pi_=pi_, po=po):
            return v.tensor_tensor(
                blkx[:].rearrange("p (a n) -> p a n", a=8),
                gcb[:, po:po + 64].unsqueeze(1).to_broadcast([8, 8, 64]),
                cc_["blk8"][:].rearrange("p (a n) -> p a n", a=8), ALU.mult)
        P.emit("vector", f_gblk, reads=["a_gcb", "a_blk8"], writes=["a_blkx"],
               tag=f"{tg}Ab{pi_}")
        def f_am1(t, pi_=pi_, po=po):
            t.matmul(sb["pb4"][64 * pi_:64 * pi_ + 64, 0:512],
                     gcb[:, po:po + 64], cc_["blk8"][:],
                     start=(pi_ == 0), stop=False, tile_position=(0, 64 * pi_))
            return t.matmul(sb["pb4"][64 * pi_:64 * pi_ + 64, 0:512],
                            env["negones8"][:], blkx[:],
                            start=False, stop=(pi_ == 1), tile_position=(0, 64 * pi_))
        P.emit("tensor", f_am1, reads=["a_gcb", "a_blk8", "a_blkx", "a_no8"],
               writes=["pb4"], tag=f"{tg}Am{pi_}")
        def f_hblk(v, pi_=pi_, po=po):
            return v.tensor_tensor(
                blkx[:].rearrange("p (a n) -> p a n", a=8),
                gpb[:, po:po + 64].unsqueeze(1).to_broadcast([8, 8, 64]),
                cc_["blk8"][:].rearrange("p (a n) -> p a n", a=8), ALU.mult)
        P.emit("vector", f_hblk, reads=["a_gpb", "a_blk8"], writes=["a_blkx"],
               tag=f"{tg}Hb{pi_}")
        def f_hm1(t, pi_=pi_, po=po):
            t.matmul(sb["pb5"][64 * pi_:64 * pi_ + 64, 0:512],
                     env["ones8"][:], blkx[:],
                     start=(pi_ == 0), stop=False, tile_position=(0, 64 * pi_))
            return t.matmul(sb["pb5"][64 * pi_:64 * pi_ + 64, 0:512],
                            negg[:, po:po + 64], cc_["blk8"][:],
                            start=False, stop=(pi_ == 1), tile_position=(0, 64 * pi_))
        P.emit("tensor", f_hm1, reads=["a_negg", "a_blk8", "a_blkx", "a_ones8"],
               writes=["pb5"], tag=f"{tg}Hm{pi_}")
    P.emit("vector", lambda v: v.tensor_add(tmpw[:], sb["pb4"][:, 0:512],
                                            cc_["mskL"][:]),
           reads=["pb4", "a_mskL"], writes=["a_tmpw"], tag=f"{tg}Aa1")
    P.emit("scalar", lambda s_: s_.activation(tmpw[:], tmpw[:], AF.Exp),
           reads=["a_tmpw"], writes=["a_tmpw"], tag=f"{tg}Aa2")
    for pi_ in range(2):
        bank = "pb0" if pi_ == 0 else "pb1"
        def f_akk(v, pi_=pi_, bank=bank):
            return v.tensor_mul(tmpw[64 * pi_:64 * pi_ + 64, :],
                                tmpw[64 * pi_:64 * pi_ + 64, :],
                                sb[bank][64 * pi_:64 * pi_ + 64, 0:512])
        P.emit("vector", f_akk, reads=["a_tmpw", bank], writes=["a_tmpw"],
               tag=f"{tg}Aa3{pi_}")
    def f_ab(v):
        return v.tensor_tensor(
            mats["Aa"][:].rearrange("p (a n) -> p a n", a=8),
            tmpw[:].rearrange("p (a n) -> p a n", a=8),
            bhp[:].unsqueeze(2).to_broadcast([128, 8, 64]), ALU.mult)
    P.emit("vector", f_ab, reads=["a_tmpw", "a_bhp"], writes=["a_Aa"], tag=f"{tg}Aa4")
    P.emit("vector", lambda v: v.tensor_add(tmpw[:], sb["pb5"][:, 0:512],
                                            cc_["mskU"][:]),
           reads=["pb5", "a_mskU"], writes=["a_tmpw"], tag=f"{tg}Ha1")
    P.emit("scalar", lambda s_: s_.activation(tmpw[:], tmpw[:], AF.Exp),
           reads=["a_tmpw"], writes=["a_tmpw"], tag=f"{tg}Ha2")
    for pi_ in range(2):
        bank2 = "pb2" if pi_ == 0 else "pb3"
        def f_hkq(v, pi_=pi_, bank2=bank2):
            return v.tensor_mul(mats["HTa"][64 * pi_:64 * pi_ + 64, :],
                                tmpw[64 * pi_:64 * pi_ + 64, :],
                                sb[bank2][64 * pi_:64 * pi_ + 64, 0:512])
        P.emit("vector", f_hkq, reads=["a_tmpw", bank2], writes=["a_HTa"],
               tag=f"{tg}Ha3{pi_}")
        def f_hd(v, pi_=pi_, bank2=bank2):
            return v.tensor_mul(mats["HTd"][64 * pi_:64 * pi_ + 64, :],
                                env["idphp"][64 * pi_:64 * pi_ + 64, :],
                                sb[bank2][64 * pi_:64 * pi_ + 64, 0:512])
        P.emit("vector", f_hd, reads=["a_idphp", bank2], writes=["a_HTd"],
               tag=f"{tg}Hd{pi_}")

    for pi_ in range(2):
        bank = "pb4" if pi_ == 0 else "pb5"
        def f_b0(t, pi_=pi_, bank=bank):
            last = None
            for h in _HS[pi_]:
                    last = t.matmul(sb[bank][64 * _api(h):64 * _api(h) + 64,
                                             64 * _acb(h):64 * _acb(h) + 64],
                                    slot(mats["Aa"], h), idslice(h),
                                    start=True, stop=True,
                                    tile_position=(64 * _api(h), 64 * _api(h)))
            return last
        P.emit("tensor", f_b0, reads=["a_Aa", "identb"], writes=[bank],
               tag=f"{tg}b0{pi_}")
    for pi_ in range(2):
        bank = "pb4" if pi_ == 0 else "pb5"
        if pi_ == 0:
            P.emit("scalar", lambda sc_, bank=bank: sc_.activation(
                mats["B0"][0:64, :], sb[bank][0:64, 0:512], AF.Copy),
                reads=[bank], writes=["a_B0"], tag=f"{tg}b0e{pi_}")
        else:
            P.emit("vector", lambda v, bank=bank: v.tensor_copy(
                mats["B0"][64:128, :], sb[bank][64:128, 0:512]),
                reads=[bank], writes=["a_B0"], tag=f"{tg}b0e{pi_}")
        def f_pp0(v, pi_=pi_, bank=bank):
            return v.scalar_tensor_tensor(
                mats["PP"][64 * pi_:64 * pi_ + 64, :],
                sb[bank][64 * pi_:64 * pi_ + 64, 0:512], -1.0,
                cc_["eyehp"][64 * pi_:64 * pi_ + 64, :], ALU.mult, ALU.add)
        P.emit("vector", f_pp0, reads=[bank, "a_eyehp"], writes=["a_PP"],
               tag=f"{tg}pp0{pi_}")
    xprev, bprev = "Aa", "B0"
    for k in range(1, 6):
        xk = f"X{k}"
        bk = f"B{k}"
        for pi_ in range(2):
            bank = "pb4" if pi_ == 0 else "pb5"
            def f_x(t, pi_=pi_, bank=bank, xprev=xprev, bprev=bprev):
                last = None
                for h in _HS[pi_]:
                    last = t.matmul(sb[bank][64 * _api(h):64 * _api(h) + 64,
                                             64 * _acb(h):64 * _acb(h) + 64],
                                    slot(mats[bprev], h), slot(mats[xprev], h),
                                    start=True, stop=True,
                                    tile_position=(64 * _api(h), 64 * _api(h)))
                return last
            P.emit("tensor", f_x, reads=[f"a_{xprev}", f"a_{bprev}"], writes=[bank],
                   tag=f"{tg}x{k}_{pi_}")
        for pi_ in range(2):
            bank = "pb4" if pi_ == 0 else "pb5"
            if pi_ == 0:
                P.emit("scalar", lambda sc_, bank=bank, xk=xk: sc_.activation(
                    mats[xk][0:64, :], sb[bank][0:64, 0:512], AF.Copy),
                    reads=[bank], writes=[f"a_{xk}"], tag=f"{tg}xe{k}_{pi_}")
            else:
                P.emit("vector", lambda v, bank=bank, xk=xk: v.tensor_copy(
                    mats[xk][64:128, :], sb[bank][64:128, 0:512]),
                    reads=[bank], writes=[f"a_{xk}"], tag=f"{tg}xe{k}_{pi_}")
        if k <= 4:
            for pi_ in range(2):
                bank = "pb6" if pi_ == 0 else "pb7"
                def f_b(t, pi_=pi_, bank=bank, xprev=xprev, bprev=bprev):
                    last = None
                    for h in _HS[pi_]:
                        last = t.matmul(sb[bank][64 * _api(h):64 * _api(h) + 64,
                                                 64 * _acb(h):64 * _acb(h) + 64],
                                        slot(mats[xprev], h), slot(mats[bprev], h),
                                        start=True, stop=True,
                                        tile_position=(64 * _api(h), 64 * _api(h)))
                    return last
                P.emit("tensor", f_b, reads=[f"a_{xprev}", f"a_{bprev}"],
                       writes=[bank], tag=f"{tg}bb{k}_{pi_}")
            for pi_ in range(2):
                bank = "pb6" if pi_ == 0 else "pb7"
                if pi_ == 0:
                    P.emit("scalar", lambda sc_, bank=bank, bk=bk: sc_.activation(
                        mats[bk][0:64, :], sb[bank][0:64, 0:512], AF.Copy),
                        reads=[bank], writes=[f"a_{bk}"], tag=f"{tg}be{k}_{pi_}")
                else:
                    P.emit("vector", lambda v, bank=bank, bk=bk: v.tensor_copy(
                        mats[bk][64:128, :], sb[bank][64:128, 0:512]),
                        reads=[bank], writes=[f"a_{bk}"], tag=f"{tg}be{k}_{pi_}")
        for pi_ in range(2):
            bank = "pb2" if pi_ == 0 else "pb3"
            def f_pr(t, pi_=pi_, bank=bank, xk=xk):
                last = None
                for h in _HS[pi_]:
                    last = t.matmul(sb[bank][64 * _api(h):64 * _api(h) + 64,
                                             64 * _acb(h):64 * _acb(h) + 64],
                                    slot(mats[xk], h), slot(mats["PP"], h),
                                    start=True, stop=True,
                                    tile_position=(64 * _api(h), 64 * _api(h)))
                return last
            P.emit("tensor", f_pr, reads=[f"a_{xk}", "a_PP"], writes=[bank],
                   tag=f"{tg}pr{k}_{pi_}")
        for pi_ in range(2):
            bank = "pb2" if pi_ == 0 else "pb3"
            def f_pe(v, pi_=pi_, bank=bank):
                return v.tensor_add(mats["PP"][64 * pi_:64 * pi_ + 64, :],
                                    mats["PP"][64 * pi_:64 * pi_ + 64, :],
                                    sb[bank][64 * pi_:64 * pi_ + 64, 0:512])
            P.emit("vector", f_pe, reads=[bank, "a_PP"], writes=["a_PP"],
                   tag=f"{tg}pe{k}_{pi_}")
        xprev, bprev = xk, bk

    for pi_ in range(2):
        bank = "pb0" if pi_ == 0 else "pb1"
        def f_kt(t, pi_=pi_, bank=bank):
            last = None
            for h in _HS[pi_]:
                    last = t.matmul(sb[bank][64 * _api(h):64 * _api(h) + 64,
                                             64 * _acb(h):64 * _acb(h) + 64],
                                    slot(mats["kda"], h), idslice(h),
                                    start=True, stop=True,
                                    tile_position=(64 * _api(h), 64 * _api(h)))
            return last
        P.emit("tensor", f_kt, reads=["a_kda", "identb"], writes=[bank],
               tag=f"{tg}kt{pi_}")
    for pi_ in range(2):
        bank = "pb0" if pi_ == 0 else "pb1"
        def f_ktl(v, pi_=pi_, bank=bank):
            return v.tensor_tensor(
                mats["KTl"][64 * pi_:64 * pi_ + 64, :].rearrange("p (a n) -> p a n", a=8),
                sb[bank][64 * pi_:64 * pi_ + 64, 0:512].rearrange("p (a n) -> p a n", a=8),
                gcghp[64 * pi_:64 * pi_ + 64, :].unsqueeze(2).to_broadcast([64, 8, 64]),
                ALU.mult)
        P.emit("vector", f_ktl, reads=[bank, "a_gcghp"], writes=["a_KTl"],
               tag=f"{tg}ktl{pi_}")
        def f_grr(v, pi_=pi_, bank=bank):
            return v.tensor_tensor(
                mats["GRr"][64 * pi_:64 * pi_ + 64, :].rearrange("p (a n) -> p a n", a=8),
                sb[bank][64 * pi_:64 * pi_ + 64, 0:512].rearrange("p (a n) -> p a n", a=8),
                bghp[64 * pi_:64 * pi_ + 64, :].unsqueeze(2).to_broadcast([64, 8, 64]),
                ALU.mult)
        P.emit("vector", f_grr, reads=[bank, "a_bghp"], writes=["a_GRr"],
               tag=f"{tg}grr{pi_}")
    for pi_ in range(2):
        bank = "pb2" if pi_ == 0 else "pb3"
        def f_vt(t, pi_=pi_, bank=bank):
            last = None
            for h in _HS[pi_]:
                tile = h // 2
                r0 = 64 * (h % 2)
                last = t.matmul(sb[bank][64 * _api(h):64 * _api(h) + 64,
                                         64 * _acb(h):64 * _acb(h) + 64],
                                vs[r0:r0 + 64, tile * SC + o: tile * SC + o + 64],
                                idslice(h), start=True, stop=True,
                                tile_position=(r0, 64 * _api(h)))
            return last
        P.emit("tensor", f_vt, reads=["a_vs", "identb"], writes=[bank],
               tag=f"{tg}vt{pi_}")
    for pi_ in range(2):
        bank = "pb2" if pi_ == 0 else "pb3"
        def f_vts(v, pi_=pi_, bank=bank):
            return v.tensor_tensor(
                mats["VTs"][64 * pi_:64 * pi_ + 64, :].rearrange("p (a n) -> p a n", a=8),
                sb[bank][64 * pi_:64 * pi_ + 64, 0:512].rearrange("p (a n) -> p a n", a=8),
                bhp[64 * pi_:64 * pi_ + 64, :].unsqueeze(2).to_broadcast([64, 8, 64]),
                ALU.mult)
        P.emit("vector", f_vts, reads=[bank, "a_bhp"], writes=["a_VTs"],
               tag=f"{tg}vts{pi_}")
        P.emit("vector", lambda v, pi_=pi_, bank=bank: v.tensor_copy(
            mats["VTu"][64 * pi_:64 * pi_ + 64, :],
            sb[bank][64 * pi_:64 * pi_ + 64, 0:512]),
            reads=[bank], writes=["a_VTu"], tag=f"{tg}vtu{pi_}")

    def hp_mm(name, lhs, rhs, banks, outm, outk):
        for pi_ in range(2):
            bank = banks[pi_]
            def f_m(t, pi_=pi_, bank=bank, lhs=lhs, rhs=rhs):
                last = None
                for h in _HS[pi_]:
                    last = t.matmul(sb[bank][64 * _api(h):64 * _api(h) + 64,
                                             64 * _acb(h):64 * _acb(h) + 64],
                                    slot(mats[lhs], h), slot(mats[rhs], h),
                                    start=True, stop=True,
                                    tile_position=(64 * _api(h), 64 * _api(h)))
                return last
            P.emit("tensor", f_m, reads=[f"a_{lhs}", f"a_{rhs}"], writes=[bank],
                   tag=f"{tg}{name}{pi_}")
        for pi_ in range(2):
            bank = banks[pi_]
            P.emit("vector", lambda v, pi_=pi_, bank=bank, outm=outm: v.tensor_copy(
                mats[outm][64 * pi_:64 * pi_ + 64, :],
                sb[bank][64 * pi_:64 * pi_ + 64, 0:512]),
                reads=[bank], writes=[outk], tag=f"{tg}{name}e{pi_}")

    hp_mm("G", "PP", "GRr", ("pb4", "pb5"), "GGm", "a_GGm")
    hp_mm("U", "PP", "VTs", ("pb6", "pb7"), "UVm", "a_UVm")
    for pi_ in range(2):
        bank = "pb0" if pi_ == 0 else "pb1"
        def f_w1(t, pi_=pi_, bank=bank):
            last = None
            for h in _HS[pi_]:
                    last = t.matmul(sb[bank][64 * _api(h):64 * _api(h) + 64,
                                             64 * _acb(h):64 * _acb(h) + 64],
                                    slot(mats["GGm"], h), slot(mats["HTa"], h),
                                    start=True, stop=True,
                                    tile_position=(64 * _api(h), 64 * _api(h)))
            return last
        P.emit("tensor", f_w1, reads=["a_GGm", "a_HTa"], writes=[bank],
               tag=f"{tg}w1{pi_}")
    for pi_ in range(2):
        bank = "pb0" if pi_ == 0 else "pb1"
        def f_e2(v, pi_=pi_, bank=bank):
            return v.tensor_sub(mats["E2m"][64 * pi_:64 * pi_ + 64, :],
                                mats["qsa"][64 * pi_:64 * pi_ + 64, :],
                                sb[bank][64 * pi_:64 * pi_ + 64, 0:512])
        P.emit("vector", f_e2, reads=[bank, "a_qsa"], writes=["a_E2m"],
               tag=f"{tg}e2{pi_}")

    for pi_ in range(2):
        bank = "pb2" if pi_ == 0 else "pb3"
        def f_f(t, pi_=pi_, bank=bank):
            last = None
            for i, h in enumerate(_HS[pi_]):
                last = t.matmul(sb[bank][64 * _api(h):64 * _api(h) + 64,
                                         64 * _acb(h):64 * _acb(h) + 64],
                                slot(mats["UVm"], h), slot(mats["HTa"], h),
                                start=(i == 0), stop=False,
                                tile_position=(64 * _api(h), 64 * _api(h)))
            return last
        P.emit("tensor", f_f, reads=["a_UVm", "a_HTa"], writes=[bank],
               tag=f"{tg}F{pi_}")
    for pi_ in range(2):
        bank = "pb2" if pi_ == 0 else "pb3"
        def f_dp(t, pi_=pi_, bank=bank):
            last = None
            for h in _HS[pi_]:
                    last = t.matmul(sb[bank][64 * _api(h):64 * _api(h) + 64,
                                             64 * _acb(h):64 * _acb(h) + 64],
                                    slot(mats["VTu"], h), slot(mats["HTd"], h),
                                    start=False, stop=False,
                                    tile_position=(64 * _api(h), 64 * _api(h)))
            return last
        P.emit("tensor", f_dp, reads=["a_VTu", "a_HTd"], writes=[bank],
               tag=f"{tg}Dp{pi_}")
    for pi_ in range(2):
        bank = "pb2" if pi_ == 0 else "pb3"
        def f_m2(t, pi_=pi_, bank=bank):
            last = None
            for h in _HS[pi_]:
                    last = t.matmul(sb[bank][64 * _api(h):64 * _api(h) + 64,
                                             64 * _acb(h):64 * _acb(h) + 64],
                                    slot(mats["Sb"], h), slot(mats["E2m"], h),
                                    start=False, stop=True,
                                    tile_position=(64 * _api(h), 64 * _api(h)))
            return last
        P.emit("tensor", f_m2, reads=["a_Sb", "a_E2m"], writes=[bank],
               tag=f"{tg}m2{pi_}")
    for pi_ in range(2):
        bank = "pb2" if pi_ == 0 else "pb3"
        for grp in range(2):
            def f_oe(v, pi_=pi_, bank=bank, grp=grp):
                src = sb[bank][64 * pi_:64 * pi_ + 64,
                               grp * 256:grp * 256 + 256]                     .rearrange("p (a n) -> p a n", a=4)
                dst = ofm[64 * pi_:64 * pi_ + 64, :]                     .rearrange("p (a n) -> p a n", a=8)[:, grp::2, o:o + 64]
                return v.tensor_copy(dst, src)
            P.emit("vector", f_oe, reads=[bank], writes=["a_ofm"],
                   tag=f"{tg}oe{pi_}{grp}")

    for pi_ in range(2):
        bank = "pb4" if pi_ == 0 else "pb5"
        def f_pc(t, pi_=pi_, bank=bank):
            last = None
            for h in _HS[pi_]:
                    last = t.matmul(sb[bank][64 * _api(h):64 * _api(h) + 64,
                                             64 * _acb(h):64 * _acb(h) + 64],
                                    slot(mats["GGm"], h), slot(mats["KTl"], h),
                                    start=True, stop=True,
                                    tile_position=(64 * _api(h), 64 * _api(h)))
            return last
        P.emit("tensor", f_pc, reads=["a_GGm", "a_KTl"], writes=[bank],
               tag=f"{tg}Pc{pi_}")
    for pi_ in range(2):
        bank = "pb4" if pi_ == 0 else "pb5"
        def f_pt(v, pi_=pi_, bank=bank):
            return v.scalar_tensor_tensor(
                mats["E2m"][64 * pi_:64 * pi_ + 64, :],
                sb[bank][64 * pi_:64 * pi_ + 64, 0:512], -1.0,
                gcim[64 * pi_:64 * pi_ + 64, :], ALU.mult, ALU.add)
        P.emit("vector", f_pt, reads=[bank, "a_gcim"], writes=["a_E2m"],
               tag=f"{tg}pt{pi_}")
    for pi_ in range(2):
        bank = "pb6" if pi_ == 0 else "pb7"
        def f_r(t, pi_=pi_, bank=bank):
            last = None
            for h in _HS[pi_]:
                    last = t.matmul(sb[bank][64 * _api(h):64 * _api(h) + 64,
                                             64 * _acb(h):64 * _acb(h) + 64],
                                    slot(mats["KTl"], h), slot(mats["UVm"], h),
                                    start=True, stop=True,
                                    tile_position=(64 * _api(h), 64 * _api(h)))
            return last
        P.emit("tensor", f_r, reads=["a_KTl", "a_UVm"], writes=[bank],
               tag=f"{tg}R{pi_}")
    for pi_ in range(2):
        bank = "pb6" if pi_ == 0 else "pb7"
        P.emit("vector", lambda v, pi_=pi_, bank=bank: v.tensor_copy(
            rf32[64 * pi_:64 * pi_ + 64, :],
            sb[bank][64 * pi_:64 * pi_ + 64, 0:512]),
            reads=[bank], writes=["a_tmpw"], tag=f"{tg}re{pi_}")
    for pi_ in range(2):
        bank = "pb0" if pi_ == 0 else "pb1"
        def f_s(t, pi_=pi_, bank=bank):
            last = None
            for h in _HS[pi_]:
                    last = t.matmul(sb[bank][64 * _api(h):64 * _api(h) + 64,
                                             64 * _acb(h):64 * _acb(h) + 64],
                                    slot(mats["E2m"], h), slot(mats["Sb"], h),
                                    start=True, stop=True,
                                    tile_position=(64 * _api(h), 64 * _api(h)))
            return last
        P.emit("tensor", f_s, reads=["a_E2m", "a_Sb"], writes=[bank],
               tag=f"{tg}su{pi_}")
    for pi_ in range(2):
        bank = "pb0" if pi_ == 0 else "pb1"
        def f_se(v, pi_=pi_, bank=bank):
            return v.tensor_add(sf32[64 * pi_:64 * pi_ + 64, :],
                                rf32[64 * pi_:64 * pi_ + 64, :],
                                sb[bank][64 * pi_:64 * pi_ + 64, 0:512])
        P.emit("vector", f_se, reads=[bank, "a_tmpw"], writes=["a_sf32"],
               tag=f"{tg}se{pi_}")
        P.emit("vector", lambda v, pi_=pi_: v.tensor_copy(
            mats["Sb"][64 * pi_:64 * pi_ + 64, :],
            sf32[64 * pi_:64 * pi_ + 64, :]),
            reads=["a_sf32"], writes=["a_Sb"], tag=f"{tg}sc{pi_}")


def emit_attn_b(cx, x_in, xn_in, x_out, xn_out):
    nc, P = cx.nc, cx.P
    pref = "b"
    xnd = cx.dram[xn_in]
    xind = cx.dram[x_in]
    xoutd = cx.dram[x_out]
    qdd, kdd, vtd = cx.dram["qd"], cx.dram["kdd"], cx.dram["vtd"]
    NTCH = T // TCH
    sb = cx.sb
    identb = cx.sb["identb"]

    def _sec_b0():
        with contextlib.ExitStack() as stk:
            sbuf = lambda nm, shp, dt: cx.sbuf(f"{pref}{nm}", shp, dt, stack=stk)
            xnc = sbuf("p_xnc", [128, 8 * TCH], BF16)
            xmacc = sbuf("p_xm", [128, 8], F32)
            tmpc = sbuf("p_tmp", [128, 1], F32)
            xmb = sbuf("p_xmb", [128, 8], BF16)
            sw1t = sbuf("p_sw1", [128, 8 * 512], BF16)
            pw1t = sbuf("p_pw1", [128, 3 * 8 * 512], BF16)
            sw2t = sbuf("p_sw2", [128, 4 * 3], BF16)
            pw2t = sbuf("p_pw2", [128, 3 * 4 * 16], BF16)
            sb1t = sbuf("p_sb1", [128, 4], F32)
            pb1t = sbuf("p_pb1", [128, 12], F32)
            pb2t = sbuf("p_pb2", [16, 3], F32)
            sb2t = sbuf("p_sb2", [3, 1], F32)
            indT = sbuf("p_indT", [16, 8 * 128], BF16)
            h1b = sbuf("p_h1", [128, 4], BF16)
            hidb = sbuf("p_hid", [128, 4], BF16)
            gf32 = sbuf("p_gf32", [16, 1], F32)
            scr = sbuf("p_scr", [3, 4], F32)
            polrow = sbuf("p_pol", [1, 4], F32)
            zrow = sbuf("p_z", [1, 1], F32)
            rowbuf = sbuf("p_row", [1, 16], F32)
            hgrow = sbuf("p_hg", [1, 16], F32)
            hgrowb = sbuf("p_hgb", [1, 16], BF16)
            gcolb = sbuf("p_gcb", [16, 1], BF16)
            hgcolb = sbuf("p_hgc", [16, 1], BF16)
            onesb = sbuf("p_onesb", [1, 1], BF16)
            P.emit("vector", lambda v: v.memset(onesb[:], 1.0), writes=["b_onesb"],
                   tag="bonesb")
            P.emit("vector", lambda v: v.memset(xmacc[:], 0.0), writes=["b_xmacc"],
                   tag="bxmz")
            P.emit("gpsimd", lambda e: e.dma_start(
                out=sw1t[:].rearrange("p (a n) -> p a n", a=8),
                in_=cx.dram["b_sw1_t"].rearrange("(a p) n -> p a n", p=128)),
                writes=["b_sw1t"], dma=True, tag="bldsw1")
            P.emit("gpsimd", lambda e: e.dma_start(
                out=pw1t[:].rearrange("p (q a n) -> p q a n", q=3, a=8),
                in_=cx.dram["b_pw1_t"].rearrange("q (a p) n -> p q a n", p=128)),
                writes=["b_pw1t"], dma=True, tag="bldpw1")
            P.emit("gpsimd", lambda e: e.dma_start(
                out=sw2t[:].rearrange("p (a n) -> p a n", a=4),
                in_=cx.dram["b_sw2_t"].rearrange("(a p) n -> p a n", p=128)),
                writes=["b_sw2t"], dma=True, tag="bldsw2")
            P.emit("gpsimd", lambda e: e.dma_start(
                out=pw2t[:].rearrange("p (q a n) -> p q a n", q=3, a=4),
                in_=cx.dram["b_pw2_t"].rearrange("q (a p) n -> p q a n", p=128)),
                writes=["b_pw2t"], dma=True, tag="bldpw2")
            P.emit("sync", lambda e: e.dma_start(out=sb1t[:], in_=cx.dram["b_sb1_t"]),
                   writes=["b_sb1t"], dma=True, tag="bldsb1")
            P.emit("sync", lambda e: e.dma_start(out=pb1t[:], in_=cx.dram["b_pb1_t"]),
                   writes=["b_pb1t"], dma=True, tag="bldpb1")
            P.emit("sync", lambda e: e.dma_start(out=pb2t[:], in_=cx.dram["b_pb2_t"]),
                   writes=["b_pb2t"], dma=True, tag="bldpb2")
            P.emit("sync", lambda e: e.dma_start(out=sb2t[:], in_=cx.dram["b_sb2_t"]),
                   writes=["b_sb2t"], dma=True, tag="bldsb2")
            P.emit("gpsimd", lambda e: e.dma_start(
                out=indT[:].rearrange("p (a n) -> p a n", a=8),
                in_=cx.dram["c_ind_hc_T"].rearrange("a p n -> p a n")),
                writes=["b_indT"], dma=True, tag="bldind")
            for c in range(NTCH):
                sl = np.s_[c * TCH:(c + 1) * TCH]
                P.emit("sync", lambda e, sl=sl: e.dma_start(
                    out=xnc[:].rearrange("p (a n) -> p a n", a=8),
                    in_=xnd[:, sl].rearrange("(a p) n -> p a n", p=128)),
                    reads=[f"{xn_in}:{c}"], writes=["b_xnc"], dma=True, tag=f"bxml{c}")
                for i in range(8):
                    P.emit("vector", lambda v, i=i: v.reduce_sum(
                        tmpc[:, 0:1], xnc[:, i * TCH:(i + 1) * TCH],
                        axis=mybir.AxisListType.X),
                        reads=["b_xnc"], writes=["b_tmpc"], tag=f"bxr{c}_{i}")
                    P.emit("vector", lambda v, i=i: v.tensor_add(
                        xmacc[:, i:i + 1], xmacc[:, i:i + 1], tmpc[:, 0:1]),
                        reads=["b_tmpc", "b_xmacc"], writes=["b_xmacc"], tag=f"bxa{c}_{i}")
            P.emit("vector", lambda v: v.tensor_scalar(xmacc[:], xmacc[:], 1.0 / T, None,
                                                       ALU.mult),
                   reads=["b_xmacc"], writes=["b_xmacc"], tag="bxm_s")
            P.emit("vector", lambda v: v.tensor_copy(xmb[:], xmacc[:]),
                   reads=["b_xmacc"], writes=["b_xmb"], tag="bxm_c")

            def f_h1(t):
                last = None
                for m4 in range(4):
                    for k in range(8):
                        last = t.matmul(sb["pb0"][:, m4:m4 + 1],
                                        sw1t[:, k * 512 + m4 * 128: k * 512 + (m4 + 1) * 128],
                                        xmb[:, k:k + 1], start=(k == 0), stop=(k == 7))
                return last
            P.emit("tensor", f_h1, reads=["b_xmb", "b_sw1t"], writes=["pb0"], tag="bh1mm")
            for m4 in range(4):
                P.emit("scalar", lambda s, m4=m4: s.activation(
                    h1b[:, m4:m4 + 1], sb["pb0"][:, m4:m4 + 1], AF.Gelu,
                    bias=sb1t[:, m4:m4 + 1]),
                    reads=["pb0", "b_sb1t"], writes=["b_h1b"], tag=f"bh1g{m4}")

            def f_pol(t):
                last = None
                for k in range(4):
                    last = t.matmul(sb["pb1"][0:3, 0:1], sw2t[:, k * 3:(k + 1) * 3],
                                    h1b[:, k:k + 1], start=(k == 0), stop=(k == 3))
                return last
            P.emit("tensor", f_pol, reads=["b_h1b", "b_sw2t"], writes=["pb1"], tag="bpolmm")
            P.emit("vector", lambda v: v.tensor_add(scr[0:3, 0:1], sb["pb1"][0:3, 0:1],
                                                    sb2t[:]),
                   reads=["pb1", "b_sb2t"], writes=["b_scr"], tag="bpola")
            P.emit("vector", lambda v: v.tensor_copy(gcolb[0:3, 0:1], scr[0:3, 0:1]),
                   reads=["b_scr"], writes=["b_gcolb"], tag="bpolc")
            P.emit("tensor", lambda t: t.matmul(sb["pb2"][0:1, 0:3], gcolb[0:3, 0:1],
                                                identb[0:3, 0:3], start=True, stop=True),
                   reads=["b_gcolb", "identb"], writes=["pb2"], tag="bpoltr")
            P.emit("scalar", lambda s: s.activation(polrow[0:1, 0:3], sb["pb2"][0:1, 0:3],
                                                    AF.Exp, accum_out=zrow[0:1, 0:1]),
                   reads=["pb2"], writes=["b_polrow", "b_zrow"], tag="bpolex")
            P.emit("vector", lambda v: v.reciprocal(zrow[0:1, 0:1], zrow[0:1, 0:1]),
                   reads=["b_zrow"], writes=["b_zrow"], tag="bpolrc")
            P.emit("vector", lambda v: v.tensor_scalar(polrow[0:1, 0:3], polrow[0:1, 0:3],
                                                       zrow[0:1, 0:1], None, ALU.mult),
                   reads=["b_zrow", "b_polrow"], writes=["b_polrow"], tag="bpoln")
            for p_ in range(3):
                def f_hid(t, p_=p_):
                    last = None
                    for m4 in range(4):
                        for k in range(8):
                            last = t.matmul(
                                sb["pb0"][:, m4:m4 + 1],
                                pw1t[:, p_ * 4096 + k * 512 + m4 * 128:
                                     p_ * 4096 + k * 512 + (m4 + 1) * 128],
                                xmb[:, k:k + 1], start=(k == 0), stop=(k == 7))
                    return last
                P.emit("tensor", f_hid, reads=["b_xmb", "b_pw1t"], writes=["pb0"],
                       tag=f"bhid{p_}")
                for m4 in range(4):
                    P.emit("scalar", lambda s, m4=m4, p_=p_: s.activation(
                        hidb[:, m4:m4 + 1], sb["pb0"][:, m4:m4 + 1], AF.Gelu,
                        bias=pb1t[:, p_ * 4 + m4: p_ * 4 + m4 + 1]),
                        reads=["pb0", "b_pb1t"], writes=["b_hidb"], tag=f"bhg{p_}_{m4}")

                def f_gt(t, p_=p_):
                    last = None
                    for k in range(4):
                        last = t.matmul(sb["pb1"][0:16, 0:1],
                                        pw2t[:, p_ * 64 + k * 16: p_ * 64 + (k + 1) * 16],
                                        hidb[:, k:k + 1], start=(k == 0), stop=(k == 3))
                    return last
                P.emit("tensor", f_gt, reads=["b_hidb", "b_pw2t"], writes=["pb1"],
                       tag=f"bgt{p_}")
                P.emit("vector", lambda v, p_=p_: v.tensor_add(
                    gf32[:], sb["pb1"][0:16, 0:1], pb2t[:, p_:p_ + 1]),
                    reads=["pb1", "b_pb2t"], writes=["b_gf32"], tag=f"bga{p_}")
                P.emit("vector", lambda v: v.tensor_copy(gcolb[:], gf32[:]),
                       reads=["b_gf32"], writes=["b_gcolb"], tag=f"bgc{p_}")
                P.emit("tensor", lambda t: t.matmul(sb["pb2"][0:1, 0:16], gcolb[0:16, 0:1],
                                                    identb[0:16, 0:16], start=True,
                                                    stop=True),
                       reads=["b_gcolb", "identb"], writes=["pb2"], tag=f"bgtr{p_}")
                P.emit("scalar", lambda s: s.activation(rowbuf[0:1, 0:16],
                                                        sb["pb2"][0:1, 0:16],
                                                        AF.Exp, accum_out=zrow[0:1, 0:1]),
                       reads=["pb2"], writes=["b_rowbuf", "b_zrow"], tag=f"bgex{p_}")
                P.emit("vector", lambda v: v.reciprocal(zrow[0:1, 0:1], zrow[0:1, 0:1]),
                       reads=["b_zrow"], writes=["b_zrow"], tag=f"bgrc{p_}")
                P.emit("vector", lambda v: v.tensor_scalar(rowbuf[0:1, 0:16],
                                                           rowbuf[0:1, 0:16],
                                                           zrow[0:1, 0:1], None, ALU.mult),
                       reads=["b_zrow", "b_rowbuf"], writes=["b_rowbuf"], tag=f"bgn{p_}")
                if p_ == 0:
                    P.emit("vector", lambda v, p_=p_: v.tensor_scalar(
                        hgrow[0:1, :], rowbuf[0:1, 0:16], polrow[0:1, p_:p_ + 1], None,
                        ALU.mult),
                        reads=["b_rowbuf", "b_polrow"], writes=["b_hgrow"], tag=f"bhga{p_}")
                else:
                    P.emit("vector", lambda v, p_=p_: v.scalar_tensor_tensor(
                        hgrow[0:1, :], rowbuf[0:1, 0:16], polrow[0:1, p_:p_ + 1],
                        hgrow[0:1, :], ALU.mult, ALU.add),
                        reads=["b_rowbuf", "b_polrow", "b_hgrow"], writes=["b_hgrow"],
                        tag=f"bhga{p_}")
            P.emit("vector", lambda v: v.tensor_scalar(hgrow[0:1, :], hgrow[0:1, :], 0.125,
                                                       None, ALU.mult),
                   reads=["b_hgrow"], writes=["b_hgrow"], tag="bhgs")
            P.emit("vector", lambda v: v.tensor_copy(hgrowb[0:1, :], hgrow[0:1, :]),
                   reads=["b_hgrow"], writes=["b_hgrowb"], tag="bhgc")
            P.emit("tensor", lambda t: t.matmul(sb["pb2"][0:16, 0:1], hgrowb[0:1, :],
                                                onesb[0:1, 0:1], start=True, stop=True),
                   reads=["b_hgrowb", "b_onesb"], writes=["pb2"], tag="bhgtr")
            P.emit("vector", lambda v: v.tensor_copy(hgcolb[:], sb["pb2"][0:16, 0:1]),
                   reads=["pb2"], writes=["b_hgcolb"], tag="bhgcc")

            def f_hgc(t):
                last = None
                for j in range(8):
                    last = t.matmul(sb["pb3"][:, j:j + 1], indT[:, j * 128:(j + 1) * 128],
                                    hgcolb[0:16, 0:1], start=True, stop=True)
                return last
            P.emit("tensor", f_hgc, reads=["b_hgcolb", "b_indT"], writes=["pb3"],
                   tag="bhgcm")
            P.emit("vector", lambda v: v.tensor_copy(cx.sb["hgc_g"][:], sb["pb3"][:, 0:8]),
                   reads=["pb3"], writes=["hgc_g"], tag="bhgce")
            P.barrier()
    _sec_b0()

    def _sec_b1a():
        with contextlib.ExitStack() as stk:
            sbuf = lambda nm, shp, dt: cx.sbuf(f"{pref}{nm}", shp, dt, stack=stk)
            wqs = sbuf("wqs", [128, 8 * 1024], BF16)
            wks = sbuf("wks", [128, 8 * 1024], BF16)
            wvs = sbuf("wvs", [128, 8 * 1024], BF16)
            bqc = sbuf("bqc", [128, 8], F32)
            bkc = sbuf("bkc", [128, 8], F32)
            bvr = sbuf("bvr", [1, 1024], BF16)
            xnc = [sbuf(f"xnc{i}", [128, 8 * TCH], BF16) for i in range(2)]
            qsb = sbuf("qsb", [128, 512], BF16)
            ksb = sbuf("ksb", [128, 512], BF16)
            vsb = sbuf("vsb", [128, 1024], BF16)
            onesbr = sbuf("onesbr", [1, 128], BF16)
            P.emit("vector", lambda v: v.memset(onesbr[:], 1.0), writes=["b_onesbr"],
                   tag="bones2")
            for nm, key in (("wqs", "bwq_t"), ("wks", "bwk_t"), ("wvs", "bwv_t")):
                P.emit("gpsimd", lambda e, nm=nm, key=key: e.dma_start(
                    out=cx.sb[f"{pref}{nm}"][:].rearrange("p (a n) -> p a n", a=8),
                    in_=cx.dram[key].rearrange("(a p) n -> p a n", p=128)),
                    writes=[f"b_{nm}"], dma=True, tag=f"bld{nm}")
            P.emit("sync", lambda e: e.dma_start(out=bqc[:], in_=cx.dram["b_bqc"]),
                   writes=["b_bqc"], dma=True, tag="bldbq")
            P.emit("sync", lambda e: e.dma_start(out=bkc[:], in_=cx.dram["b_bkc"]),
                   writes=["b_bkc"], dma=True, tag="bldbk")
            P.emit("gpsimd", lambda e: e.dma_start(out=bvr[:], in_=cx.dram["b_bvr"]),
                   writes=["b_bvr"], dma=True, tag="bldbv")
            for c in range(NTCH):
                bi = c % 2
                sl = np.s_[c * TCH:(c + 1) * TCH]
                kxn = f"b_xnc{bi}"
                P.emit("sync", lambda e, sl=sl, bi=bi: e.dma_start(
                    out=xnc[bi][:].rearrange("p (a n) -> p a n", a=8),
                    in_=xnd[:, sl].rearrange("(a p) n -> p a n", p=128)),
                    reads=[f"{xn_in}:{c}"], writes=[kxn], dma=True, tag=f"bpx{c}")
                for m in range(8):
                    def f_qm(t, m=m, bi=bi):
                        last = None
                        for k in range(8):
                            last = t.matmul(
                                sb["pb0"][:, 0:512],
                                wqs[:, k * 1024 + m * 128: k * 1024 + (m + 1) * 128],
                                xnc[bi][:, k * TCH:(k + 1) * TCH],
                                start=(k == 0), stop=(k == 7))
                        return last
                    P.emit("tensor", f_qm, reads=[kxn, "b_wqs"], writes=["pb0"],
                           tag=f"bqmm{c}_{m}")

                    def f_qe(v, m=m):
                        return v.scalar_tensor_tensor(
                            qsb[:], sb["pb0"][:, 0:512], bqc[:, m:m + 1],
                            cx.sb["hgc_g"][:, m:m + 1].to_broadcast([128, 512]),
                            ALU.add, ALU.mult)
                    P.emit("vector", f_qe, reads=["pb0", "b_bqc", "hgc_g"],
                           writes=["b_qsb"], tag=f"bqe{c}_{m}")
                    P.emit("sync", lambda e, m=m, sl=sl: e.dma_start(
                        out=qdd[m * 128:(m + 1) * 128, sl], in_=qsb[:]),
                        reads=["b_qsb"], writes=[f"qd:{c}_{m}"], dma=True,
                        tag=f"bqs{c}_{m}")

                    def f_km(t, m=m, bi=bi):
                        last = None
                        for k in range(8):
                            last = t.matmul(
                                sb["pb1"][:, 0:512],
                                wks[:, k * 1024 + m * 128: k * 1024 + (m + 1) * 128],
                                xnc[bi][:, k * TCH:(k + 1) * TCH],
                                start=(k == 0), stop=(k == 7))
                        return last
                    P.emit("tensor", f_km, reads=[kxn, "b_wks"], writes=["pb1"],
                           tag=f"bkmm{c}_{m}")
                    P.emit("vector", lambda v, m=m: v.tensor_scalar(
                        ksb[:], sb["pb1"][:, 0:512], bkc[:, m:m + 1], None, ALU.add),
                        reads=["pb1", "b_bkc"], writes=["b_ksb"], tag=f"bke{c}_{m}")
                    P.emit("sync", lambda e, m=m, sl=sl: e.dma_start(
                        out=kdd[m * 128:(m + 1) * 128, sl], in_=ksb[:]),
                        reads=["b_ksb"], writes=[f"kd:{c}_{m}"], dma=True,
                        tag=f"bks{c}_{m}")
                for tb in range(4):
                    def f_vm(t, tb=tb, bi=bi):
                        last = None
                        for nn in range(2):
                            dstp = sb["pb2"] if nn == 0 else sb["pb3"]
                            for k in range(8):
                                t.matmul(dstp[:, 0:512],
                                         xnc[bi][:, k * TCH + tb * 128: k * TCH + (tb + 1) * 128],
                                         wvs[:, k * 1024 + nn * 512: k * 1024 + (nn + 1) * 512],
                                         start=(k == 0), stop=False)
                            last = t.matmul(dstp[:, 0:512], onesbr[0:1, :],
                                            bvr[0:1, nn * 512:(nn + 1) * 512],
                                            start=False, stop=True)
                        return last
                    P.emit("tensor", f_vm, reads=[kxn, "b_wvs", "b_bvr", "b_onesbr"],
                           writes=["pb2", "pb3"], tag=f"bvmm{c}_{tb}")

                    def f_ve(v):
                        v.tensor_copy(vsb[:, 0:512], sb["pb2"][:, 0:512])
                        return v.tensor_copy(vsb[:, 512:1024], sb["pb3"][:, 0:512])
                    P.emit("vector", f_ve, reads=["pb2", "pb3"], writes=["b_vsb"],
                           tag=f"bve{c}_{tb}")
                    P.emit("sync", lambda e, c=c, tb=tb: e.dma_start(
                        out=vtd[c * 4 + tb].rearrange("p (h n) -> p h n", n=65)[:, :, 0:64],
                        in_=vsb[:].rearrange("p (h n) -> p h n", n=64)),
                        reads=["b_vsb"], writes=[f"vt:{c}_{tb}"], dma=True,
                        tag=f"bvs{c}_{tb}")
            P.barrier()
    _sec_b1a()

    def _sec_b1bc():
        with contextlib.ExitStack() as stk:
            sbuf = lambda nm, shp, dt: cx.sbuf(f"{pref}{nm}", shp, dt, stack=stk)
            kds = sbuf("kds", [128, 8 * 4096], BF16)
            vts = sbuf("vts", [128, 32 * 1040], BF16)
            wos = sbuf("wos", [128, 8 * 1024], BF16)
            boc = sbuf("boc", [128, 8], F32)
            mskb = sbuf("mskb", [128, 4 * 512], BF16)
            qg = sbuf("qg", [128, 8 * 512], BF16)
            pbuf = sbuf("pbuf", [128, 4 * 512], BF16)
            rrec2 = [sbuf("rrec0", [1, 512], F32), sbuf("rrec1", [1, 512], F32)]
            otmps = [sbuf("otmp0", [128, 512], F32), sbuf("otmp1", [128, 512], F32)]
            xck = sbuf("xck", [128, 8 * 256], F32)
            ln = LNHelper(cx, 256, "blna", stack=stk) if xn_out is not None else None
            P.emit("sync", lambda e: e.dma_start(
                out=kds[:].rearrange("p (a n) -> p a n", a=8),
                in_=kdd.rearrange("(a p) n -> p a n", p=128)),
                reads=[f"kd:{c}_{m}" for c in range(8) for m in range(8)],
                writes=["b_kds"], dma=True, tag="bldkds")
            P.emit("sync", lambda e: e.dma_start(
                out=vts[:].rearrange("p (a n) -> p a n", a=32),
                in_=vtd.rearrange("a p n -> p a n")),
                reads=[f"vt:{c}_{tb}" for c in range(8) for tb in range(4)],
                writes=["b_vts"], dma=True, tag="bldvts")
            P.emit("vector", lambda v: v.memset(
                vts[:].rearrange("p (a n) -> p a n", n=65)[:, :, 64:65]
                .rearrange("p a n -> p (a n)"), 1.0),
                writes=["b_vts"], tag="bvones")
            P.emit("gpsimd", lambda e: e.dma_start(
                out=wos[:].rearrange("p (a n) -> p a n", a=8),
                in_=cx.dram["bwo_t"].rearrange("(a p) n -> p a n", p=128)),
                writes=["b_wos"], dma=True, tag="bldwo")
            P.emit("sync", lambda e: e.dma_start(out=boc[:], in_=cx.dram["b_boc"]),
                   writes=["b_boc"], dma=True, tag="bldbo")
            P.emit("gpsimd", lambda e: e.dma_start(
                out=mskb[:].rearrange("p (a n) -> p a n", a=4),
                in_=cx.dram["c_att_mask"].rearrange("a p n -> p a n")),
                writes=["b_mskb"], dma=True, tag="bldmsk")
            xck2 = sbuf("xck2", [128, 8 * 256], F32)
            xck_ = [xck, xck2]
            for g in range(8):
                gsl = np.s_[g * 512:(g + 1) * 512]
                P.emit("sync", lambda e, gsl=gsl: e.dma_start(
                    out=qg[:].rearrange("p (a n) -> p a n", a=8),
                    in_=qdd[:, gsl].rearrange("(a p) n -> p a n", p=128)),
                    reads=[f"qd:{g}_{m}" for m in range(8)],
                    writes=[f"b_qg{t}_{pp}" for t in range(8) for pp in range(2)],
                    dma=True, tag=f"bqg{g}")
                for h in range(16):
                    tile = h // 2
                    r0 = 64 * (h % 2)
                    nkb = 4 * g + 4
                    pav = "pb2" if h % 2 == 0 else "pb3"
                    pex = "pb4" if h % 2 == 0 else "pb5"
                    for kq in range(0, nkb, 4):
                        kbs = list(range(kq, min(kq + 4, nkb)))
                        for kb in kbs:
                            pqk = ("pb0", "pb1", "pb6", "pb7")[kb % 4]

                            def f_qk(t, tile=tile, r0=r0, kb=kb, pqk=pqk):
                                return t.matmul(
                                    sb[pqk][:, 0:512],
                                    kds[r0:r0 + 64,
                                        tile * 4096 + kb * 128: tile * 4096 + (kb + 1) * 128],
                                    qg[r0:r0 + 64, tile * 512:(tile + 1) * 512],
                                    start=True, stop=True, tile_position=(r0, 0))
                            P.emit("tensor", f_qk,
                                   reads=["b_kds", f"b_qg{tile}_{h % 2}"], writes=[pqk],
                                   tag=f"bqk{g}_{h}_{kb}")
                            if kb >= 4 * g:
                                r = kb - 4 * g

                                def f_mask(v, r=r, pqk=pqk):
                                    return v.tensor_add(sb[pqk][:, 0:512], sb[pqk][:, 0:512],
                                                        mskb[:, r * 512:(r + 1) * 512])
                                P.emit("vector", f_mask, reads=[pqk, "b_mskb"], writes=[pqk],
                                       tag=f"bmk{g}_{h}_{kb}")

                            def f_exp(s, kb=kb, pqk=pqk):
                                return s.activation(pbuf[:, (kb % 4) * 512:(kb % 4) * 512 + 512],
                                                    sb[pqk][:, 0:512], AF.Exp)
                            P.emit("scalar", f_exp, reads=[pqk], writes=[f"b_pbuf{kb % 4}"],
                                   tag=f"bex{g}_{h}_{kb}")
                        def f_av(t, kbs=tuple(kbs), h=h, nkb=nkb, pav=pav):
                            last = None
                            for kb in kbs:
                                last = t.matmul(
                                    sb[pav][0:65, 0:512],
                                    vts[:, kb * 1040 + h * 65: kb * 1040 + (h + 1) * 65],
                                    pbuf[:, (kb % 4) * 512:(kb % 4) * 512 + 512],
                                    start=(kb == 0), stop=(kb == nkb - 1))
                            return last
                        P.emit("tensor", f_av,
                               reads=[f"b_pbuf{kb % 4}" for kb in kbs] + ["b_vts"],
                               writes=[pav], tag=f"bav{g}_{h}_{kq}")

                    P.emit("vector", lambda v, pav=pav, h=h: v.reciprocal(
                        rrec2[h % 2][0:1, 0:512], sb[pav][64:65, 0:512]),
                        reads=[pav], writes=[f"b_rrec{h % 2}"], tag=f"brc{g}_{h}")

                    def emit_fin(hf):
                        pavf = "pb2" if hf % 2 == 0 else "pb3"
                        pexf = "pb4" if hf % 2 == 0 else "pb5"
                        tf = hf // 2
                        r0f = 64 * (hf % 2)

                        def f_re(t, pexf=pexf, hf=hf):
                            return t.matmul(sb[pexf][0:64, 0:512],
                                            cx.sb["ones_row"][0:1, 0:64],
                                            rrec2[hf % 2][0:1, 0:512],
                                            start=True, stop=True)
                        P.emit("tensor", f_re, reads=[f"b_rrec{hf % 2}", "ones_row"],
                               writes=[pexf], tag=f"bre{g}_{hf}")

                        def f_oc(v, pavf=pavf, hf=hf):
                            return v.tensor_copy(otmps[hf % 2][0:64, 0:512],
                                                 sb[pavf][0:64, 0:512])
                        P.emit("vector", f_oc, reads=[pavf], writes=[f"b_otmp{hf % 2}"],
                               tag=f"boc{g}_{hf}")

                        def f_ev(v, pexf=pexf, tf=tf, r0f=r0f, hf=hf):
                            return v.tensor_tensor(
                                qg[r0f:r0f + 64, tf * 512:(tf + 1) * 512],
                                otmps[hf % 2][0:64, 0:512], sb[pexf][0:64, 0:512],
                                ALU.mult)
                        P.emit("vector", f_ev, reads=[f"b_otmp{hf % 2}", pexf],
                               writes=[f"b_qg{tf}_{hf % 2}"], tag=f"bev{g}_{hf}")
                    if h > 0:
                        emit_fin(h - 1)
                    if h == 15:
                        emit_fin(15)
                for hh in range(2):
                    csl = np.s_[g * 512 + hh * 256: g * 512 + (hh + 1) * 256]
                    cidx = g * 2 + hh
                    P.emit("sync", lambda e, csl=csl, hh=hh: e.dma_start(
                        out=xck_[hh][:].rearrange("p (a n) -> p a n", a=8),
                        in_=xind[:, csl].rearrange("(a p) n -> p a n", p=128)),
                        reads=[f"{x_in}:{cidx // 2}"],
                        writes=[f"b_xck{hh}"], dma=True, tag=f"blx{g}_{hh}")
                qgall = [f"b_qg{t}_{pp}" for t in range(8) for pp in range(2)]
                for m in range(8):
                    pwo = "pb6" if m % 2 == 0 else "pb7"

                    def f_wo(t, m=m, pwo=pwo):
                        last = None
                        for k in range(8):
                            last = t.matmul(
                                sb[pwo][:, 0:512],
                                wos[:, k * 1024 + m * 128: k * 1024 + (m + 1) * 128],
                                qg[:, k * 512:(k + 1) * 512],
                                start=(k == 0), stop=(k == 7))
                        return last
                    P.emit("tensor", f_wo, reads=qgall + ["b_wos"], writes=[pwo],
                           tag=f"bwo{g}_{m}")
                    for hh in range(2):
                        def f_woe(v, m=m, hh=hh, pwo=pwo):
                            return v.scalar_tensor_tensor(
                                xck_[hh][:, m * 256:(m + 1) * 256],
                                sb[pwo][:, hh * 256:(hh + 1) * 256],
                                boc[:, m:m + 1], xck_[hh][:, m * 256:(m + 1) * 256],
                                ALU.add, ALU.add)
                        P.emit("vector", f_woe, reads=[pwo, "b_boc", f"b_xck{hh}"],
                               writes=[f"b_xck{hh}"], tag=f"bwoe{g}_{m}_{hh}")
                for hh in range(2):
                    csl = np.s_[g * 512 + hh * 256: g * 512 + (hh + 1) * 256]
                    cidx = g * 2 + hh
                    P.emit("sync", lambda e, csl=csl, hh=hh: e.dma_start(
                        out=xoutd[:, csl].rearrange("(a p) n -> p a n", p=128),
                        in_=xck_[hh][:].rearrange("p (a n) -> p a n", a=8)),
                        reads=[f"b_xck{hh}"], writes=[f"{x_out}:{cidx // 2}"],
                        dma=True, tag=f"bso{g}_{hh}")
                    if xn_out is not None:
                        ln.emit([f"b_xck{hh}"], xck_[hh][:],
                                cx.dram[xn_out][:, csl].rearrange("(a p) n -> p a n", p=128),
                                [f"{xn_out}:{cidx // 2}"])
            P.barrier()
    _sec_b1bc()


def emit_copy_dram(cx, src, dst):
    P = cx.P
    s = cx.dram[src]
    dd = cx.dram[dst]
    with contextlib.ExitStack() as stk:
        buf = cx.sbuf("cpbuf", [128, 8 * TCH], F32, stack=stk)
        for c in range(T // TCH):
            sl = np.s_[c * TCH:(c + 1) * TCH]
            P.emit("sync", lambda e, sl=sl: e.dma_start(
                out=buf[:].rearrange("p (a n) -> p a n", a=8),
                in_=s[:, sl].rearrange("(a p) n -> p a n", p=128)),
                reads=[f"{src}:{c}"], writes=["cpbuf"], dma=True, tag="cp_in")
            P.emit("sync", lambda e, sl=sl: e.dma_start(
                out=dd[:, sl].rearrange("(a p) n -> p a n", p=128),
                in_=buf[:].rearrange("p (a n) -> p a n", a=8)),
                reads=["cpbuf"], writes=[f"{dst}:{c}"], dma=True, tag="cp_out")
        P.barrier()



def build_program(hp, phases=("ln0", "attn_a", "ffn1", "attn_b", "ffn2"), dbg=()):
    nc = bass.Bass("TRN2", target_bir_lowering=False)
    P = Prog()
    with contextlib.ExitStack() as stack:
        cx = Ctx(nc, P, stack)
        xin = nc.dram_tensor("x_t", [D, T], F32, kind="ExternalInput").ap()
        out = nc.dram_tensor("out_t", [D, T], F32, kind="ExternalOutput").ap()
        extra = {}
        for k, arr in hp.items():
            t = nc.dram_tensor(k, list(arr.shape), F32, kind="ExternalInput").ap()
            extra[k] = np.asarray(arr, dtype=np.float32)
            cx.dram[k] = t
        def scratch(name, shape, dtype):
            kind = "ExternalOutput" if name in dbg else "Internal"
            cx.dram[name] = nc.dram_tensor(name, shape, dtype, kind=kind).ap()
        cx.dram["xr1"] = nc.dram_tensor("xr1", [D, T], F32, kind="Internal").ap()
        scratch("xr2", [D, T], F32)
        scratch("xtmp", [D, T], F32)
        scratch("wtd", [32, 128, T], BF16)
        scratch("qd", [D, T], BF16)
        scratch("kdd", [D, T], BF16)
        scratch("vtd", [32, 128, 1040], BF16)
        scratch("xn0", [D, T], BF16)
        scratch("dbgt", [D, T], F32)
        if "dbgt" in dbg:
            cx.dram["DBGTAP"] = True
        scratch("xn1", [D, T], BF16)
        cx.dram["x_in"] = xin
        cx.dram["out"] = out

        PS = stack.enter_context(nc.psum_tensor("PS", [128, 4096], F32))
        cx.sb["PS"] = PS
        for i in range(8):
            cx.sb[f"pb{i}"] = PS[:, i * 512:(i + 1) * 512]

        cx.sbuf("onescol", [128, 1], BF16)
        P.emit("vector", lambda v: v.memset(cx.sb["onescol"][:], 1.0),
               writes=["onescol"], tag="onescol")
        cx.sbuf("ones_row", [1, 128], F32)
        P.emit("vector", lambda v: v.memset(cx.sb["ones_row"][:], 1.0),
               writes=["ones_row"], tag="onesrow")
        cx.sbuf("epscol", [128, 1], F32)
        P.emit("vector", lambda v: v.memset(cx.sb["epscol"][:], 1e-5),
               writes=["epscol"], tag="epscol")
        cx.sbuf("hgc_g", [128, 8], F32)
        cx.sbuf("ones_row_b", [1, 128], BF16)
        P.emit("vector", lambda v: v.memset(cx.sb["ones_row_b"][:], 1.0),
               writes=["ones_row_b"], tag="onesrowb")
        identb = cx.sbuf("identb", [128, 128], BF16)
        P.emit("gpsimd", lambda e: e.dma_start(out=identb[:], in_=cx.dram["c_ident"]),
               writes=["identb"], dma=True, tag="ld_ident")

        if "ln0" in phases:
            emit_ln_input(cx)
            P.barrier()
        resid, xn = "x_in", "xn0"
        if "attn_a" in phases:
            emit_attn_a(cx, x_in=resid, xn_in=xn, x_out="xr1", xn_out="xn1")
            resid, xn = "xr1", "xn1"
        if "ffn1" in phases:
            nxt_xn = None if phases[-1] == "ffn1" else "xn0"
            xo = "out" if phases[-1] == "ffn1" else "xr2"
            emit_ffn(cx, x_in=resid, xn_in=xn, x_out=xo, xn_out=nxt_xn, idx=1)
            resid, xn = xo, nxt_xn
        if "attn_b" in phases:
            emit_attn_b(cx, x_in=resid, xn_in=xn, x_out="xr1", xn_out="xn1")
            resid, xn = "xr1", "xn1"
        if "ffn2" in phases:
            emit_ffn(cx, x_in=resid, xn_in=xn, x_out="out", xn_out=None, idx=2)
            resid = "out"
        if resid != "out":
            emit_copy_dram(cx, resid, "out")
        P.emit("sync", lambda e: None,
               reads=[f"out:{c}" for c in range(T // TCH)], tag="final_wait")
        P.finalize_and_emit(nc)
    return nc, extra


def kernel(**inputs):
    hp = host_prep(inputs)
    nc, extra = build_program(hp)
    x = np.asarray(inputs["L0_attn_a_x"], dtype=np.float32)
    in_maps = []
    for b in range(NCORE):
        m = dict(extra)
        m["x_t"] = _cc(x[b].T)
        in_maps.append(m)
    res = run_bass_kernel_spmd(nc, in_maps, core_ids=list(range(NCORE)))
    outs = [np.asarray(res.results[b]["out_t"]).T for b in range(NCORE)]
    return np.stack(outs, axis=0).astype(np.float32)


# version 13
# speedup vs baseline: 1.0197x; 1.0197x over previous
import contextlib
import numpy as np
import concourse.bass as bass
import concourse.mybir as mybir
from concourse.bass_utils import run_bass_kernel_spmd

F32 = mybir.dt.float32
BF16 = mybir.dt.float16
AF = mybir.ActivationFunctionType
ALU = mybir.AluOpType

T = 4096
D = 1024
NCORE = 8
TCH = 512

ENGINES = ("tensor", "vector", "scalar", "gpsimd", "sync")


class _Op:
    __slots__ = ("oid", "engine", "fn", "deps", "is_dma", "inc_num", "vc", "tag", "dma_slot")

    def __init__(self, oid, engine, fn, is_dma, tag):
        self.oid = oid
        self.engine = engine
        self.fn = fn
        self.deps = []
        self.is_dma = is_dma
        self.inc_num = None
        self.vc = {}
        self.tag = tag


class Prog:
    def __init__(self):
        self.ops = {e: [] for e in ENGINES}
        self.all_ops = []
        self.last_writer = {}
        self.readers = {}
        self.pending_barrier = {}

    def emit(self, engine, fn, reads=(), writes=(), dma=False, tag=""):
        op = _Op(len(self.all_ops), engine, fn, dma, tag)
        deps = {}
        if engine in self.pending_barrier:
            for d in self.pending_barrier.pop(engine):
                deps[d.oid] = d
        for k in reads:
            w = self.last_writer.get(k)
            if w is not None:
                deps[w.oid] = w
        for k in writes:
            w = self.last_writer.get(k)
            if w is not None:
                deps[w.oid] = w
            for r in self.readers.get(k, ()):
                deps[r.oid] = r
        for d in deps.values():
            if d is op:
                continue
            if (d.engine == engine and d.engine == "tensor"
                    and not d.is_dma and not dma):
                continue
            op.deps.append(d)
        self.all_ops.append(op)
        self.ops[engine].append(op)
        for k in reads:
            self.readers.setdefault(k, []).append(op)
        for k in writes:
            self.last_writer[k] = op
            self.readers[k] = [op]
        return op

    def barrier(self):
        lasts = [self.ops[e][-1] for e in ENGINES if self.ops[e]]
        self.pending_barrier = {e: list(lasts) for e in ENGINES}

    NSLOT = 31

    def finalize_and_emit(self, nc):
        for op in self.all_ops:
            for d in op.deps:
                d.inc_num = -1
        counters = {}
        slot_uses = [0] * self.NSLOT
        next_slot = [0]
        for op in self.all_ops:
            if op.is_dma:
                sl = next_slot[0]
                next_slot[0] = (sl + 1) % self.NSLOT
                slot_uses[sl] += 1
                op.dma_slot = sl
                op.inc_num = slot_uses[sl]
            elif op.inc_num is not None:
                key = (op.engine, False)
                counters[key] = counters.get(key, 0) + 1
                op.inc_num = counters[key]

        def skey(op):
            return ("slot", op.dma_slot) if op.is_dma else (op.engine, False)

        def merge(a, b):
            for k, v in b.items():
                if a.get(k, 0) < v:
                    a[k] = v
        waits = {}
        know = {e: {} for e in ENGINES}
        n_waits = 0
        for op in self.all_ops:
            k = know[op.engine]
            need = {}
            for d in op.deps:
                merge(need, d.vc)
                key = skey(d)
                if d.inc_num is not None and need.get(key, 0) < d.inc_num:
                    need[key] = d.inc_num
            if op.is_dma and op.inc_num > 1:
                key = ("slot", op.dma_slot)
                if need.get(key, 0) < op.inc_num - 1:
                    need[key] = op.inc_num - 1
            wl = [(key, val) for key, val in need.items() if k.get(key, 0) < val]
            merge(k, need)
            vc = dict(need)
            if op.inc_num is not None and op.inc_num > 0:
                key = skey(op)
                if vc.get(key, 0) < op.inc_num:
                    vc[key] = op.inc_num
            op.vc = vc
            waits[op.oid] = wl
            n_waits += len(wl)
        n_incs = sum(1 for op in self.all_ops
                     if op.inc_num is not None and not op.is_dma)
        n_dma = sum(1 for op in self.all_ops if op.is_dma)
        print(f"Prog: {len(self.all_ops)} ops ({n_dma} dma), {n_incs} incs, "
              f"{n_waits} waits")

        with contextlib.ExitStack() as st:
            sems = {}
            for e in ENGINES:
                sems[(e, False)] = st.enter_context(nc.semaphore(f"p_{e}"))
            for i in range(self.NSLOT):
                sems[("slot", i)] = st.enter_context(nc.semaphore(f"ds{i}"))
            block = st.enter_context(nc.Block())
            prog = self

            def run_engine(engobj, ename):
                for op in prog.ops[ename]:
                    for (key, val) in waits[op.oid]:
                        engobj.wait_ge(sems[key], val * (16 if key[0] == "slot" else 1))
                    inst = op.fn(engobj)
                    if op.is_dma:
                        inst.then_inc(sems[("slot", op.dma_slot)], 16)
                    elif op.inc_num is not None:
                        if inst is None:
                            raise RuntimeError(f"op '{op.tag}' needs inc, fn returned None")
                        inst.then_inc(sems[(op.engine, False)], 1)

            block.tensor(lambda t: run_engine(t, "tensor"))
            block.vector(lambda v: run_engine(v, "vector"))
            block.scalar(lambda s: run_engine(s, "scalar"))
            block.gpsimd(lambda g: run_engine(g, "gpsimd"))
            block.sync(lambda s: run_engine(s, "sync"))



def _cc(a):
    return np.ascontiguousarray(a)


def host_prep(inputs):
    d = {}
    H, HD = 16, 64
    perm = np.concatenate([np.arange(h * HD, (h + 1) * HD, 2) for h in range(H)]
                          + [np.arange(h * HD + 1, (h + 1) * HD, 2) for h in range(H)])
    def _pi(h):
        return (h % 4) % 2
    def _cb(h):
        return h // 4 + 4 * ((h % 4) // 2)
    sigma = [0] * 16
    for h in range(16):
        sigma[_pi(h) * 8 + _cb(h)] = h
    d["aWq_t"] = _cc(np.asarray(inputs["L0_attn_a_Wq"])[perm, :].T)
    d["aWk_t"] = _cc(np.asarray(inputs["L0_attn_a_Wk"])[perm, :].T)
    d["aWv_t"] = _cc(np.asarray(inputs["L0_attn_a_Wv"]).T)
    d["aWg_t"] = _cc(np.asarray(inputs["L0_attn_a_Wg"]).T)
    d["aWo_t"] = _cc(np.asarray(inputs["L0_attn_a_Wo"]).T)
    d["aWb_t"] = _cc(np.asarray(inputs["L0_attn_a_Wb"]).T[:, sigma])
    d["aWgk_t"] = _cc(np.asarray(inputs["L0_attn_a_Wgk"]).T[:, sigma])
    cw = np.asarray(inputs["L0_attn_a_cqw"])[perm, 0, :]
    d["a_cqw"] = _cc(cw.reshape(8, 128, 4).transpose(1, 0, 2).reshape(128, 32))
    d["a_cqb"] = _cc(np.asarray(inputs["L0_attn_a_cqb"])[perm].reshape(8, 128).T)
    cw = np.asarray(inputs["L0_attn_a_ckw"])[perm, 0, :]
    d["a_ckw"] = _cc(cw.reshape(8, 128, 4).transpose(1, 0, 2).reshape(128, 32))
    d["a_ckb"] = _cc(np.asarray(inputs["L0_attn_a_ckb"])[perm].reshape(8, 128).T)
    cw = np.asarray(inputs["L0_attn_a_cvw"])[:, 0, :]
    d["a_cvw"] = _cc(cw.reshape(8, 128, 4).transpose(1, 0, 2).reshape(128, 32))
    d["a_cvb"] = _cc(np.asarray(inputs["L0_attn_a_cvb"]).reshape(8, 128).T)
    d["a_bb"] = _cc(np.asarray(inputs["L0_attn_a_bb"])[sigma].reshape(2, 8).T)
    d["a_bgk_dt"] = _cc((np.asarray(inputs["L0_attn_a_bgk"])
                         + np.asarray(inputs["L0_attn_a_dt_bias"]))[sigma].reshape(2, 8).T)
    d["a_Alog"] = _cc(np.asarray(inputs["L0_attn_a_A_log"])[sigma].reshape(2, 8).T)
    C = 64
    i_ar = np.arange(C)
    mlow = np.where(i_ar[:, None] > i_ar[None, :], 0.0, -1e30).astype(np.float32)
    mup = np.where(i_ar[:, None] < i_ar[None, :], 0.0, -1e30).astype(np.float32)
    d["c_maskhpL"] = _cc(np.tile(mlow, (2, 8)))
    d["c_maskhpU"] = _cc(np.tile(mup, (2, 8)))
    dphp = np.zeros((128, 8), np.float32)
    onhp = np.zeros((128, 1), np.float32)
    on_w = np.asarray(inputs["L0_attn_a_onorm_w"])
    for h in range(16):
        dphp[64 * _pi(h):64 * _pi(h) + 64, _cb(h)] = np.asarray(inputs["L0_attn_a_Dp"])[h]
    onhp[0:64, 0] = on_w
    onhp[64:128, 0] = on_w
    d["a_Dphp"] = _cc(dphp)
    d["a_onormc"] = _cc(onhp)
    d["c_blk8"] = _cc(np.kron(np.eye(8, dtype=np.float32), np.ones((1, 64), np.float32)))
    d["c_eyehp"] = _cc(np.tile(np.eye(64, dtype=np.float32), (2, 8)))
    d["c_lind4"] = _cc(np.eye(4, dtype=np.float32).repeat(32, axis=0))
    d["c_lind4T"] = _cc(np.eye(4, dtype=np.float32).repeat(32, axis=1))
    d["c_lind2"] = _cc(np.eye(2, dtype=np.float32).repeat(64, axis=0))
    d["c_lind2T"] = _cc(np.eye(2, dtype=np.float32).repeat(64, axis=1))
    inv_freq = 1.0 / 10000.0 ** (np.arange(0, HD, 2, dtype=np.float64) / HD)
    fsel = inv_freq[(2 * np.arange(32)) % 32]
    fr = np.outer(fsel, np.arange(T, dtype=np.float64))
    d["rope_cos"] = _cc(np.tile(np.cos(fr).astype(np.float32), (4, 1)))
    d["rope_sin"] = _cc(np.tile(np.sin(fr).astype(np.float32), (4, 1)))

    d["bwq_t"] = _cc(np.asarray(inputs["L0_attn_b_wq"]).T)
    d["bwk_t"] = _cc(np.asarray(inputs["L0_attn_b_wk"]).T)
    d["bwv_t"] = _cc(np.asarray(inputs["L0_attn_b_wv"]).T)
    d["bwo_t"] = _cc(np.asarray(inputs["L0_attn_b_wo"]).T)
    d["b_bqc"] = _cc(np.asarray(inputs["L0_attn_b_bq"]).reshape(8, 128).T)
    d["b_bkc"] = _cc(np.asarray(inputs["L0_attn_b_bk"]).reshape(8, 128).T)
    d["b_bvr"] = _cc(np.asarray(inputs["L0_attn_b_bv"]).reshape(1, 1024))
    d["b_boc"] = _cc(np.asarray(inputs["L0_attn_b_bo"]).reshape(8, 128).T)
    d["b_pw1_t"] = _cc(np.transpose(np.asarray(inputs["L0_attn_b_pw1"]), (0, 2, 1)))
    d["b_pb1_t"] = _cc(np.asarray(inputs["L0_attn_b_pb1"]).reshape(3, 4, 128)
                       .transpose(2, 0, 1).reshape(128, 12))
    d["b_pw2_t"] = _cc(np.transpose(np.asarray(inputs["L0_attn_b_pw2"]), (0, 2, 1)))
    d["b_pb2_t"] = _cc(np.asarray(inputs["L0_attn_b_pb2"]).T)
    d["b_sw1_t"] = _cc(np.asarray(inputs["L0_attn_b_sw1"]).T)
    d["b_sb1_t"] = _cc(np.asarray(inputs["L0_attn_b_sb1"]).reshape(4, 128).T)
    d["b_sw2_t"] = _cc(np.asarray(inputs["L0_attn_b_sw2"]).T)
    d["b_sb2_t"] = _cc(np.asarray(inputs["L0_attn_b_sb2"]).reshape(3, 1))
    km = np.arange(128)
    qm = np.arange(512)
    d["c_att_mask"] = np.stack([
        np.where((r * 128 + km)[:, None] > qm[None, :], -1e30, 0.0).astype(np.float32)
        for r in range(4)])

    d["fWg"] = _cc(np.asarray(inputs["L0_ffn_Wg"]))
    d["fbg"] = _cc(np.asarray(inputs["L0_ffn_bg"]).reshape(32, 128).T)
    d["fWu"] = _cc(np.asarray(inputs["L0_ffn_Wu"]))
    d["fbu"] = _cc(np.asarray(inputs["L0_ffn_bu"]).reshape(32, 128).T)
    d["fWd"] = _cc(np.asarray(inputs["L0_ffn_Wd"]))
    d["fbd"] = _cc(np.asarray(inputs["L0_ffn_bd"]).reshape(8, 128).T)
    d["fWq"] = _cc(np.asarray(inputs["L0_ffn_Wq"]))
    d["fkeys_t"] = _cc(np.transpose(np.asarray(inputs["L0_ffn_keys"]), (2, 0, 3, 1)))
    d["f_de_t"] = _cc(np.asarray(inputs["L0_ffn_down_embed"]).T)
    d["f_ue"] = _cc(np.asarray(inputs["L0_ffn_up_embed"]))

    d["c_ident"] = np.eye(128, dtype=np.float32)
    ind_split = np.zeros((8, 128, 16), dtype=np.float32)
    for i in range(4):
        for r in range(128):
            ind_split[i, r, 4 * i + r // 32] = 1.0
            ind_split[4 + i, r, 4 * i + r // 32] = 1.0
    d["c_ind_split"] = ind_split
    ind_hc = np.zeros((8, 128, 16), dtype=np.float32)
    for j in range(8):
        for r in range(128):
            ind_hc[j, r, 2 * j + r // 64] = 1.0
    d["c_ind_hc"] = ind_hc
    d["c_ind_hc_T"] = _cc(np.transpose(ind_hc, (0, 2, 1)))
    C = 64
    i_ar = np.arange(C)
    d["c_mask_strictlow"] = np.where(i_ar[:, None] > i_ar[None, :], 0.0, -1e30).astype(np.float32)
    d["c_mask_strictup"] = np.where(i_ar[:, None] < i_ar[None, :], 0.0, -1e30).astype(np.float32)
    d["c_eye64"] = np.eye(64, dtype=np.float32)
    return d



class Ctx:
    def __init__(self, nc, P, stack):
        self.nc = nc
        self.P = P
        self.stack = stack
        self.dram = {}
        self.sb = {}
        self.uid = 0

    def u(self, base):
        self.uid += 1
        return f"{base}{self.uid}"

    def sbuf(self, name, shape, dtype, stack=None):
        t = (stack or self.stack).enter_context(self.nc.sbuf_tensor(name, shape, dtype))
        self.sb[name] = t
        return t

    def load_const(self, name, dtype=BF16, stack=None, shape=None):
        ap = self.dram[name]
        shp = list(shape if shape is not None else ap.shape)
        if len(shp) == 1:
            shp = [1] + shp
        t = self.sbuf(f"sb_{name}", shp, dtype, stack=stack)
        eng = "gpsimd" if dtype != F32 else "sync"
        src = ap if len(ap.shape) > 1 else ap.rearrange("n -> 1 n")
        self.P.emit(eng, lambda e: e.dma_start(out=t[:], in_=src),
                    writes=[f"sb_{name}"], dma=True, tag=f"ld_{name}")
        return t



class LNHelper:

    def __init__(self, cx, cols, name, stack=None):
        self.cx = cx
        self.cols = cols
        self.n = name
        cx.sbuf(f"{name}_xb", [128, 8 * cols], BF16, stack=stack)
        cx.sbuf(f"{name}_st", [1, 2 * cols], F32, stack=stack)
        cx.sbuf(f"{name}_tmp", [128, cols], F32, stack=stack)


    def emit(self, x_keys, xsb, xn_dst_ap, xn_keys, psA="pb6", psB="pb7"):
        cx, P, sb, cols, n = self.cx, self.cx.P, self.cx.sb, self.cols, self.n
        xb, st, tmp = sb[f"{n}_xb"], sb[f"{n}_st"], sb[f"{n}_tmp"]
        xnb = xb
        x2b = xb
        rk = list(x_keys)
        P.emit("vector", lambda v: v.tensor_copy(xb[:], xsb),
               reads=rk, writes=[f"{n}_xb"], tag=f"{n}cast")

        def mk_mm(src, bank):
            def f(t):
                last = None
                for i in range(8):
                    last = t.matmul(sb[bank][0:1, 0:cols], sb["onescol"][:],
                                    src[:, i * cols:(i + 1) * cols],
                                    start=(i == 0), stop=(i == 7))
                return last
            return f
        P.emit("tensor", mk_mm(xb, psA), reads=[f"{n}_xb", "onescol"],
               writes=[psA], tag=f"{n}mm1")
        P.emit("scalar", lambda s: s.activation(x2b[:], xsb, AF.Square),
               reads=rk + [f"{n}_xb"], writes=[f"{n}_xb"], tag=f"{n}sq")
        P.emit("tensor", mk_mm(x2b, psB), reads=[f"{n}_xb", "onescol"],
               writes=[psB], tag=f"{n}mm2")

        def f_mu(v):
            v.tensor_scalar(st[0:1, 0:cols], sb[psA][0:1, 0:cols], 1.0 / D, None, ALU.mult)
            return v.tensor_scalar(st[0:1, cols:], sb[psB][0:1, 0:cols],
                                   1.0 / D, None, ALU.mult)
        P.emit("vector", f_mu, reads=[psA, psB], writes=[f"{n}_st"], tag=f"{n}mu")

        P.emit("vector", lambda v: v.tensor_mul(tmp[0:1, 0:cols], st[0:1, 0:cols],
                                                st[0:1, 0:cols]),
               reads=[f"{n}_st"], writes=[f"{n}_tmp"], tag=f"{n}musq")
        P.emit("vector", lambda v: v.tensor_sub(st[0:1, cols:], st[0:1, cols:],
                                                tmp[0:1, 0:cols]),
               reads=[f"{n}_tmp"], writes=[f"{n}_st"], tag=f"{n}var")

        P.emit("scalar", lambda s: s.activation(st[0:1, cols:], st[0:1, cols:],
                                                AF.Ln, bias=sb["epscol"][0:1, 0:1]),
               reads=[f"{n}_st", "epscol"], writes=[f"{n}_st"], tag=f"{n}ln")
        P.emit("scalar", lambda s: s.activation(st[0:1, cols:], st[0:1, cols:],
                                                AF.Exp, scale=-0.5),
               reads=[f"{n}_st"], writes=[f"{n}_st"], tag=f"{n}rstd")

        def f_bc(t):
            t.matmul(sb[psA][:, 0:cols], sb["ones_row"][0:1, :], st[0:1, 0:cols],
                     start=True, stop=True)
            return t.matmul(sb[psB][:, 0:cols], sb["ones_row"][0:1, :],
                            st[0:1, cols:], start=True, stop=True)
        P.emit("tensor", f_bc, reads=[f"{n}_st", "ones_row"], writes=[psA, psB],
               tag=f"{n}bc")

        for i in range(8):
            P.emit("vector", lambda v, i=i: v.tensor_sub(
                tmp[:, 0:cols], xsb[:, i * cols:(i + 1) * cols], sb[psA][:, 0:cols]),
                reads=rk + [psA], writes=[f"{n}_tmp"], tag=f"{n}ns{i}")
            P.emit("vector", lambda v, i=i: v.tensor_mul(
                xnb[:, i * cols:(i + 1) * cols], tmp[:, 0:cols],
                sb[psB][:, 0:cols]),
                reads=[f"{n}_tmp", psB], writes=[f"{n}_xb"], tag=f"{n}nm{i}")
        if xn_dst_ap is not None:
            P.emit("sync", lambda e: e.dma_start(
                out=xn_dst_ap, in_=xnb[:].rearrange("p (a n) -> p a n", a=8)),
                reads=[f"{n}_xb"], writes=list(xn_keys), dma=True, tag=f"{n}st")
        return f"{n}_xb"



def emit_ln_input(cx):
    P = cx.P
    xin, xn0 = cx.dram["x_in"], cx.dram["xn0"]
    with contextlib.ExitStack() as stk:
        xbuf = [cx.sbuf(f"ph0_x{i}", [128, 8 * TCH], F32, stack=stk) for i in range(2)]
        ln = LNHelper(cx, TCH, "ph0ln", stack=stk)
        for c in range(T // TCH):
            b = c % 2
            sl = np.s_[c * TCH:(c + 1) * TCH]
            key = f"ph0x{b}"
            P.emit("sync", lambda e, sl=sl, b=b: e.dma_start(
                out=xbuf[b][:].rearrange("p (a n) -> p a n", a=8),
                in_=xin[:, sl].rearrange("(a p) n -> p a n", p=128)),
                writes=[key], dma=True, tag=f"ph0ld{c}")
            ln.emit([key], xbuf[b][:],
                    xn0[:, sl].rearrange("(a p) n -> p a n", p=128), [f"xn0:{c}"])



def emit_ffn(cx, x_in, xn_in, x_out, xn_out, idx):
    nc, P = cx.nc, cx.P
    pref = f"f{idx}"
    xnd = cx.dram[xn_in]
    xind = cx.dram[x_in]
    xoutd = cx.dram[x_out]
    xtmpd = cx.dram["xtmp"]
    wtdd = cx.dram["wtd"]
    NTCH = T // TCH

    def _part_a1():
        with contextlib.ExitStack() as stk:
            sbuf = lambda nm, shp, dt: cx.sbuf(f"{pref}{nm}", shp, dt, stack=stk)
            sb = cx.sb
            de_t = sbuf("de", [128, 8 * 4096], BF16)
            wq = sbuf("wq", [128, 8 * 512], BF16)
            keys = sbuf("keys", [128, 8 * 64], BF16)
            identb = cx.sb["identb"]
            for j in range(4):
                P.emit("gpsimd", lambda e, j=j: e.dma_start(
                    out=de_t[:, j * 8192:(j + 1) * 8192].rearrange("p (a n) -> p a n", a=2),
                    in_=cx.dram["f_de_t"][:, j * 1024:(j + 1) * 1024]
                        .rearrange("(a p) n -> p a n", p=128)),
                    writes=[f"{pref}de:{j}"], dma=True, tag=f"{pref}ldde{j}")
            P.emit("gpsimd", lambda e: e.dma_start(
                out=wq[:].rearrange("p (a n) -> p a n", a=8),
                in_=cx.dram["fWq"].rearrange("(a p) n -> p a n", p=128)),
                writes=[f"{pref}wq"], dma=True, tag=f"{pref}ldwq")
            P.emit("gpsimd", lambda e: e.dma_start(
                out=keys[0:64, :].rearrange("p (a n) -> p a n", a=8),
                in_=cx.dram["fkeys_t"].rearrange("p h n k -> n (p h) k")),
                writes=[f"{pref}keys"], dma=True, tag=f"{pref}ldk1")
            P.emit("gpsimd", lambda e: e.dma_start(
                out=keys[64:128, :].rearrange("p (a n) -> p a n", a=8),
                in_=cx.dram["fkeys_t"].rearrange("p h n k -> n (p h) k")),
                writes=[f"{pref}keys"], dma=True, tag=f"{pref}ldk2")

            xnc = [sbuf(f"xnc{i}", [128, 8 * TCH], BF16) for i in range(2)]
            q = sbuf("q", [128, 4 * TCH], BF16)
            sims = sbuf("sims", [128, 8 * 64], F32)
            top8 = sbuf("top8", [128, 8 * 8], F32)
            cmb = sbuf("cmb", [128, 64], F32)
            c8 = sbuf("c8", [128, 8], F32)
            colv = sbuf("colv", [128, 8], F32)
            zcol = sbuf("zcol", [128, 4], F32)
            efx = sbuf("efx", [128, 8 * 64], F32)
            smw = sbuf("smw", [128, 4096], BF16)
            hbuf = sbuf("hbuf", [128, 4096], BF16)
            dsil = sbuf("dsil", [128, 4096], BF16)
            wt = sbuf("wt", [128, 32 * 128], BF16)

            for c in range(NTCH):
                bi = c % 2
                sl = np.s_[c * TCH:(c + 1) * TCH]
                kxn = f"{pref}xnc{bi}"
                P.emit("sync", lambda e, sl=sl, bi=bi: e.dma_start(
                    out=xnc[bi][:].rearrange("p (a n) -> p a n", a=8),
                    in_=xnd[:, sl].rearrange("(a p) n -> p a n", p=128)),
                    reads=[f"{xn_in}:{c}"], writes=[kxn], dma=True, tag=f"{pref}ldxn{c}")
                for m in range(4):
                    def f_q(t, m=m, bi=bi):
                        last = None
                        for k in range(8):
                            last = t.matmul(sb["pb0"][:, 0:TCH],
                                            wq[:, k * 512 + m * 128: k * 512 + (m + 1) * 128],
                                            xnc[bi][:, k * TCH:(k + 1) * TCH],
                                            start=(k == 0), stop=(k == 7))
                        return last
                    P.emit("tensor", f_q, reads=[kxn, f"{pref}wq"], writes=["pb0"],
                           tag=f"{pref}qmm{c}_{m}")
                    P.emit("vector", lambda v, m=m: v.tensor_copy(
                        q[:, m * TCH:(m + 1) * TCH], sb["pb0"][:, 0:TCH]),
                        reads=["pb0"], writes=[f"{pref}q:{m}"], tag=f"{pref}qe{c}_{m}")
                for tb in range(TCH // 128):
                    def f_sims(t, tb=tb):
                        last = None
                        for p_ in range(2):
                            for h_ in range(4):
                                g = p_ * 4 + h_
                                slot = SLOT[(p_, h_)]
                                tile = 2 * p_ + h_ // 2
                                r0 = 64 * (h_ % 2)
                                bank = "pb1" if h_ % 2 == 0 else "pb2"
                                col = (slot % 4) * 64
                                last = t.matmul(
                                    sb[bank][:, col:col + 64],
                                    q[r0:r0 + 64,
                                      tile * TCH + tb * 128: tile * TCH + (tb + 1) * 128],
                                    keys[r0:r0 + 64, g * 64:(g + 1) * 64],
                                    start=True, stop=True,
                                    tile_position=(r0, 0))
                        return last
                    P.emit("tensor", f_sims, reads=[f"{pref}q:{m}" for m in range(4)]
                           + [f"{pref}keys"], writes=["pb1", "pb2"], tag=f"{pref}sims{c}_{tb}")
                    def f_sime(sc_):
                        sc_.activation(sims[:, 0:256], sb["pb1"][:, 0:256], AF.Copy)
                        return sc_.activation(sims[:, 256:512], sb["pb2"][:, 0:256],
                                              AF.Copy)
                    P.emit("scalar", f_sime, reads=["pb1", "pb2"], writes=[f"{pref}sims"],
                           tag=f"{pref}sime{c}_{tb}")
                    for s_ in range(8):
                        P.emit("vector", lambda v, s_=s_: v.max(
                            out=top8[:, s_ * 8:(s_ + 1) * 8], in_=sims[:, s_ * 64:(s_ + 1) * 64]),
                            reads=[f"{pref}sims"], writes=[f"{pref}top8"],
                            tag=f"{pref}max{c}_{tb}_{s_}")
                    def f_negth(v):
                        return v.tensor_scalar(
                            colv[:, 0:8],
                            top8[:].rearrange("p (s e) -> p s e", s=8)[:, :, 7],
                            -1.0, None, ALU.mult)
                    P.emit("vector", f_negth, reads=[f"{pref}top8"], writes=[f"{pref}colv"],
                           tag=f"{pref}negth{c}_{tb}")
                    for s_ in range(8):
                        def f_exp(sc, s_=s_):
                            return sc.activation(efx[:, s_ * 64:(s_ + 1) * 64],
                                                 sims[:, s_ * 64:(s_ + 1) * 64], AF.Exp,
                                                 bias=colv[:, s_:s_ + 1])
                        P.emit("scalar", f_exp, reads=[f"{pref}sims", f"{pref}colv"],
                               writes=[f"{pref}efx"], tag=f"{pref}fex{c}_{tb}_{s_}")
                        def f_mask(v, s_=s_):
                            return v.scalar_tensor_tensor(
                                efx[:, s_ * 64:(s_ + 1) * 64], sims[:, s_ * 64:(s_ + 1) * 64],
                                top8[:, s_ * 8 + 7: s_ * 8 + 8], efx[:, s_ * 64:(s_ + 1) * 64],
                                ALU.is_ge, ALU.mult)
                        P.emit("vector", f_mask, reads=[f"{pref}sims", f"{pref}top8",
                                                        f"{pref}efx"],
                               writes=[f"{pref}efx"], tag=f"{pref}fm{c}_{tb}_{s_}")
                    for h in range(4):
                        s_x = SLOT[(0, h)]
                        s_y = SLOT[(1, h)]
                        def f_cmb(v, s_x=s_x, s_y=s_y):
                            return v.tensor_tensor(
                                cmb[:].rearrange("p (a b) -> p a b", a=8),
                                top8[:, s_x * 8:(s_x + 1) * 8].unsqueeze(2)
                                    .to_broadcast([128, 8, 8]),
                                top8[:, s_y * 8:(s_y + 1) * 8].unsqueeze(1)
                                    .to_broadcast([128, 8, 8]),
                                ALU.add)
                        P.emit("vector", f_cmb, reads=[f"{pref}top8"], writes=[f"{pref}cmb"],
                               tag=f"{pref}cmb{c}_{tb}_{h}")
                        P.emit("vector", lambda v: v.max(out=c8[:], in_=cmb[:]),
                               reads=[f"{pref}cmb"], writes=[f"{pref}c8"],
                               tag=f"{pref}c8{c}_{tb}_{h}")
                        P.emit("vector", lambda v: v.tensor_scalar(
                            zcol[:, 0:1], c8[:, 7:8], -1.0, None, ALU.mult),
                            reads=[f"{pref}c8"], writes=[f"{pref}zcol"],
                            tag=f"{pref}nthc{c}_{tb}_{h}")
                        def f_ec8(sc):
                            return sc.activation(c8[:], c8[:], AF.Exp, bias=zcol[:, 0:1],
                                                 accum_out=zcol[:, 1:2])
                        P.emit("scalar", f_ec8, reads=[f"{pref}c8", f"{pref}zcol"],
                               writes=[f"{pref}c8", f"{pref}zcol"],
                               tag=f"{pref}ec8{c}_{tb}_{h}")
                        P.emit("vector", lambda v: v.reciprocal(zcol[:, 2:3], zcol[:, 1:2]),
                               reads=[f"{pref}zcol"], writes=[f"{pref}zcol"],
                               tag=f"{pref}rz{c}_{tb}_{h}")
                        def f_sumth(v, s_x=s_x, s_y=s_y):
                            return v.tensor_add(zcol[:, 3:4], colv[:, s_x:s_x + 1],
                                                colv[:, s_y:s_y + 1])
                        P.emit("vector", f_sumth, reads=[f"{pref}colv"],
                               writes=[f"{pref}zcol"], tag=f"{pref}sth{c}_{tb}_{h}")
                        def f_es(sc):
                            return sc.activation(zcol[:, 3:4], zcol[:, 3:4], AF.Exp,
                                                 scale=-1.0, bias=zcol[:, 0:1])
                        P.emit("scalar", f_es, reads=[f"{pref}zcol"], writes=[f"{pref}zcol"],
                               tag=f"{pref}esc{c}_{tb}_{h}")
                        def f_stau(v):
                            v.tensor_mul(zcol[:, 3:4], zcol[:, 3:4], zcol[:, 2:3])
                            return v.tensor_scalar(zcol[:, 1:2], zcol[:, 2:3],
                                                   1.0 - 1e-6, None, ALU.mult)
                        P.emit("vector", f_stau, reads=[f"{pref}zcol"],
                               writes=[f"{pref}zcol"], tag=f"{pref}stau{c}_{tb}_{h}")
                        def f_scale(v, s_x=s_x):
                            return v.tensor_scalar(efx[:, s_x * 64:(s_x + 1) * 64],
                                                   efx[:, s_x * 64:(s_x + 1) * 64],
                                                   zcol[:, 3:4], None, ALU.mult)
                        P.emit("vector", f_scale, reads=[f"{pref}zcol", f"{pref}efx"],
                               writes=[f"{pref}efx"], tag=f"{pref}fsc{c}_{tb}_{h}")
                        dst = smw if h == 0 else hbuf
                        dk = f"{pref}smw" if h == 0 else f"{pref}hbuf"
                        def f_outer(v, dst=dst, s_x=s_x, s_y=s_y):
                            return v.tensor_tensor(
                                dst[:].rearrange("p (a b) -> p a b", a=64),
                                efx[:, s_x * 64:(s_x + 1) * 64].unsqueeze(2)
                                    .to_broadcast([128, 64, 64]),
                                efx[:, s_y * 64:(s_y + 1) * 64].unsqueeze(1)
                                    .to_broadcast([128, 64, 64]),
                                ALU.mult)
                        P.emit("vector", f_outer, reads=[f"{pref}efx"], writes=[dk],
                               tag=f"{pref}out{c}_{tb}_{h}")
                        def f_thr(v, dst=dst):
                            return v.scalar_tensor_tensor(dst[:], dst[:], zcol[:, 1:2],
                                                          dst[:], ALU.is_ge, ALU.mult)
                        P.emit("vector", f_thr, reads=[f"{pref}zcol", dk], writes=[dk],
                               tag=f"{pref}thr{c}_{tb}_{h}")
                        if h > 0:
                            P.emit("vector", lambda v: v.tensor_add(smw[:], smw[:], hbuf[:]),
                                   reads=[f"{pref}hbuf", f"{pref}smw"], writes=[f"{pref}smw"],
                                   tag=f"{pref}acch{c}_{tb}_{h}")
                    for half in range(2):
                        def f_dots(t, tb=tb, half=half, bi=bi):
                            last = None
                            for nn in range(4):
                                for k in range(8):
                                    last = t.matmul(
                                        sb[f"pb{nn}"][:, 0:512],
                                        xnc[bi][:, k * TCH + tb * 128: k * TCH + (tb + 1) * 128],
                                        de_t[:, k * 4096 + half * 2048 + nn * 512:
                                             k * 4096 + half * 2048 + (nn + 1) * 512],
                                        start=(k == 0), stop=(k == 7))
                            return last
                        P.emit("tensor", f_dots, reads=[kxn] + [f"{pref}de:{j}" for j in range(4)],
                               writes=["pb0", "pb1", "pb2", "pb3"],
                               tag=f"{pref}dmm{c}_{tb}_{half}")
                        for nn in range(4):
                            def f_sil(s, nn=nn, half=half):
                                return s.activation(
                                    dsil[:, half * 2048 + nn * 512: half * 2048 + (nn + 1) * 512],
                                    sb[f"pb{nn}"][:, 0:512], AF.Silu)
                            P.emit("scalar", f_sil, reads=[f"pb{nn}"], writes=[f"{pref}dsil"],
                                   tag=f"{pref}sil{c}_{tb}_{half}_{nn}")
                    P.emit("vector", lambda v: v.tensor_mul(hbuf[:], dsil[:], smw[:]),
                           reads=[f"{pref}dsil", f"{pref}smw"], writes=[f"{pref}hbuf"],
                           tag=f"{pref}wt{c}_{tb}")
                    for grp in range(8):
                        def f_tr(t, grp=grp):
                            last = None
                            for j4 in range(4):
                                j = grp * 4 + j4
                                last = t.matmul(sb["pb4"][:, j4 * 128:(j4 + 1) * 128],
                                                hbuf[:, j * 128:(j + 1) * 128], identb[:],
                                                start=True, stop=True)
                            return last
                        P.emit("tensor", f_tr, reads=[f"{pref}hbuf", "identb"],
                               writes=["pb4"], tag=f"{pref}tr{c}_{tb}_{grp}")
                        P.emit("scalar", lambda sc_, grp=grp: sc_.activation(
                            wt[:, grp * 512:(grp + 1) * 512], sb["pb4"][:, 0:512],
                            AF.Copy),
                            reads=["pb4"], writes=[f"{pref}wtb"], tag=f"{pref}tre{c}_{tb}_{grp}")
                    def f_wst(e, c=c, tb=tb):
                        return e.dma_start(
                            out=wtdd[:, :, c * TCH + tb * 128: c * TCH + (tb + 1) * 128]
                                .rearrange("j p n -> p j n"),
                            in_=wt[:].rearrange("p (j n) -> p j n", j=32))
                    P.emit("sync", f_wst, reads=[f"{pref}wtb"],
                           writes=[f"wtd:{c}_{tb}"], dma=True, tag=f"{pref}wst{c}_{tb}")
            P.barrier()
    _part_a1()

    def _part_a2():
        with contextlib.ExitStack() as stk:
            sbuf = lambda nm, shp, dt: cx.sbuf(f"{pref}{nm}", shp, dt, stack=stk)
            sb = cx.sb
            ue = sbuf("ue", [128, 32 * 1024], BF16)
            wtl2 = [sbuf(f"wtl{i}", [128, 32 * 512], BF16) for i in range(2)]
            esacc = sbuf("esacc", [128, 8 * TCH], F32)
            xchunk = sbuf("xchunk", [128, 8 * TCH], F32)
            for j in range(4):
                P.emit("gpsimd", lambda e, j=j: e.dma_start(
                    out=ue[:, j * 8192:(j + 1) * 8192].rearrange("p (a n) -> p a n", a=8),
                    in_=cx.dram["f_ue"][j * 1024:(j + 1) * 1024, :]
                        .rearrange("(a p) n -> p a n", p=128)),
                    writes=[f"{pref}ue:{j}"], dma=True, tag=f"{pref}ldue{j}")
            for c in range(NTCH):
                sl = np.s_[c * TCH:(c + 1) * TCH]
                wtl = wtl2[c % 2]
                P.emit("sync", lambda e, c=c, wtl=wtl: e.dma_start(
                    out=wtl[:].rearrange("p (j n) -> p j n", j=32),
                    in_=wtdd[:, :, c * TCH:(c + 1) * TCH].rearrange("j p n -> p j n")),
                    reads=[f"wtd:{c}_{tb}" for tb in range(4)],
                    writes=[f"{pref}wtl{c % 2}"],
                    dma=True, tag=f"{pref}wld{c}")
                P.emit("sync", lambda e, sl=sl: e.dma_start(
                    out=xchunk[:].rearrange("p (a n) -> p a n", a=8),
                    in_=xind[:, sl].rearrange("(a p) n -> p a n", p=128)),
                    reads=[f"{x_in}:{c}"], writes=[f"{pref}xch"], dma=True,
                    tag=f"{pref}ldx{c}")
                for m in range(8):
                    pes = "pb4" if m % 2 == 0 else "pb5"

                    def f_es(t, m=m, pes=pes, wtl=wtl):
                        last = None
                        for j in range(32):
                            last = t.matmul(
                                sb[pes][:, 0:512],
                                ue[:, j * 1024 + m * 128: j * 1024 + (m + 1) * 128],
                                wtl[:, j * 512:(j + 1) * 512],
                                start=(j == 0), stop=(j == 31))
                        return last
                    P.emit("tensor", f_es, reads=[f"{pref}wtl{c % 2}"]
                           + [f"{pref}ue:{j}" for j in range(4)],
                           writes=[pes], tag=f"{pref}es{c}_{m}")

                    def f_ese(v, m=m, pes=pes):
                        return v.tensor_add(
                            xchunk[:, m * TCH:(m + 1) * TCH],
                            xchunk[:, m * TCH:(m + 1) * TCH],
                            sb[pes][:, 0:512])
                    P.emit("vector", f_ese, reads=[pes, f"{pref}xch"],
                           writes=[f"{pref}xch"], tag=f"{pref}ese{c}_{m}")
                P.emit("sync", lambda e, sl=sl: e.dma_start(
                    out=xtmpd[:, sl].rearrange("(a p) n -> p a n", p=128),
                    in_=xchunk[:].rearrange("p (a n) -> p a n", a=8)),
                    reads=[f"{pref}xch"], writes=[f"xtmp:{c}"], dma=True,
                    tag=f"{pref}stx{c}")
            P.barrier()
    _part_a2()

    NBLK = 4
    BW = 1024
    def _part_b():
        with contextlib.ExitStack() as stk:
            sbuf = lambda nm, shp, dt: cx.sbuf(f"{pref}{nm}", shp, dt, stack=stk)
            sb = cx.sb
            acc = sbuf("macc", [128, 8 * 512], F32)
            wgh = [sbuf(f"mwg{i}", [128, 8 * 512], BF16) for i in range(3)]
            wuh = [sbuf(f"mwu{i}", [128, 8 * 512], BF16) for i in range(3)]
            wdh = [sbuf(f"mwd{i}", [128, 8 * 512], BF16) for i in range(3)]
            xnc = [sbuf(f"mxn{i}", [128, 8 * TCH], BF16) for i in range(2)]
            hg2 = [sbuf(f"mhg{i}", [128, 8 * 512], BF16) for i in range(2)]
            hh2 = [sbuf(f"mh{i}", [128, 8 * 512], BF16) for i in range(2)]
            bgs = sbuf("mbg", [128, 32], F32)
            bus = sbuf("mbu", [128, 32], F32)
            bds = sbuf("mbd", [128, 8], F32)
            xck = sbuf("mxck", [128, 8 * TCH], F32)
            P.emit("sync", lambda e: e.dma_start(
                out=bgs[:], in_=cx.dram["fbg"]),
                writes=[f"{pref}mbg"], dma=True, tag=f"{pref}ldbg")
            P.emit("sync", lambda e: e.dma_start(
                out=bus[:], in_=cx.dram["fbu"]),
                writes=[f"{pref}mbu"], dma=True, tag=f"{pref}ldbu")
            P.emit("sync", lambda e: e.dma_start(
                out=bds[:], in_=cx.dram["fbd"]),
                writes=[f"{pref}mbd"], dma=True, tag=f"{pref}ldbd")
            ln = LNHelper(cx, TCH, f"{pref}mln", stack=stk) if xn_out is not None else None

            units = [(tq, blk) for tq in range(8) for blk in range(NBLK)]

            def emit_wloads(u, hfs=(0, 1)):
                blk = units[u][1]
                for hf in hfs:
                    par = (u * 2 + hf) % 3
                    c0 = blk * BW + hf * 512
                    P.emit("gpsimd", lambda e, c0=c0, par=par: e.dma_start(
                        out=wgh[par][:].rearrange("p (a n) -> p a n", a=8),
                        in_=cx.dram["fWg"][:, c0:c0 + 512]
                        .rearrange("(a p) n -> p a n", p=128)),
                        writes=[f"{pref}mwg{par}"], dma=True, tag=f"{pref}lwg{u}_{hf}")
                    P.emit("gpsimd", lambda e, c0=c0, par=par: e.dma_start(
                        out=wuh[par][:].rearrange("p (a n) -> p a n", a=8),
                        in_=cx.dram["fWu"][:, c0:c0 + 512]
                        .rearrange("(a p) n -> p a n", p=128)),
                        writes=[f"{pref}mwu{par}"], dma=True, tag=f"{pref}lwu{u}_{hf}")
                    P.emit("gpsimd", lambda e, blk=blk, hf=hf, par=par: e.dma_start(
                        out=wdh[par][:].rearrange("p (a n) -> p a n", a=8),
                        in_=cx.dram["fWd"][blk * BW:(blk + 1) * BW,
                                           hf * 512:(hf + 1) * 512]
                        .rearrange("(a p) n -> p a n", p=128)),
                        writes=[f"{pref}mwd{par}"], dma=True, tag=f"{pref}lwd{u}_{hf}")

            emit_wloads(0)
            for u, (tq, blk) in enumerate(units):
                bi = tq % 2
                hb = blk % 2
                kxn = f"{pref}mxn{bi}"
                if u == 0 or (blk == 1 and tq + 1 < 8):
                    ltq = tq if u == 0 else tq + 1
                    sl = np.s_[ltq * TCH:(ltq + 1) * TCH]
                    P.emit("sync", lambda e, sl=sl, ltq=ltq: e.dma_start(
                        out=xnc[ltq % 2][:].rearrange("p (a n) -> p a n", a=8),
                        in_=xnd[:, sl].rearrange("(a p) n -> p a n", p=128)),
                        reads=[f"{xn_in}:{ltq}"], writes=[f"{pref}mxn{ltq % 2}"],
                        dma=True, tag=f"{pref}mxl{ltq}")
                if u + 1 < len(units):
                    emit_wloads(u + 1, hfs=(0,))
                for m in range(8):
                    par = (u * 2 + m // 4) % 3
                    m4 = m % 4
                    pg = f"pb{m % 2}"
                    pu = f"pb{2 + m % 2}"

                    def f_g(t, m4=m4, par=par, pg=pg, bi=bi):
                        last = None
                        for k in range(8):
                            last = t.matmul(
                                sb[pg][:, 0:512],
                                wgh[par][:, k * 512 + m4 * 128: k * 512 + (m4 + 1) * 128],
                                xnc[bi][:, k * TCH:(k + 1) * TCH],
                                start=(k == 0), stop=(k == 7))
                        return last
                    P.emit("tensor", f_g, reads=[kxn, f"{pref}mwg{par}"], writes=[pg],
                           tag=f"{pref}gmm{u}_{m}")

                    def f_u(t, m4=m4, par=par, pu=pu, bi=bi):
                        last = None
                        for k in range(8):
                            last = t.matmul(
                                sb[pu][:, 0:512],
                                wuh[par][:, k * 512 + m4 * 128: k * 512 + (m4 + 1) * 128],
                                xnc[bi][:, k * TCH:(k + 1) * TCH],
                                start=(k == 0), stop=(k == 7))
                        return last
                    P.emit("tensor", f_u, reads=[kxn, f"{pref}mwu{par}"], writes=[pu],
                           tag=f"{pref}umm{u}_{m}")

                    def f_sg(s, m=m, pg=pg, blk=blk, hb=hb):
                        return s.activation(hg2[hb][:, m * 512:(m + 1) * 512],
                                            sb[pg][:, 0:512], AF.Silu,
                                            bias=bgs[:, blk * 8 + m: blk * 8 + m + 1])
                    P.emit("scalar", f_sg, reads=[pg, f"{pref}mbg"],
                           writes=[f"{pref}mhg{hb}"], tag=f"{pref}sg{u}_{m}")

                    def f_h(v, m=m, pu=pu, blk=blk, hb=hb):
                        return v.scalar_tensor_tensor(
                            hh2[hb][:, m * 512:(m + 1) * 512], sb[pu][:, 0:512],
                            bus[:, blk * 8 + m: blk * 8 + m + 1],
                            hg2[hb][:, m * 512:(m + 1) * 512], ALU.add, ALU.mult)
                    P.emit("vector", f_h, reads=[pu, f"{pref}mbu", f"{pref}mhg{hb}"],
                           writes=[f"{pref}mh{hb}"], tag=f"{pref}h{u}_{m}")
                for m in range(8):
                    par = (u * 2 + m // 4) % 3
                    m4 = m % 4
                    pd = f"pb{4 + m % 2}"

                    def f_d(t, m4=m4, par=par, pd=pd, hb=hb):
                        last = None
                        for k in range(8):
                            last = t.matmul(
                                sb[pd][:, 0:512],
                                wdh[par][:, k * 512 + m4 * 128: k * 512 + (m4 + 1) * 128],
                                hh2[hb][:, k * 512:(k + 1) * 512],
                                start=(k == 0), stop=(k == 7))
                        return last
                    P.emit("tensor", f_d, reads=[f"{pref}mh{hb}", f"{pref}mwd{par}"],
                           writes=[pd], tag=f"{pref}dmm{u}_{m}")
                    if blk == 0:
                        def f_a(v, m=m, pd=pd):
                            return v.tensor_copy(acc[:, m * 512:(m + 1) * 512],
                                                 sb[pd][:, 0:512])
                    else:
                        def f_a(v, m=m, pd=pd):
                            return v.tensor_add(acc[:, m * 512:(m + 1) * 512],
                                                acc[:, m * 512:(m + 1) * 512],
                                                sb[pd][:, 0:512])
                    P.emit("vector", f_a, reads=[pd, f"{pref}macc"],
                           writes=[f"{pref}macc"], tag=f"{pref}macc{u}_{m}")
                    if m == 3 and u + 1 < len(units):
                        emit_wloads(u + 1, hfs=(1,))
                if blk == NBLK - 1:
                    sl = np.s_[tq * TCH:(tq + 1) * TCH]
                    P.emit("sync", lambda e, sl=sl: e.dma_start(
                        out=xck[:].rearrange("p (a n) -> p a n", a=8),
                        in_=xtmpd[:, sl].rearrange("(a p) n -> p a n", p=128)),
                        reads=[f"xtmp:{tq}"], writes=[f"{pref}mxck"], dma=True,
                        tag=f"{pref}eplx{tq}")

                    def f_fin(v):
                        last = None
                        for m in range(8):
                            last = v.scalar_tensor_tensor(
                                xck[:, m * TCH:(m + 1) * TCH],
                                acc[:, m * 512:(m + 1) * 512],
                                bds[:, m:m + 1], xck[:, m * TCH:(m + 1) * TCH],
                                ALU.add, ALU.add)
                        return last
                    P.emit("vector", f_fin, reads=[f"{pref}macc", f"{pref}mbd",
                                                   f"{pref}mxck"],
                           writes=[f"{pref}mxck"], tag=f"{pref}fin{tq}")
                    P.emit("sync", lambda e, sl=sl: e.dma_start(
                        out=xoutd[:, sl].rearrange("(a p) n -> p a n", p=128),
                        in_=xck[:].rearrange("p (a n) -> p a n", a=8)),
                        reads=[f"{pref}mxck"], writes=[f"{x_out}:{tq}"], dma=True,
                        tag=f"{pref}stxo{tq}")
                    if xn_out is not None:
                        ln.emit([f"{pref}mxck"], xck[:],
                                cx.dram[xn_out][:, sl].rearrange("(a p) n -> p a n", p=128),
                                [f"{xn_out}:{tq}"])
            P.barrier()
    _part_b()


SLOT = {}
_s0 = 0
for _p in range(2):
    for _h in range(4):
        pass
def _fix_slots():
    for p in range(2):
        for h in range(4):
            if h % 2 == 0:
                SLOT[(p, h)] = p * 2 + h // 2
            else:
                SLOT[(p, h)] = 4 + p * 2 + (h - 1) // 2
_fix_slots()


def _api(h):
    return h % 2


def _acb(h):
    return h // 4 + 4 * ((h % 4) // 2)


def emit_attn_a(cx, x_in, xn_in, x_out, xn_out):
    nc, P = cx.nc, cx.P
    sb = cx.sb
    xnd = cx.dram[xn_in]
    xind = cx.dram[x_in]
    xoutd = cx.dram[x_out]
    identb = cx.sb["identb"]
    SC = 256
    NSC = T // SC

    def _sec_a():
      with contextlib.ExitStack() as stk:
        sbuf = lambda nm, shp, dt: cx.sbuf(f"a{nm}", shp, dt, stack=stk)

        for nm, key in (("wq", "aWq_t"), ("wk", "aWk_t"), ("wv", "aWv_t"),
                        ("wg", "aWg_t"), ("wo", "aWo_t")):
            t = sbuf(nm, [128, 8 * 1024], BF16)
            P.emit("gpsimd", lambda e, t=t, key=key: e.dma_start(
                out=t[:].rearrange("p (a n) -> p a n", a=8),
                in_=cx.dram[key].rearrange("(a p) n -> p a n", p=128)),
                writes=[f"a_{nm}"], dma=True, tag=f"ald{nm}")
        wbt = sbuf("wbt", [128, 8 * 16], BF16)
        P.emit("gpsimd", lambda e: e.dma_start(
            out=wbt[:].rearrange("p (a n) -> p a n", a=8),
            in_=cx.dram["aWb_t"].rearrange("(a p) n -> p a n", p=128)),
            writes=["a_wbt"], dma=True, tag="aldwbt")
        wgkt = sbuf("wgkt", [128, 8 * 16], BF16)
        P.emit("gpsimd", lambda e: e.dma_start(
            out=wgkt[:].rearrange("p (a n) -> p a n", a=8),
            in_=cx.dram["aWgk_t"].rearrange("(a p) n -> p a n", p=128)),
            writes=["a_wgkt"], dma=True, tag="aldwgkt")
        consts = {}
        for nm, key, dt_ in (("cqw", "a_cqw", F32), ("cqb", "a_cqb", F32),
                             ("ckw", "a_ckw", F32), ("ckb", "a_ckb", F32),
                             ("cvw", "a_cvw", F32), ("cvb", "a_cvb", F32),
                             ("bbv", "a_bb", F32), ("bgkdt", "a_bgk_dt", F32),
                             ("alog", "a_Alog", F32), ("dphp", "a_Dphp", F32),
                             ("onrm", "a_onormc", F32),
                             ("blk8", "c_blk8", BF16),
                             ("eyehp", "c_eyehp", BF16),
                             ("mskL", "c_maskhpL", BF16),
                             ("mskU", "c_maskhpU", BF16),
                             ("li4", "c_lind4", BF16), ("li4T", "c_lind4T", BF16)):
            tt = sbuf(nm, list(cx.dram[key].shape), dt_)
            consts[nm] = tt
            eng = "gpsimd" if dt_ != F32 else "sync"
            P.emit(eng, lambda e, tt=tt, key=key: e.dma_start(out=tt[:], in_=cx.dram[key]),
                   writes=[f"a_{nm}"], dma=True, tag=f"aldc{nm}")
        wq, wk, wv, wg, wo = (cx.sb["awq"], cx.sb["awk"], cx.sb["awv"],
                              cx.sb["awg"], cx.sb["awo"])
        cc_ = consts

        neal = sbuf("neal", [8, 2], F32)
        idphp = sbuf("idphp", [128, 512], BF16)
        ones8 = sbuf("ones8", [8, 64], BF16)
        negones8 = sbuf("negones8", [8, 64], BF16)
        onef = sbuf("onef", [128, 1], F32)
        e12 = sbuf("e12", [128, 1], F32)
        e6 = sbuf("e6", [128, 1], F32)
        P.emit("vector", lambda v: v.memset(ones8[:], 1.0), writes=["a_ones8"], tag="ao8")
        P.emit("vector", lambda v: v.memset(negones8[:], -1.0), writes=["a_no8"], tag="an8")
        P.emit("vector", lambda v: v.memset(onef[:], 1.0), writes=["a_onef"], tag="aof")
        P.emit("vector", lambda v: v.memset(e12[:], 1e-12), writes=["a_e12"], tag="ae12")
        P.emit("vector", lambda v: v.memset(e6[:], 1e-6), writes=["a_e6"], tag="ae6")
        P.emit("scalar", lambda s: s.activation(neal[:], cc_["alog"][:], AF.Exp),
               reads=["a_alog"], writes=["a_neal"], tag="aneal1")
        P.emit("vector", lambda v: v.tensor_scalar(neal[:], neal[:], -1.0, None, ALU.mult),
               reads=["a_neal"], writes=["a_neal"], tag="aneal2")
        P.emit("vector", lambda v: v.tensor_tensor(
            idphp[:].rearrange("p (a n) -> p a n", a=8),
            cc_["eyehp"][:].rearrange("p (a n) -> p a n", a=8),
            cc_["dphp"][:].unsqueeze(2).to_broadcast([128, 8, 64]), ALU.mult),
            reads=["a_eyehp", "a_dphp"], writes=["a_idphp"], tag="aidp")

        xnc = [sbuf(f"xnc{i}", [128, 8 * SC], BF16) for i in range(2)]
        csq = sbuf("csq", [128, 8 * SC], BF16)
        csk = sbuf("csk", [128, 8 * SC], BF16)
        qs = sbuf("qs", [128, 8 * SC], BF16)
        ks = sbuf("ks", [128, 8 * SC], BF16)
        vs = sbuf("vs", [128, 8 * SC], BF16)
        gbuf = sbuf("gbuf", [128, 8 * SC], BF16)
        halo = {q: sbuf(f"halo{q}", [128, 24], F32) for q in "qkv"}
        for q in "qkv":
            P.emit("vector", lambda v, q=q: v.memset(halo[q][:], 0.0),
                   writes=[f"a_halo{q}"], tag=f"ahz{q}")
        ctmp = sbuf("ctmp", [128, SC], F32)
        rc = sbuf("rc", [128, SC], F32)
        rs = sbuf("rs", [128, SC], F32)
        rt1 = sbuf("rt1", [128, SC], F32)
        rt2 = sbuf("rt2", [128, SC], F32)
        rt3 = sbuf("rt3", [128, SC], F32)
        rt4 = sbuf("rt4", [128, SC], F32)
        sqt = sbuf("sqt", [128, SC], BF16)
        rstd4 = sbuf("rstd4", [4, SC], F32)
        rstd4b = sbuf("rstd4b", [4, SC], BF16)
        bsc = sbuf("bsc", [8, 2 * SC], F32)
        lal = sbuf("lal", [8, 2 * SC], F32)
        t8a = sbuf("t8a", [8, SC], F32)
        t8b = sbuf("t8b", [8, SC], F32)
        zs8 = sbuf("zs8", [8, 64], F32)
        P.emit("vector", lambda v: v.memset(zs8[:], 0.0), writes=["a_zs8"], tag="azs8")
        gcum = sbuf("gcum", [8, 2 * 64], F32)
        gprev = sbuf("gprev", [8, 2 * 64], F32)
        gcb = sbuf("gcb", [8, 2 * 64], BF16)
        gpb = sbuf("gpb", [8, 2 * 64], BF16)
        negg = sbuf("negg", [8, 2 * 64], BF16)
        bcb = sbuf("bcb", [8, 2 * 64], BF16)
        gpe = sbuf("gpe", [8, 2 * 64], BF16)
        ggr = sbuf("ggr", [8, 2 * 64], F32)
        ggrb = sbuf("ggrb", [8, 2 * 64], BF16)
        gcc = sbuf("gcc", [8, 2], F32)
        gccb = sbuf("gccb", [8, 2], BF16)
        gcrow = sbuf("gcrow", [1, 2 * 8], BF16)
        blkx = sbuf("blkx", [8, 512], BF16)
        gphp = sbuf("gphp", [128, 8], F32)
        bhp = sbuf("bhp", [128, 8], F32)
        ghp = sbuf("ghp", [128, 8], F32)
        bghp = sbuf("bghp", [128, 8], F32)
        gcghp = sbuf("gcghp", [128, 8], F32)
        gchp = sbuf("gchp", [128, 8], BF16)
        gcim = sbuf("gcim", [128, 512], BF16)
        mats = {}
        for nm in ("kda", "qda", "qsa", "Aa", "HTa", "HTd", "B0", "B1", "B2",
                   "B3", "B4", "X1", "X2", "X3", "X4", "X5", "PP", "KTl",
                   "GRr", "VTs", "VTu", "UVm", "GGm", "E2m", "Sb"):
            mats[nm] = sbuf(nm, [128, 512], BF16)
        tmpw = sbuf("tmpw", [128, 512], F32)
        rf32 = tmpw
        sf32 = sbuf("sf32", [128, 512], F32)
        P.emit("vector", lambda v: v.memset(sf32[:], 0.0), writes=["a_sf32"],
               tag="aSfz")
        P.emit("vector", lambda v: v.memset(mats["Sb"][:], 0.0), writes=["a_Sb"],
               tag="aSz")

        ofm = sbuf("ofm", [128, 8 * SC], F32)
        og8 = csq
        tmps = sbuf("tmps", [128, SC], F32)
        tmpsb = sbuf("tmpsb", [128, SC], BF16)
        rstd2 = sbuf("rstd2", [2, 8 * SC], F32)
        rstd2b = sbuf("rstd2b", [2, 8 * SC], BF16)
        li2 = sbuf("li2", [128, 2], BF16)
        li2T = sbuf("li2T", [2, 128], BF16)
        P.emit("gpsimd", lambda e: e.dma_start(out=li2[:], in_=cx.dram["c_lind2"]),
               writes=["a_li2"], dma=True, tag="aldli2")
        P.emit("gpsimd", lambda e: e.dma_start(out=li2T[:], in_=cx.dram["c_lind2T"]),
               writes=["a_li2T"], dma=True, tag="aldli2T")
        xck = sbuf("xck", [128, 8 * SC], F32)
        ln = LNHelper(cx, SC, "alna", stack=stk) if xn_out is not None else None

        def slot(mat, h):
            p_, c_ = _api(h), _acb(h)
            return mat[64 * p_:64 * p_ + 64, 64 * c_:64 * c_ + 64]

        def idslice(h):
            p_ = _api(h)
            return identb[64 * p_:64 * p_ + 64, 64 * p_:64 * p_ + 64]

        for sc in range(NSC):
            bi = sc % 2
            sl = np.s_[sc * SC:(sc + 1) * SC]
            kxn = f"a_xnc{bi}"
            P.emit("sync", lambda e, sl=sl, bi=bi: e.dma_start(
                out=xnc[bi][:].rearrange("p (a n) -> p a n", a=8),
                in_=xnd[:, sl].rearrange("(a p) n -> p a n", p=128)),
                reads=[f"{xn_in}:{sc // 2}"], writes=[kxn], dma=True, tag=f"alx{sc}")
            P.emit("sync", lambda e, sl=sl: e.dma_start(out=rc[:], in_=cx.dram["rope_cos"][:, sl]),
                   writes=["a_rc"], dma=True, tag=f"alrc{sc}")
            P.emit("sync", lambda e, sl=sl: e.dma_start(out=rs[:], in_=cx.dram["rope_sin"][:, sl]),
                   writes=["a_rs"], dma=True, tag=f"alrs{sc}")
            for qty, wmat, wkey, cwn, cbn, dstb, dstk in (
                    ("q", wq, "a_wq", "cqw", "cqb", csq, "a_csq"),
                    ("k", wk, "a_wk", "ckw", "ckb", csk, "a_csk"),
                    ("v", wv, "a_wv", "cvw", "cvb", vs, "a_vs")):
                for m in range(8):
                    bank = f"pb{m % 4}"
                    def f_p(t, m=m, bi=bi, wmat=wmat, bank=bank):
                        last = None
                        for k in range(8):
                            last = t.matmul(
                                sb[bank][:, 0:SC],
                                wmat[:, k * 1024 + m * 128: k * 1024 + (m + 1) * 128],
                                xnc[bi][:, k * SC:(k + 1) * SC],
                                start=(k == 0), stop=(k == 7))
                        return last
                    P.emit("tensor", f_p, reads=[kxn, wkey], writes=[bank],
                           tag=f"apm{sc}{qty}{m}")
                    cw = cc_[cwn]
                    cb = cc_[cbn]
                    hb = halo[qty]
                    def f_c0(v, m=m, bank=bank, cw=cw):
                        return v.tensor_scalar(ctmp[:], sb[bank][:, 0:SC],
                                               cw[:, m * 4 + 3: m * 4 + 4], None,
                                               ALU.mult)
                    P.emit("vector", f_c0, reads=[bank, f"a_{cwn}"], writes=["a_ctmp"],
                           tag=f"ac0{sc}{qty}{m}")
                    for s_ in range(3):
                        w_i = 2 - s_
                        sh = 3 - w_i
                        def f_cs(v, m=m, bank=bank, cw=cw, w_i=w_i, sh=sh):
                            return v.scalar_tensor_tensor(
                                ctmp[:, sh:SC], sb[bank][:, 0:SC - sh],
                                cw[:, m * 4 + w_i: m * 4 + w_i + 1],
                                ctmp[:, sh:SC], ALU.mult, ALU.add)
                        P.emit("vector", f_cs, reads=[bank, f"a_{cwn}", "a_ctmp"],
                               writes=["a_ctmp"], tag=f"acs{sc}{qty}{m}_{s_}")
                        def f_cb(v, m=m, cw=cw, w_i=w_i, sh=sh, hb=hb):
                            return v.scalar_tensor_tensor(
                                ctmp[:, 0:sh], hb[:, m * 3 + (3 - sh):m * 3 + 3],
                                cw[:, m * 4 + w_i: m * 4 + w_i + 1],
                                ctmp[:, 0:sh], ALU.mult, ALU.add)
                        P.emit("vector", f_cb, reads=[f"a_halo{qty}", f"a_{cwn}",
                                                      "a_ctmp"],
                               writes=["a_ctmp"], tag=f"acb{sc}{qty}{m}_{s_}")
                    P.emit("vector", lambda v, m=m, bank=bank, hb=hb: v.tensor_copy(
                        hb[:, m * 3:m * 3 + 3], sb[bank][:, SC - 3:SC]),
                        reads=[bank], writes=[f"a_halo{qty}"], tag=f"ahs{sc}{qty}{m}")
                    def f_si(s_, m=m, dstb=dstb, cb=cb):
                        return s_.activation(dstb[:, m * SC:(m + 1) * SC], ctmp[:],
                                             AF.Silu, bias=cb[:, m:m + 1])
                    P.emit("scalar", f_si, reads=["a_ctmp", f"a_{cbn}"], writes=[dstk],
                           tag=f"asi{sc}{qty}{m}")
            for m in range(8):
                bank = f"pb{m % 4}"
                def f_g(t, m=m, bi=bi, bank=bank):
                    last = None
                    for k in range(8):
                        last = t.matmul(
                            sb[bank][:, 0:SC],
                            wg[:, k * 1024 + m * 128: k * 1024 + (m + 1) * 128],
                            xnc[bi][:, k * SC:(k + 1) * SC],
                            start=(k == 0), stop=(k == 7))
                    return last
                P.emit("tensor", f_g, reads=[kxn, "a_wg"], writes=[bank],
                       tag=f"agm{sc}{m}")
                P.emit("vector", lambda v, m=m, bank=bank: v.tensor_copy(
                    gbuf[:, m * SC:(m + 1) * SC], sb[bank][:, 0:SC]),
                    reads=[bank], writes=["a_gbuf"], tag=f"age{sc}{m}")
            for pi_ in range(2):
                def f_bm(t, pi_=pi_, bi=bi):
                    last = None
                    for k in range(8):
                        last = t.matmul(
                            sb["pb4"][0:8, 0:SC],
                            wbt[:, k * 16 + 8 * pi_: k * 16 + 8 * pi_ + 8],
                            xnc[bi][:, k * SC:(k + 1) * SC],
                            start=(k == 0), stop=(k == 7))
                    return last
                P.emit("tensor", f_bm, reads=[kxn, "a_wbt"], writes=["pb4"],
                       tag=f"abm{sc}_{pi_}")
                def f_bs(s_, pi_=pi_):
                    return s_.activation(bsc[:, pi_ * SC:(pi_ + 1) * SC],
                                         sb["pb4"][0:8, 0:SC], AF.Sigmoid,
                                         bias=cc_["bbv"][0:8, pi_:pi_ + 1])
                P.emit("scalar", f_bs, reads=["pb4", "a_bbv"], writes=["a_bsc"],
                       tag=f"abs{sc}_{pi_}")
                def f_am(t, pi_=pi_, bi=bi):
                    last = None
                    for k in range(8):
                        last = t.matmul(
                            sb["pb5"][0:8, 0:SC],
                            wgkt[:, k * 16 + 8 * pi_: k * 16 + 8 * pi_ + 8],
                            xnc[bi][:, k * SC:(k + 1) * SC],
                            start=(k == 0), stop=(k == 7))
                    return last
                P.emit("tensor", f_am, reads=[kxn, "a_wgkt"], writes=["pb5"],
                       tag=f"aam{sc}_{pi_}")
                def f_a1(s_, pi_=pi_):
                    return s_.activation(t8a[:], sb["pb5"][0:8, 0:SC], AF.Abs,
                                         bias=cc_["bgkdt"][0:8, pi_:pi_ + 1])
                P.emit("scalar", f_a1, reads=["pb5", "a_bgkdt"], writes=["a_t8a"],
                       tag=f"aa1{sc}_{pi_}")
                P.emit("scalar", lambda s_: s_.activation(t8b[:], t8a[:], AF.Exp,
                                                          scale=-1.0),
                       reads=["a_t8a"], writes=["a_t8b"], tag=f"aa2{sc}_{pi_}")
                P.emit("scalar", lambda s_: s_.activation(t8b[:], t8b[:], AF.Ln,
                                                          bias=onef[0:8, 0:1]),
                       reads=["a_t8b", "a_onef"], writes=["a_t8b"], tag=f"aa3{sc}_{pi_}")
                def f_a4(v, pi_=pi_):
                    return v.tensor_scalar(t8a[:], sb["pb5"][0:8, 0:SC],
                                           cc_["bgkdt"][0:8, pi_:pi_ + 1],
                                           0.0, ALU.add, ALU.max)
                P.emit("vector", f_a4, reads=["pb5", "a_bgkdt"], writes=["a_t8a"],
                       tag=f"aa4{sc}_{pi_}")
                P.emit("vector", lambda v: v.tensor_add(t8a[:], t8a[:], t8b[:]),
                       reads=["a_t8a", "a_t8b"], writes=["a_t8a"], tag=f"aa5{sc}_{pi_}")
                def f_a6(v, pi_=pi_):
                    return v.tensor_scalar(lal[:, pi_ * SC:(pi_ + 1) * SC], t8a[:],
                                           neal[0:8, pi_:pi_ + 1], None,
                                           ALU.mult)
                P.emit("vector", f_a6, reads=["a_t8a", "a_neal"], writes=["a_lal"],
                       tag=f"aa6{sc}_{pi_}")
            for qty, src, dstb, dkey, skey_ in (("q", csq, qs, "a_qs", "a_csq"),
                                                ("k", csk, ks, "a_ks", "a_csk")):
                for i in range(4):
                    A1 = src[:, i * SC:(i + 1) * SC]
                    A2 = src[:, (4 + i) * SC:(5 + i) * SC]
                    P.emit("vector", lambda v, A1=A1: v.tensor_mul(rt1[:], A1, rc[:]),
                           reads=[skey_, "a_rc"], writes=["a_rt1"], tag=f"ar1{sc}{qty}{i}")
                    P.emit("vector", lambda v, A2=A2: v.tensor_mul(rt2[:], A2, rs[:]),
                           reads=[skey_, "a_rs"], writes=["a_rt2"], tag=f"ar2{sc}{qty}{i}")
                    P.emit("vector", lambda v: v.tensor_sub(rt3[:], rt1[:], rt2[:]),
                           reads=["a_rt1", "a_rt2"], writes=["a_rt3"], tag=f"ar3{sc}{qty}{i}")
                    P.emit("vector", lambda v, A1=A1: v.tensor_mul(rt1[:], A1, rs[:]),
                           reads=[skey_, "a_rs"], writes=["a_rt1"], tag=f"ar4{sc}{qty}{i}")
                    P.emit("vector", lambda v, A2=A2: v.tensor_mul(rt2[:], A2, rc[:]),
                           reads=[skey_, "a_rc"], writes=["a_rt2"], tag=f"ar5{sc}{qty}{i}")
                    P.emit("vector", lambda v: v.tensor_add(rt4[:], rt1[:], rt2[:]),
                           reads=["a_rt1", "a_rt2"], writes=["a_rt4"], tag=f"ar6{sc}{qty}{i}")
                    P.emit("scalar", lambda s_: s_.activation(sqt[:], rt3[:], AF.Square),
                           reads=["a_rt3"], writes=["a_sqt"], tag=f"al1{sc}{qty}{i}")
                    P.emit("tensor", lambda t: t.matmul(sb["pb6"][0:4, 0:SC],
                                                        cc_["li4"][:], sqt[:],
                                                        start=True, stop=False),
                           reads=["a_sqt", "a_li4"], writes=["pb6"], tag=f"al2{sc}{qty}{i}")
                    P.emit("scalar", lambda s_: s_.activation(sqt[:], rt4[:], AF.Square),
                           reads=["a_rt4"], writes=["a_sqt"], tag=f"al3{sc}{qty}{i}")
                    P.emit("tensor", lambda t: t.matmul(sb["pb6"][0:4, 0:SC],
                                                        cc_["li4"][:], sqt[:],
                                                        start=False, stop=True),
                           reads=["a_sqt", "a_li4"], writes=["pb6"], tag=f"al4{sc}{qty}{i}")
                    P.emit("scalar", lambda s_: s_.activation(rstd4[:], sb["pb6"][0:4, 0:SC],
                                                              AF.Ln, bias=e12[0:4, 0:1]),
                           reads=["pb6", "a_e12"], writes=["a_rstd4"], tag=f"al5{sc}{qty}{i}")
                    P.emit("scalar", lambda s_: s_.activation(rstd4[:], rstd4[:], AF.Exp,
                                                              scale=-0.5),
                           reads=["a_rstd4"], writes=["a_rstd4"], tag=f"al6{sc}{qty}{i}")
                    P.emit("vector", lambda v: v.tensor_copy(rstd4b[:], rstd4[:]),
                           reads=["a_rstd4"], writes=["a_rstd4b"], tag=f"al7{sc}{qty}{i}")
                    P.emit("tensor", lambda t: t.matmul(sb["pb7"][:, 0:SC],
                                                        cc_["li4T"][:], rstd4b[:],
                                                        start=True, stop=True),
                           reads=["a_rstd4b", "a_li4T"], writes=["pb7"], tag=f"al8{sc}{qty}{i}")
                    def f_q1(v, i=i, dstb=dstb):
                        return v.tensor_mul(dstb[:, i * SC:(i + 1) * SC], rt3[:],
                                            sb["pb7"][:, 0:SC])
                    P.emit("vector", f_q1, reads=["a_rt3", "pb7"], writes=[dkey],
                           tag=f"al9{sc}{qty}{i}")
                    def f_q2(v, i=i, dstb=dstb):
                        return v.tensor_mul(dstb[:, (4 + i) * SC:(5 + i) * SC], rt4[:],
                                            sb["pb7"][:, 0:SC])
                    P.emit("vector", f_q2, reads=["a_rt4", "pb7"], writes=[dkey],
                           tag=f"alA{sc}{qty}{i}")

            if sc == 0 and cx.dram.get("DBGTAP") is not None:
                dd = cx.dram["dbgt"]
                for nm, buf, row in (("qs", qs, 0), ("ks", ks, 128), ("vs", vs, 256),
                                     ("gb", gbuf, 384)):
                    P.emit("gpsimd", lambda e, buf=buf, row=row: e.dma_start(
                        out=dd[row:row + 128, 0:2048], in_=buf[:]),
                        reads=[f"a_{nm if nm != 'gb' else 'gbuf'}"], writes=["DBGT"],
                        dma=True, tag=f"dtap{nm}")
                P.emit("sync", lambda e: e.dma_start(
                    out=dd[512:520, 0:512], in_=bsc[:]),
                    reads=["a_bsc"], writes=["DBGT"], dma=True, tag="dtapb")
                P.emit("sync", lambda e: e.dma_start(
                    out=dd[520:528, 0:512], in_=lal[:]),
                    reads=["a_lal"], writes=["DBGT"], dma=True, tag="dtapl")
            for ci in range(SC // 64):
                o = ci * 64
                emit_scan_chunk(cx, sc, ci, o, mats, locals())
            for m in range(8):
                P.emit("scalar", lambda s_, m=m: s_.activation(
                    sqt[:], ofm[:, m * SC:(m + 1) * SC], AF.Square),
                    reads=["a_ofm"], writes=["a_sqt"], tag=f"aq1{sc}{m}")
                P.emit("tensor", lambda t, m=m: t.matmul(
                    sb["pb6"][0:2, m * 64:m * 64 + SC // 4] if False else
                    sb["pb6"][0:2, 0:SC],
                    li2[:], sqt[:], start=True, stop=True),
                    reads=["a_sqt", "a_li2"], writes=["pb6"], tag=f"aq2{sc}{m}")
                P.emit("scalar", lambda s_, m=m: s_.activation(
                    rstd2[:, m * SC:(m + 1) * SC], sb["pb6"][0:2, 0:SC], AF.Ln,
                    bias=e6[0:2, 0:1], scale=1.0 / 64),
                    reads=["pb6", "a_e6"], writes=["a_rstd2"], tag=f"aq3{sc}{m}")
            P.emit("scalar", lambda s_: s_.activation(rstd2[:], rstd2[:], AF.Exp,
                                                      scale=-0.5),
                   reads=["a_rstd2"], writes=["a_rstd2"], tag=f"aq4{sc}")
            P.emit("vector", lambda v: v.tensor_copy(rstd2b[:], rstd2[:]),
                   reads=["a_rstd2"], writes=["a_rstd2b"], tag=f"aq5{sc}")
            for m in range(8):
                P.emit("tensor", lambda t, m=m: t.matmul(
                    sb["pb7"][:, 0:SC], li2T[:], rstd2b[:, m * SC:(m + 1) * SC],
                    start=True, stop=True),
                    reads=["a_rstd2b", "a_li2T"], writes=["pb7"], tag=f"aq6{sc}{m}")
                def f_on(v, m=m):
                    return v.scalar_tensor_tensor(
                        tmps[:], ofm[:, m * SC:(m + 1) * SC], cc_["onrm"][:, 0:1],
                        sb["pb7"][:, 0:SC], ALU.mult, ALU.mult)
                P.emit("vector", f_on, reads=["a_ofm", "a_onrm", "pb7"],
                       writes=["a_tmps"], tag=f"aq7{sc}{m}")
                P.emit("scalar", lambda s_: s_.activation(tmpsb[:], tmps[:], AF.Silu),
                       reads=["a_tmps"], writes=["a_tmpsb"], tag=f"aq8{sc}{m}")
                P.emit("vector", lambda v, m=m: v.tensor_mul(
                    og8[:, m * SC:(m + 1) * SC], gbuf[:, m * SC:(m + 1) * SC],
                    tmpsb[:]),
                    reads=["a_tmpsb", "a_gbuf"], writes=["a_csq"], tag=f"aq9{sc}{m}")
            if sc == 0 and cx.dram.get("DBGTAP") is not None:
                P.emit("sync", lambda e: e.dma_start(
                    out=cx.dram["dbgt"][640:768, 0:2048], in_=ofm[:]),
                    reads=["a_ofm"], writes=["DBGT"], dma=True, tag="dtapo")
                P.emit("gpsimd", lambda e: e.dma_start(
                    out=cx.dram["dbgt"][768:896, 0:2048], in_=og8[:]),
                    reads=["a_csq"], writes=["DBGT"], dma=True, tag="dtapog")
            P.emit("sync", lambda e, sl=sl: e.dma_start(
                out=xck[:].rearrange("p (a n) -> p a n", a=8),
                in_=xind[:, sl].rearrange("(a p) n -> p a n", p=128)),
                reads=[f"{x_in}:{sc // 2}"], writes=["a_xck"], dma=True,
                tag=f"axl{sc}")
            for m in range(8):
                def f_wo(t, m=m):
                    last = None
                    for k in range(8):
                        last = t.matmul(
                            sb["pb0"][:, 0:SC],
                            wo[:, k * 1024 + m * 128: k * 1024 + (m + 1) * 128],
                            og8[:, k * SC:(k + 1) * SC],
                            start=(k == 0), stop=(k == 7))
                    return last
                P.emit("tensor", f_wo, reads=["a_csq", "a_wo"], writes=["pb0"],
                       tag=f"awo{sc}{m}")
                P.emit("vector", lambda v, m=m: v.tensor_add(
                    xck[:, m * SC:(m + 1) * SC], xck[:, m * SC:(m + 1) * SC],
                    sb["pb0"][:, 0:SC]),
                    reads=["pb0", "a_xck"], writes=["a_xck"], tag=f"awoe{sc}{m}")
            P.emit("sync", lambda e, sl=sl: e.dma_start(
                out=xoutd[:, sl].rearrange("(a p) n -> p a n", p=128),
                in_=xck[:].rearrange("p (a n) -> p a n", a=8)),
                reads=["a_xck"], writes=[f"{x_out}:{sc // 2}"], dma=True,
                tag=f"axs{sc}")
            if xn_out is not None:
                ln.emit(["a_xck"], xck[:],
                        cx.dram[xn_out][:, sl].rearrange("(a p) n -> p a n", p=128),
                        [f"{xn_out}:{sc // 2}"])
        P.barrier()
    _sec_a()


def emit_scan_chunk(cx, sc, ci, o, mats, env):
    P = cx.P
    sb = cx.sb
    SC = 256
    cc_ = env["cc_"]
    identb = cx.sb["identb"]
    (gcum, gprev, gcb, gpb, negg, bcb, gpe, ggr, ggrb, gcc, gccb, gcrow, blkx,
     gphp, bhp, ghp, bghp, gcghp, gchp, gcim, tmpw, rf32, sf32, ofm, lal, bsc,
     zs8, qs, ks, vs) = (env[k] for k in (
        "gcum", "gprev", "gcb", "gpb", "negg", "bcb", "gpe", "ggr", "ggrb",
        "gcc", "gccb", "gcrow", "blkx", "gphp", "bhp", "ghp", "bghp", "gcghp",
        "gchp", "gcim", "tmpw", "rf32", "sf32", "ofm", "lal", "bsc", "zs8",
        "qs", "ks", "vs"))
    tg = f"S{sc}_{ci}"

    def slot(mat, h):
        p_, c_ = _api(h), _acb(h)
        return mat[64 * p_:64 * p_ + 64, 64 * c_:64 * c_ + 64]

    def idslice(h):
        p_ = _api(h)
        return identb[64 * p_:64 * p_ + 64, 64 * p_:64 * p_ + 64]

    for pi_ in range(2):
        po = pi_ * 64
        def f_cum(v, pi_=pi_, po=po):
            return v.tensor_tensor_scan(
                gcum[:, po:po + 64], lal[:, pi_ * SC + o: pi_ * SC + o + 64],
                zs8[:], 0.0, ALU.add, ALU.add)
        P.emit("vector", f_cum, reads=["a_lal", "a_zs8"], writes=["a_gcum"],
               tag=f"{tg}cum{pi_}")
        P.emit("vector", lambda v, po=po: v.tensor_copy(
            gprev[:, po + 1:po + 64], gcum[:, po:po + 63]),
            reads=["a_gcum"], writes=["a_gprev"], tag=f"{tg}gp{pi_}")
        P.emit("vector", lambda v, po=po: v.memset(gprev[:, po:po + 1], 0.0),
               writes=["a_gprev"], tag=f"{tg}gpz{pi_}")
        P.emit("vector", lambda v, po=po: v.tensor_copy(gcb[:, po:po + 64],
                                                        gcum[:, po:po + 64]),
               reads=["a_gcum"], writes=["a_gcb"], tag=f"{tg}c1{pi_}")
        P.emit("vector", lambda v, po=po: v.tensor_copy(gpb[:, po:po + 64],
                                                        gprev[:, po:po + 64]),
               reads=["a_gprev"], writes=["a_gpb"], tag=f"{tg}c2{pi_}")
        P.emit("vector", lambda v, po=po: v.tensor_scalar(
            negg[:, po:po + 64], gcum[:, po:po + 64], -1.0, None, ALU.mult),
            reads=["a_gcum"], writes=["a_negg"], tag=f"{tg}c3{pi_}")
        def f_bc(v, pi_=pi_, po=po):
            return v.tensor_copy(bcb[:, po:po + 64],
                                 bsc[:, pi_ * SC + o: pi_ * SC + o + 64])
        P.emit("vector", f_bc, reads=["a_bsc"], writes=["a_bcb"], tag=f"{tg}c4{pi_}")
        P.emit("scalar", lambda s_, po=po: s_.activation(gpe[:, po:po + 64],
                                                         gprev[:, po:po + 64], AF.Exp),
               reads=["a_gprev"], writes=["a_gpe"], tag=f"{tg}c5{pi_}")
        def f_ggr(s_, po=po):
            return s_.activation(ggr[:, po:po + 64], gcum[:, po:po + 64], AF.Exp,
                                 scale=-1.0, bias=gcum[:, po + 63:po + 64])
        P.emit("scalar", f_ggr, reads=["a_gcum"], writes=["a_ggr"], tag=f"{tg}c6{pi_}")
        P.emit("vector", lambda v, po=po: v.tensor_copy(ggrb[:, po:po + 64],
                                                        ggr[:, po:po + 64]),
               reads=["a_ggr"], writes=["a_ggrb"], tag=f"{tg}c7{pi_}")
        def f_gcc(s_, pi_=pi_, po=po):
            return s_.activation(gcc[:, pi_:pi_ + 1], gcum[:, po + 63:po + 64], AF.Exp)
        P.emit("scalar", f_gcc, reads=["a_gcum"], writes=["a_gcc"], tag=f"{tg}c8{pi_}")
        P.emit("vector", lambda v, pi_=pi_: v.tensor_copy(gccb[:, pi_:pi_ + 1],
                                                          gcc[:, pi_:pi_ + 1]),
               reads=["a_gcc"], writes=["a_gccb"], tag=f"{tg}c9{pi_}")
        for j, (srcb, srck, dsthp, dk, expit) in enumerate((
                (gcb, "a_gcb", ghp, "a_ghp", True),
                (gpb, "a_gpb", gphp, "a_gphp", True),
                (bcb, "a_bcb", bhp, "a_bhp", False),
                (ggrb, "a_ggrb", gcghp, "a_gcghp", False))):
            def f_tr(t, srcb=srcb, po=po, j=j):
                return t.matmul(sb["pb6"][0:64, j * 8:(j + 1) * 8],
                                srcb[:, po:po + 64], identb[0:8, 0:8],
                                start=True, stop=True)
            P.emit("tensor", f_tr, reads=[srck, "identb"], writes=["pb6"],
                   tag=f"{tg}tr{pi_}{j}")
            if expit:
                def f_ev(s_, dsthp=dsthp, pi_=pi_, j=j):
                    return s_.activation(dsthp[64 * pi_:64 * pi_ + 64, 0:8],
                                         sb["pb6"][0:64, j * 8:(j + 1) * 8], AF.Exp)
                P.emit("scalar", f_ev, reads=["pb6"], writes=[dk], tag=f"{tg}te{pi_}{j}")
            else:
                def f_ev(v, dsthp=dsthp, pi_=pi_, j=j):
                    return v.tensor_copy(dsthp[64 * pi_:64 * pi_ + 64, 0:8],
                                         sb["pb6"][0:64, j * 8:(j + 1) * 8])
                P.emit("vector", f_ev, reads=["pb6"], writes=[dk], tag=f"{tg}te{pi_}{j}")
    P.emit("vector", lambda v: v.tensor_mul(bghp[:], bhp[:], ghp[:]),
           reads=["a_bhp", "a_ghp"], writes=["a_bghp"], tag=f"{tg}bg")
    def f_gcr(t):
        last = None
        for pi_ in range(2):
            last = t.matmul(sb["pb6"][0:1, 64 + pi_ * 8: 64 + pi_ * 8 + 8],
                            gccb[:, pi_:pi_ + 1], identb[0:8, 0:8],
                            start=True, stop=True)
        return last
    P.emit("tensor", f_gcr, reads=["a_gccb", "identb"], writes=["pb6"], tag=f"{tg}gcr")
    P.emit("vector", lambda v: v.tensor_copy(gcrow[0:1, :], sb["pb6"][0:1, 64:80]),
           reads=["pb6"], writes=["a_gcrow"], tag=f"{tg}gcre")
    def f_gco2(t):
        t.matmul(sb["pb7"][:, 0:8], cx.sb["ones_row_b"][0:1, 0:128],
                 gcrow[0:1, 0:8], start=True, stop=True)
        return t.matmul(sb["pb7"][:, 8:16], cx.sb["ones_row_b"][0:1, 0:128],
                        gcrow[0:1, 8:16], start=True, stop=True)
    P.emit("tensor", f_gco2, reads=["a_gcrow", "ones_row_b"], writes=["pb7"],
           tag=f"{tg}gco")
    P.emit("vector", lambda v: v.tensor_copy(gchp[0:64, 0:8], sb["pb7"][0:64, 0:8]),
           reads=["pb7"], writes=["a_gchp"], tag=f"{tg}gce1")
    P.emit("vector", lambda v: v.tensor_copy(gchp[64:128, 0:8], sb["pb7"][64:128, 8:16]),
           reads=["pb7"], writes=["a_gchp"], tag=f"{tg}gce2")
    def f_gcim(v):
        return v.tensor_tensor(
            gcim[:].rearrange("p (a n) -> p a n", a=8),
            cc_["eyehp"][:].rearrange("p (a n) -> p a n", a=8),
            gchp[:].unsqueeze(2).to_broadcast([128, 8, 64]), ALU.mult)
    P.emit("vector", f_gcim, reads=["a_eyehp", "a_gchp"], writes=["a_gcim"],
           tag=f"{tg}gcim")

    for qty, srcb, srck, dstm, dk in (("k", ks, "a_ks", "kda", "a_kda"),
                                      ("q", qs, "a_qs", "qda", "a_qda")):
        for b in range(4):
            for half in range(2):
                def f_cp(v, b=b, half=half, srcb=srcb, dstm=dstm):
                    sap = srcb[32 * b:32 * b + 32, :]                         .rearrange("p (a n) -> p a n", a=8)[:, half * 4:half * 4 + 4,
                                                           o:o + 64]
                    dap = mats[dstm][64 * (b % 2) + 32 * half:
                                     64 * (b % 2) + 32 * half + 32,
                                     256 * (b // 2):256 * (b // 2) + 256]                         .rearrange("p (a n) -> p a n", a=4)
                    return v.tensor_copy(dap, sap)
                P.emit("vector", f_cp, reads=[srck], writes=[dk],
                       tag=f"{tg}as{qty}{b}{half}")
    for pi_ in range(2):
        def f_blk(v, pi_=pi_):
            po = pi_ * 64
            return v.tensor_tensor(
                blkx[:].rearrange("p (a n) -> p a n", a=8),
                gpe[:, po:po + 64].unsqueeze(1).to_broadcast([8, 8, 64]),
                cc_["blk8"][:].rearrange("p (a n) -> p a n", a=8), ALU.mult)
        P.emit("vector", f_blk, reads=["a_gpe", "a_blk8"], writes=["a_blkx"],
               tag=f"{tg}qb{pi_}")
        def f_gpm(t, pi_=pi_):
            return t.matmul(sb["pb7"][64 * pi_:64 * pi_ + 64, 0:512],
                            env["ones8"][:], blkx[:],
                            start=True, stop=True, tile_position=(0, 64 * pi_))
        P.emit("tensor", f_gpm, reads=["a_blkx", "a_ones8"], writes=["pb7"],
               tag=f"{tg}qm{pi_}")
    P.emit("vector", lambda v: v.tensor_mul(mats["qsa"][:], mats["qda"][:],
                                            sb["pb7"][:, 0:512]),
           reads=["a_qda", "pb7"], writes=["a_qsa"], tag=f"{tg}qsc")

    for h in range(16):
        pi_ = _api(h)
        bank = "pb0" if pi_ == 0 else "pb1"
        def f_kk(t, h=h, bank=bank):
            return t.matmul(sb[bank][64 * _api(h):64 * _api(h) + 64,
                                     64 * _acb(h):64 * _acb(h) + 64],
                            slot(mats["kda"], h), slot(mats["kda"], h),
                            start=True, stop=True,
                            tile_position=(64 * _api(h), 64 * _api(h)))
        P.emit("tensor", f_kk, reads=["a_kda"], writes=[bank], tag=f"{tg}kk{h}")
        bank2 = "pb2" if pi_ == 0 else "pb3"
        def f_kq(t, h=h, bank2=bank2):
            return t.matmul(sb[bank2][64 * _api(h):64 * _api(h) + 64,
                                      64 * _acb(h):64 * _acb(h) + 64],
                            slot(mats["kda"], h), slot(mats["qda"], h),
                            start=True, stop=True,
                            tile_position=(64 * _api(h), 64 * _api(h)))
        P.emit("tensor", f_kq, reads=["a_kda", "a_qda"], writes=[bank2],
               tag=f"{tg}kq{h}")

    for pi_ in range(2):
        po = pi_ * 64
        def f_gblk(v, pi_=pi_, po=po):
            return v.tensor_tensor(
                blkx[:].rearrange("p (a n) -> p a n", a=8),
                gcb[:, po:po + 64].unsqueeze(1).to_broadcast([8, 8, 64]),
                cc_["blk8"][:].rearrange("p (a n) -> p a n", a=8), ALU.mult)
        P.emit("vector", f_gblk, reads=["a_gcb", "a_blk8"], writes=["a_blkx"],
               tag=f"{tg}Ab{pi_}")
        def f_am1(t, pi_=pi_, po=po):
            t.matmul(sb["pb4"][64 * pi_:64 * pi_ + 64, 0:512],
                     gcb[:, po:po + 64], cc_["blk8"][:],
                     start=(pi_ == 0), stop=False, tile_position=(0, 64 * pi_))
            return t.matmul(sb["pb4"][64 * pi_:64 * pi_ + 64, 0:512],
                            env["negones8"][:], blkx[:],
                            start=False, stop=(pi_ == 1), tile_position=(0, 64 * pi_))
        P.emit("tensor", f_am1, reads=["a_gcb", "a_blk8", "a_blkx", "a_no8"],
               writes=["pb4"], tag=f"{tg}Am{pi_}")
        def f_hblk(v, pi_=pi_, po=po):
            return v.tensor_tensor(
                blkx[:].rearrange("p (a n) -> p a n", a=8),
                gpb[:, po:po + 64].unsqueeze(1).to_broadcast([8, 8, 64]),
                cc_["blk8"][:].rearrange("p (a n) -> p a n", a=8), ALU.mult)
        P.emit("vector", f_hblk, reads=["a_gpb", "a_blk8"], writes=["a_blkx"],
               tag=f"{tg}Hb{pi_}")
        def f_hm1(t, pi_=pi_, po=po):
            t.matmul(sb["pb5"][64 * pi_:64 * pi_ + 64, 0:512],
                     env["ones8"][:], blkx[:],
                     start=(pi_ == 0), stop=False, tile_position=(0, 64 * pi_))
            return t.matmul(sb["pb5"][64 * pi_:64 * pi_ + 64, 0:512],
                            negg[:, po:po + 64], cc_["blk8"][:],
                            start=False, stop=(pi_ == 1), tile_position=(0, 64 * pi_))
        P.emit("tensor", f_hm1, reads=["a_negg", "a_blk8", "a_blkx", "a_ones8"],
               writes=["pb5"], tag=f"{tg}Hm{pi_}")
    P.emit("vector", lambda v: v.tensor_add(tmpw[:], sb["pb4"][:, 0:512],
                                            cc_["mskL"][:]),
           reads=["pb4", "a_mskL"], writes=["a_tmpw"], tag=f"{tg}Aa1")
    P.emit("scalar", lambda s_: s_.activation(tmpw[:], tmpw[:], AF.Exp),
           reads=["a_tmpw"], writes=["a_tmpw"], tag=f"{tg}Aa2")
    for pi_ in range(2):
        bank = "pb0" if pi_ == 0 else "pb1"
        def f_akk(v, pi_=pi_, bank=bank):
            return v.tensor_mul(tmpw[64 * pi_:64 * pi_ + 64, :],
                                tmpw[64 * pi_:64 * pi_ + 64, :],
                                sb[bank][64 * pi_:64 * pi_ + 64, 0:512])
        P.emit("vector", f_akk, reads=["a_tmpw", bank], writes=["a_tmpw"],
               tag=f"{tg}Aa3{pi_}")
    def f_ab(v):
        return v.tensor_tensor(
            mats["Aa"][:].rearrange("p (a n) -> p a n", a=8),
            tmpw[:].rearrange("p (a n) -> p a n", a=8),
            bhp[:].unsqueeze(2).to_broadcast([128, 8, 64]), ALU.mult)
    P.emit("vector", f_ab, reads=["a_tmpw", "a_bhp"], writes=["a_Aa"], tag=f"{tg}Aa4")
    P.emit("vector", lambda v: v.tensor_add(tmpw[:], sb["pb5"][:, 0:512],
                                            cc_["mskU"][:]),
           reads=["pb5", "a_mskU"], writes=["a_tmpw"], tag=f"{tg}Ha1")
    P.emit("scalar", lambda s_: s_.activation(tmpw[:], tmpw[:], AF.Exp),
           reads=["a_tmpw"], writes=["a_tmpw"], tag=f"{tg}Ha2")
    for pi_ in range(2):
        bank2 = "pb2" if pi_ == 0 else "pb3"
        def f_hkq(v, pi_=pi_, bank2=bank2):
            return v.tensor_mul(mats["HTa"][64 * pi_:64 * pi_ + 64, :],
                                tmpw[64 * pi_:64 * pi_ + 64, :],
                                sb[bank2][64 * pi_:64 * pi_ + 64, 0:512])
        P.emit("vector", f_hkq, reads=["a_tmpw", bank2], writes=["a_HTa"],
               tag=f"{tg}Ha3{pi_}")
        def f_hd(v, pi_=pi_, bank2=bank2):
            return v.tensor_mul(mats["HTd"][64 * pi_:64 * pi_ + 64, :],
                                env["idphp"][64 * pi_:64 * pi_ + 64, :],
                                sb[bank2][64 * pi_:64 * pi_ + 64, 0:512])
        P.emit("vector", f_hd, reads=["a_idphp", bank2], writes=["a_HTd"],
               tag=f"{tg}Hd{pi_}")

    for h in range(16):
        bank = "pb4" if _api(h) == 0 else "pb5"
        def f_b0(t, h=h, bank=bank):
            return t.matmul(sb[bank][64 * _api(h):64 * _api(h) + 64,
                                     64 * _acb(h):64 * _acb(h) + 64],
                            slot(mats["Aa"], h), idslice(h),
                            start=True, stop=True,
                            tile_position=(64 * _api(h), 64 * _api(h)))
        P.emit("tensor", f_b0, reads=["a_Aa", "identb"], writes=[bank],
               tag=f"{tg}b0{h}")
    for pi_ in range(2):
        bank = "pb4" if pi_ == 0 else "pb5"
        if pi_ == 0:
            P.emit("scalar", lambda sc_, bank=bank: sc_.activation(
                mats["B0"][0:64, :], sb[bank][0:64, 0:512], AF.Copy),
                reads=[bank], writes=["a_B0"], tag=f"{tg}b0e{pi_}")
        else:
            P.emit("vector", lambda v, bank=bank: v.tensor_copy(
                mats["B0"][64:128, :], sb[bank][64:128, 0:512]),
                reads=[bank], writes=["a_B0"], tag=f"{tg}b0e{pi_}")
        def f_pp0(v, pi_=pi_, bank=bank):
            return v.scalar_tensor_tensor(
                mats["PP"][64 * pi_:64 * pi_ + 64, :],
                sb[bank][64 * pi_:64 * pi_ + 64, 0:512], -1.0,
                cc_["eyehp"][64 * pi_:64 * pi_ + 64, :], ALU.mult, ALU.add)
        P.emit("vector", f_pp0, reads=[bank, "a_eyehp"], writes=["a_PP"],
               tag=f"{tg}pp0{pi_}")
    xprev, bprev = "Aa", "B0"
    for k in range(1, 6):
        xk = f"X{k}"
        bk = f"B{k}"
        for h in range(16):
            bank = "pb4" if _api(h) == 0 else "pb5"
            def f_x(t, h=h, bank=bank, xprev=xprev, bprev=bprev):
                return t.matmul(sb[bank][64 * _api(h):64 * _api(h) + 64,
                                         64 * _acb(h):64 * _acb(h) + 64],
                                slot(mats[bprev], h), slot(mats[xprev], h),
                                start=True, stop=True,
                                tile_position=(64 * _api(h), 64 * _api(h)))
            P.emit("tensor", f_x, reads=[f"a_{xprev}", f"a_{bprev}"], writes=[bank],
                   tag=f"{tg}x{k}_{h}")
        for pi_ in range(2):
            bank = "pb4" if pi_ == 0 else "pb5"
            if pi_ == 0:
                P.emit("scalar", lambda sc_, bank=bank, xk=xk: sc_.activation(
                    mats[xk][0:64, :], sb[bank][0:64, 0:512], AF.Copy),
                    reads=[bank], writes=[f"a_{xk}"], tag=f"{tg}xe{k}_{pi_}")
            else:
                P.emit("vector", lambda v, bank=bank, xk=xk: v.tensor_copy(
                    mats[xk][64:128, :], sb[bank][64:128, 0:512]),
                    reads=[bank], writes=[f"a_{xk}"], tag=f"{tg}xe{k}_{pi_}")
        if k <= 4:
            for h in range(16):
                bank = "pb6" if _api(h) == 0 else "pb7"
                def f_b(t, h=h, bank=bank, xprev=xprev, bprev=bprev):
                    return t.matmul(sb[bank][64 * _api(h):64 * _api(h) + 64,
                                             64 * _acb(h):64 * _acb(h) + 64],
                                    slot(mats[xprev], h), slot(mats[bprev], h),
                                    start=True, stop=True,
                                    tile_position=(64 * _api(h), 64 * _api(h)))
                P.emit("tensor", f_b, reads=[f"a_{xprev}", f"a_{bprev}"],
                       writes=[bank], tag=f"{tg}bb{k}_{h}")
            for pi_ in range(2):
                bank = "pb6" if pi_ == 0 else "pb7"
                if pi_ == 0:
                    P.emit("scalar", lambda sc_, bank=bank, bk=bk: sc_.activation(
                        mats[bk][0:64, :], sb[bank][0:64, 0:512], AF.Copy),
                        reads=[bank], writes=[f"a_{bk}"], tag=f"{tg}be{k}_{pi_}")
                else:
                    P.emit("vector", lambda v, bank=bank, bk=bk: v.tensor_copy(
                        mats[bk][64:128, :], sb[bank][64:128, 0:512]),
                        reads=[bank], writes=[f"a_{bk}"], tag=f"{tg}be{k}_{pi_}")
        for h in range(16):
            bank = "pb2" if _api(h) == 0 else "pb3"
            def f_pr(t, h=h, bank=bank, xk=xk):
                return t.matmul(sb[bank][64 * _api(h):64 * _api(h) + 64,
                                         64 * _acb(h):64 * _acb(h) + 64],
                                slot(mats[xk], h), slot(mats["PP"], h),
                                start=True, stop=True,
                                tile_position=(64 * _api(h), 64 * _api(h)))
            P.emit("tensor", f_pr, reads=[f"a_{xk}", "a_PP"], writes=[bank],
                   tag=f"{tg}pr{k}_{h}")
        for pi_ in range(2):
            bank = "pb2" if pi_ == 0 else "pb3"
            def f_pe(v, pi_=pi_, bank=bank):
                return v.tensor_add(mats["PP"][64 * pi_:64 * pi_ + 64, :],
                                    mats["PP"][64 * pi_:64 * pi_ + 64, :],
                                    sb[bank][64 * pi_:64 * pi_ + 64, 0:512])
            P.emit("vector", f_pe, reads=[bank, "a_PP"], writes=["a_PP"],
                   tag=f"{tg}pe{k}_{pi_}")
        xprev, bprev = xk, bk

    for h in range(16):
        bank = "pb0" if _api(h) == 0 else "pb1"
        def f_kt(t, h=h, bank=bank):
            return t.matmul(sb[bank][64 * _api(h):64 * _api(h) + 64,
                                     64 * _acb(h):64 * _acb(h) + 64],
                            slot(mats["kda"], h), idslice(h),
                            start=True, stop=True,
                            tile_position=(64 * _api(h), 64 * _api(h)))
        P.emit("tensor", f_kt, reads=["a_kda", "identb"], writes=[bank],
               tag=f"{tg}kt{h}")
    for pi_ in range(2):
        bank = "pb0" if pi_ == 0 else "pb1"
        def f_ktl(v, pi_=pi_, bank=bank):
            return v.tensor_tensor(
                mats["KTl"][64 * pi_:64 * pi_ + 64, :].rearrange("p (a n) -> p a n", a=8),
                sb[bank][64 * pi_:64 * pi_ + 64, 0:512].rearrange("p (a n) -> p a n", a=8),
                gcghp[64 * pi_:64 * pi_ + 64, :].unsqueeze(2).to_broadcast([64, 8, 64]),
                ALU.mult)
        P.emit("vector", f_ktl, reads=[bank, "a_gcghp"], writes=["a_KTl"],
               tag=f"{tg}ktl{pi_}")
        def f_grr(v, pi_=pi_, bank=bank):
            return v.tensor_tensor(
                mats["GRr"][64 * pi_:64 * pi_ + 64, :].rearrange("p (a n) -> p a n", a=8),
                sb[bank][64 * pi_:64 * pi_ + 64, 0:512].rearrange("p (a n) -> p a n", a=8),
                bghp[64 * pi_:64 * pi_ + 64, :].unsqueeze(2).to_broadcast([64, 8, 64]),
                ALU.mult)
        P.emit("vector", f_grr, reads=[bank, "a_bghp"], writes=["a_GRr"],
               tag=f"{tg}grr{pi_}")
    for h in range(16):
        bank = "pb2" if _api(h) == 0 else "pb3"
        tile = h // 2
        r0 = 64 * (h % 2)
        def f_vt(t, h=h, bank=bank, tile=tile, r0=r0):
            return t.matmul(sb[bank][64 * _api(h):64 * _api(h) + 64,
                                     64 * _acb(h):64 * _acb(h) + 64],
                            vs[r0:r0 + 64, tile * SC + o: tile * SC + o + 64],
                            idslice(h), start=True, stop=True,
                            tile_position=(r0, 64 * _api(h)))
        P.emit("tensor", f_vt, reads=["a_vs", "identb"], writes=[bank],
               tag=f"{tg}vt{h}")
    for pi_ in range(2):
        bank = "pb2" if pi_ == 0 else "pb3"
        def f_vts(v, pi_=pi_, bank=bank):
            return v.tensor_tensor(
                mats["VTs"][64 * pi_:64 * pi_ + 64, :].rearrange("p (a n) -> p a n", a=8),
                sb[bank][64 * pi_:64 * pi_ + 64, 0:512].rearrange("p (a n) -> p a n", a=8),
                bhp[64 * pi_:64 * pi_ + 64, :].unsqueeze(2).to_broadcast([64, 8, 64]),
                ALU.mult)
        P.emit("vector", f_vts, reads=[bank, "a_bhp"], writes=["a_VTs"],
               tag=f"{tg}vts{pi_}")
        P.emit("vector", lambda v, pi_=pi_, bank=bank: v.tensor_copy(
            mats["VTu"][64 * pi_:64 * pi_ + 64, :],
            sb[bank][64 * pi_:64 * pi_ + 64, 0:512]),
            reads=[bank], writes=["a_VTu"], tag=f"{tg}vtu{pi_}")

    def hp_mm(name, lhs, rhs, banks, outm, outk):
        for h in range(16):
            bank = banks[_api(h)]
            def f_m(t, h=h, bank=bank, lhs=lhs, rhs=rhs):
                return t.matmul(sb[bank][64 * _api(h):64 * _api(h) + 64,
                                         64 * _acb(h):64 * _acb(h) + 64],
                                slot(mats[lhs], h), slot(mats[rhs], h),
                                start=True, stop=True,
                                tile_position=(64 * _api(h), 64 * _api(h)))
            P.emit("tensor", f_m, reads=[f"a_{lhs}", f"a_{rhs}"], writes=[bank],
                   tag=f"{tg}{name}{h}")
        for pi_ in range(2):
            bank = banks[pi_]
            P.emit("vector", lambda v, pi_=pi_, bank=bank, outm=outm: v.tensor_copy(
                mats[outm][64 * pi_:64 * pi_ + 64, :],
                sb[bank][64 * pi_:64 * pi_ + 64, 0:512]),
                reads=[bank], writes=[outk], tag=f"{tg}{name}e{pi_}")

    hp_mm("G", "PP", "GRr", ("pb4", "pb5"), "GGm", "a_GGm")
    hp_mm("U", "PP", "VTs", ("pb6", "pb7"), "UVm", "a_UVm")
    for h in range(16):
        bank = "pb0" if _api(h) == 0 else "pb1"
        def f_w1(t, h=h, bank=bank):
            return t.matmul(sb[bank][64 * _api(h):64 * _api(h) + 64,
                                     64 * _acb(h):64 * _acb(h) + 64],
                            slot(mats["GGm"], h), slot(mats["HTa"], h),
                            start=True, stop=True,
                            tile_position=(64 * _api(h), 64 * _api(h)))
        P.emit("tensor", f_w1, reads=["a_GGm", "a_HTa"], writes=[bank],
               tag=f"{tg}w1{h}")
    for pi_ in range(2):
        bank = "pb0" if pi_ == 0 else "pb1"
        def f_e2(v, pi_=pi_, bank=bank):
            return v.tensor_sub(mats["E2m"][64 * pi_:64 * pi_ + 64, :],
                                mats["qsa"][64 * pi_:64 * pi_ + 64, :],
                                sb[bank][64 * pi_:64 * pi_ + 64, 0:512])
        P.emit("vector", f_e2, reads=[bank, "a_qsa"], writes=["a_E2m"],
               tag=f"{tg}e2{pi_}")

    first = {0: True, 1: True}
    for h in range(16):
        bank = "pb2" if _api(h) == 0 else "pb3"
        st = first[_api(h)]
        first[_api(h)] = False
        def f_f(t, h=h, bank=bank, st=st):
            return t.matmul(sb[bank][64 * _api(h):64 * _api(h) + 64,
                                     64 * _acb(h):64 * _acb(h) + 64],
                            slot(mats["UVm"], h), slot(mats["HTa"], h),
                            start=st, stop=False,
                            tile_position=(64 * _api(h), 64 * _api(h)))
        P.emit("tensor", f_f, reads=["a_UVm", "a_HTa"], writes=[bank],
               tag=f"{tg}F{h}")
    for h in range(16):
        bank = "pb2" if _api(h) == 0 else "pb3"
        def f_dp(t, h=h, bank=bank):
            return t.matmul(sb[bank][64 * _api(h):64 * _api(h) + 64,
                                     64 * _acb(h):64 * _acb(h) + 64],
                            slot(mats["VTu"], h), slot(mats["HTd"], h),
                            start=False, stop=False,
                            tile_position=(64 * _api(h), 64 * _api(h)))
        P.emit("tensor", f_dp, reads=["a_VTu", "a_HTd"], writes=[bank],
               tag=f"{tg}Dp{h}")
    for h in range(16):
        bank = "pb2" if _api(h) == 0 else "pb3"
        def f_m2(t, h=h, bank=bank):
            return t.matmul(sb[bank][64 * _api(h):64 * _api(h) + 64,
                                     64 * _acb(h):64 * _acb(h) + 64],
                            slot(mats["Sb"], h), slot(mats["E2m"], h),
                            start=False, stop=True,
                            tile_position=(64 * _api(h), 64 * _api(h)))
        P.emit("tensor", f_m2, reads=["a_Sb", "a_E2m"], writes=[bank],
               tag=f"{tg}m2{h}")
    for pi_ in range(2):
        bank = "pb2" if pi_ == 0 else "pb3"
        for grp in range(2):
            def f_oe(v, pi_=pi_, bank=bank, grp=grp):
                src = sb[bank][64 * pi_:64 * pi_ + 64,
                               grp * 256:grp * 256 + 256]                     .rearrange("p (a n) -> p a n", a=4)
                dst = ofm[64 * pi_:64 * pi_ + 64, :]                     .rearrange("p (a n) -> p a n", a=8)[:, grp::2, o:o + 64]
                return v.tensor_copy(dst, src)
            P.emit("vector", f_oe, reads=[bank], writes=["a_ofm"],
                   tag=f"{tg}oe{pi_}{grp}")

    for h in range(16):
        bank = "pb4" if _api(h) == 0 else "pb5"
        def f_pc(t, h=h, bank=bank):
            return t.matmul(sb[bank][64 * _api(h):64 * _api(h) + 64,
                                     64 * _acb(h):64 * _acb(h) + 64],
                            slot(mats["GGm"], h), slot(mats["KTl"], h),
                            start=True, stop=True,
                            tile_position=(64 * _api(h), 64 * _api(h)))
        P.emit("tensor", f_pc, reads=["a_GGm", "a_KTl"], writes=[bank],
               tag=f"{tg}Pc{h}")
    for pi_ in range(2):
        bank = "pb4" if pi_ == 0 else "pb5"
        def f_pt(v, pi_=pi_, bank=bank):
            return v.scalar_tensor_tensor(
                mats["E2m"][64 * pi_:64 * pi_ + 64, :],
                sb[bank][64 * pi_:64 * pi_ + 64, 0:512], -1.0,
                gcim[64 * pi_:64 * pi_ + 64, :], ALU.mult, ALU.add)
        P.emit("vector", f_pt, reads=[bank, "a_gcim"], writes=["a_E2m"],
               tag=f"{tg}pt{pi_}")
    for h in range(16):
        bank = "pb6" if _api(h) == 0 else "pb7"
        def f_r(t, h=h, bank=bank):
            return t.matmul(sb[bank][64 * _api(h):64 * _api(h) + 64,
                                     64 * _acb(h):64 * _acb(h) + 64],
                            slot(mats["KTl"], h), slot(mats["UVm"], h),
                            start=True, stop=True,
                            tile_position=(64 * _api(h), 64 * _api(h)))
        P.emit("tensor", f_r, reads=["a_KTl", "a_UVm"], writes=[bank],
               tag=f"{tg}R{h}")
    for pi_ in range(2):
        bank = "pb6" if pi_ == 0 else "pb7"
        P.emit("vector", lambda v, pi_=pi_, bank=bank: v.tensor_copy(
            rf32[64 * pi_:64 * pi_ + 64, :],
            sb[bank][64 * pi_:64 * pi_ + 64, 0:512]),
            reads=[bank], writes=["a_tmpw"], tag=f"{tg}re{pi_}")
    for h in range(16):
        bank = "pb0" if _api(h) == 0 else "pb1"
        def f_s(t, h=h, bank=bank):
            return t.matmul(sb[bank][64 * _api(h):64 * _api(h) + 64,
                                     64 * _acb(h):64 * _acb(h) + 64],
                            slot(mats["E2m"], h), slot(mats["Sb"], h),
                            start=True, stop=True,
                            tile_position=(64 * _api(h), 64 * _api(h)))
        P.emit("tensor", f_s, reads=["a_E2m", "a_Sb"], writes=[bank],
               tag=f"{tg}su{h}")
    for pi_ in range(2):
        bank = "pb0" if pi_ == 0 else "pb1"
        def f_se(v, pi_=pi_, bank=bank):
            return v.tensor_add(sf32[64 * pi_:64 * pi_ + 64, :],
                                rf32[64 * pi_:64 * pi_ + 64, :],
                                sb[bank][64 * pi_:64 * pi_ + 64, 0:512])
        P.emit("vector", f_se, reads=[bank, "a_tmpw"], writes=["a_sf32"],
               tag=f"{tg}se{pi_}")
        P.emit("vector", lambda v, pi_=pi_: v.tensor_copy(
            mats["Sb"][64 * pi_:64 * pi_ + 64, :],
            sf32[64 * pi_:64 * pi_ + 64, :]),
            reads=["a_sf32"], writes=["a_Sb"], tag=f"{tg}sc{pi_}")


def emit_attn_b(cx, x_in, xn_in, x_out, xn_out):
    nc, P = cx.nc, cx.P
    pref = "b"
    xnd = cx.dram[xn_in]
    xind = cx.dram[x_in]
    xoutd = cx.dram[x_out]
    qdd, kdd, vtd = cx.dram["qd"], cx.dram["kdd"], cx.dram["vtd"]
    NTCH = T // TCH
    sb = cx.sb
    identb = cx.sb["identb"]

    def _sec_b0():
        with contextlib.ExitStack() as stk:
            sbuf = lambda nm, shp, dt: cx.sbuf(f"{pref}{nm}", shp, dt, stack=stk)
            xnc = sbuf("p_xnc", [128, 8 * TCH], BF16)
            xmacc = sbuf("p_xm", [128, 8], F32)
            tmpc = sbuf("p_tmp", [128, 1], F32)
            xmb = sbuf("p_xmb", [128, 8], BF16)
            sw1t = sbuf("p_sw1", [128, 8 * 512], BF16)
            pw1t = sbuf("p_pw1", [128, 3 * 8 * 512], BF16)
            sw2t = sbuf("p_sw2", [128, 4 * 3], BF16)
            pw2t = sbuf("p_pw2", [128, 3 * 4 * 16], BF16)
            sb1t = sbuf("p_sb1", [128, 4], F32)
            pb1t = sbuf("p_pb1", [128, 12], F32)
            pb2t = sbuf("p_pb2", [16, 3], F32)
            sb2t = sbuf("p_sb2", [3, 1], F32)
            indT = sbuf("p_indT", [16, 8 * 128], BF16)
            h1b = sbuf("p_h1", [128, 4], BF16)
            hidb = sbuf("p_hid", [128, 4], BF16)
            gf32 = sbuf("p_gf32", [16, 1], F32)
            scr = sbuf("p_scr", [3, 4], F32)
            polrow = sbuf("p_pol", [1, 4], F32)
            zrow = sbuf("p_z", [1, 1], F32)
            rowbuf = sbuf("p_row", [1, 16], F32)
            hgrow = sbuf("p_hg", [1, 16], F32)
            hgrowb = sbuf("p_hgb", [1, 16], BF16)
            gcolb = sbuf("p_gcb", [16, 1], BF16)
            hgcolb = sbuf("p_hgc", [16, 1], BF16)
            onesb = sbuf("p_onesb", [1, 1], BF16)
            P.emit("vector", lambda v: v.memset(onesb[:], 1.0), writes=["b_onesb"],
                   tag="bonesb")
            P.emit("vector", lambda v: v.memset(xmacc[:], 0.0), writes=["b_xmacc"],
                   tag="bxmz")
            P.emit("gpsimd", lambda e: e.dma_start(
                out=sw1t[:].rearrange("p (a n) -> p a n", a=8),
                in_=cx.dram["b_sw1_t"].rearrange("(a p) n -> p a n", p=128)),
                writes=["b_sw1t"], dma=True, tag="bldsw1")
            P.emit("gpsimd", lambda e: e.dma_start(
                out=pw1t[:].rearrange("p (q a n) -> p q a n", q=3, a=8),
                in_=cx.dram["b_pw1_t"].rearrange("q (a p) n -> p q a n", p=128)),
                writes=["b_pw1t"], dma=True, tag="bldpw1")
            P.emit("gpsimd", lambda e: e.dma_start(
                out=sw2t[:].rearrange("p (a n) -> p a n", a=4),
                in_=cx.dram["b_sw2_t"].rearrange("(a p) n -> p a n", p=128)),
                writes=["b_sw2t"], dma=True, tag="bldsw2")
            P.emit("gpsimd", lambda e: e.dma_start(
                out=pw2t[:].rearrange("p (q a n) -> p q a n", q=3, a=4),
                in_=cx.dram["b_pw2_t"].rearrange("q (a p) n -> p q a n", p=128)),
                writes=["b_pw2t"], dma=True, tag="bldpw2")
            P.emit("sync", lambda e: e.dma_start(out=sb1t[:], in_=cx.dram["b_sb1_t"]),
                   writes=["b_sb1t"], dma=True, tag="bldsb1")
            P.emit("sync", lambda e: e.dma_start(out=pb1t[:], in_=cx.dram["b_pb1_t"]),
                   writes=["b_pb1t"], dma=True, tag="bldpb1")
            P.emit("sync", lambda e: e.dma_start(out=pb2t[:], in_=cx.dram["b_pb2_t"]),
                   writes=["b_pb2t"], dma=True, tag="bldpb2")
            P.emit("sync", lambda e: e.dma_start(out=sb2t[:], in_=cx.dram["b_sb2_t"]),
                   writes=["b_sb2t"], dma=True, tag="bldsb2")
            P.emit("gpsimd", lambda e: e.dma_start(
                out=indT[:].rearrange("p (a n) -> p a n", a=8),
                in_=cx.dram["c_ind_hc_T"].rearrange("a p n -> p a n")),
                writes=["b_indT"], dma=True, tag="bldind")
            for c in range(NTCH):
                sl = np.s_[c * TCH:(c + 1) * TCH]
                P.emit("sync", lambda e, sl=sl: e.dma_start(
                    out=xnc[:].rearrange("p (a n) -> p a n", a=8),
                    in_=xnd[:, sl].rearrange("(a p) n -> p a n", p=128)),
                    reads=[f"{xn_in}:{c}"], writes=["b_xnc"], dma=True, tag=f"bxml{c}")
                for i in range(8):
                    P.emit("vector", lambda v, i=i: v.reduce_sum(
                        tmpc[:, 0:1], xnc[:, i * TCH:(i + 1) * TCH],
                        axis=mybir.AxisListType.X),
                        reads=["b_xnc"], writes=["b_tmpc"], tag=f"bxr{c}_{i}")
                    P.emit("vector", lambda v, i=i: v.tensor_add(
                        xmacc[:, i:i + 1], xmacc[:, i:i + 1], tmpc[:, 0:1]),
                        reads=["b_tmpc", "b_xmacc"], writes=["b_xmacc"], tag=f"bxa{c}_{i}")
            P.emit("vector", lambda v: v.tensor_scalar(xmacc[:], xmacc[:], 1.0 / T, None,
                                                       ALU.mult),
                   reads=["b_xmacc"], writes=["b_xmacc"], tag="bxm_s")
            P.emit("vector", lambda v: v.tensor_copy(xmb[:], xmacc[:]),
                   reads=["b_xmacc"], writes=["b_xmb"], tag="bxm_c")

            def f_h1(t):
                last = None
                for m4 in range(4):
                    for k in range(8):
                        last = t.matmul(sb["pb0"][:, m4:m4 + 1],
                                        sw1t[:, k * 512 + m4 * 128: k * 512 + (m4 + 1) * 128],
                                        xmb[:, k:k + 1], start=(k == 0), stop=(k == 7))
                return last
            P.emit("tensor", f_h1, reads=["b_xmb", "b_sw1t"], writes=["pb0"], tag="bh1mm")
            for m4 in range(4):
                P.emit("scalar", lambda s, m4=m4: s.activation(
                    h1b[:, m4:m4 + 1], sb["pb0"][:, m4:m4 + 1], AF.Gelu,
                    bias=sb1t[:, m4:m4 + 1]),
                    reads=["pb0", "b_sb1t"], writes=["b_h1b"], tag=f"bh1g{m4}")

            def f_pol(t):
                last = None
                for k in range(4):
                    last = t.matmul(sb["pb1"][0:3, 0:1], sw2t[:, k * 3:(k + 1) * 3],
                                    h1b[:, k:k + 1], start=(k == 0), stop=(k == 3))
                return last
            P.emit("tensor", f_pol, reads=["b_h1b", "b_sw2t"], writes=["pb1"], tag="bpolmm")
            P.emit("vector", lambda v: v.tensor_add(scr[0:3, 0:1], sb["pb1"][0:3, 0:1],
                                                    sb2t[:]),
                   reads=["pb1", "b_sb2t"], writes=["b_scr"], tag="bpola")
            P.emit("vector", lambda v: v.tensor_copy(gcolb[0:3, 0:1], scr[0:3, 0:1]),
                   reads=["b_scr"], writes=["b_gcolb"], tag="bpolc")
            P.emit("tensor", lambda t: t.matmul(sb["pb2"][0:1, 0:3], gcolb[0:3, 0:1],
                                                identb[0:3, 0:3], start=True, stop=True),
                   reads=["b_gcolb", "identb"], writes=["pb2"], tag="bpoltr")
            P.emit("scalar", lambda s: s.activation(polrow[0:1, 0:3], sb["pb2"][0:1, 0:3],
                                                    AF.Exp, accum_out=zrow[0:1, 0:1]),
                   reads=["pb2"], writes=["b_polrow", "b_zrow"], tag="bpolex")
            P.emit("vector", lambda v: v.reciprocal(zrow[0:1, 0:1], zrow[0:1, 0:1]),
                   reads=["b_zrow"], writes=["b_zrow"], tag="bpolrc")
            P.emit("vector", lambda v: v.tensor_scalar(polrow[0:1, 0:3], polrow[0:1, 0:3],
                                                       zrow[0:1, 0:1], None, ALU.mult),
                   reads=["b_zrow", "b_polrow"], writes=["b_polrow"], tag="bpoln")
            for p_ in range(3):
                def f_hid(t, p_=p_):
                    last = None
                    for m4 in range(4):
                        for k in range(8):
                            last = t.matmul(
                                sb["pb0"][:, m4:m4 + 1],
                                pw1t[:, p_ * 4096 + k * 512 + m4 * 128:
                                     p_ * 4096 + k * 512 + (m4 + 1) * 128],
                                xmb[:, k:k + 1], start=(k == 0), stop=(k == 7))
                    return last
                P.emit("tensor", f_hid, reads=["b_xmb", "b_pw1t"], writes=["pb0"],
                       tag=f"bhid{p_}")
                for m4 in range(4):
                    P.emit("scalar", lambda s, m4=m4, p_=p_: s.activation(
                        hidb[:, m4:m4 + 1], sb["pb0"][:, m4:m4 + 1], AF.Gelu,
                        bias=pb1t[:, p_ * 4 + m4: p_ * 4 + m4 + 1]),
                        reads=["pb0", "b_pb1t"], writes=["b_hidb"], tag=f"bhg{p_}_{m4}")

                def f_gt(t, p_=p_):
                    last = None
                    for k in range(4):
                        last = t.matmul(sb["pb1"][0:16, 0:1],
                                        pw2t[:, p_ * 64 + k * 16: p_ * 64 + (k + 1) * 16],
                                        hidb[:, k:k + 1], start=(k == 0), stop=(k == 3))
                    return last
                P.emit("tensor", f_gt, reads=["b_hidb", "b_pw2t"], writes=["pb1"],
                       tag=f"bgt{p_}")
                P.emit("vector", lambda v, p_=p_: v.tensor_add(
                    gf32[:], sb["pb1"][0:16, 0:1], pb2t[:, p_:p_ + 1]),
                    reads=["pb1", "b_pb2t"], writes=["b_gf32"], tag=f"bga{p_}")
                P.emit("vector", lambda v: v.tensor_copy(gcolb[:], gf32[:]),
                       reads=["b_gf32"], writes=["b_gcolb"], tag=f"bgc{p_}")
                P.emit("tensor", lambda t: t.matmul(sb["pb2"][0:1, 0:16], gcolb[0:16, 0:1],
                                                    identb[0:16, 0:16], start=True,
                                                    stop=True),
                       reads=["b_gcolb", "identb"], writes=["pb2"], tag=f"bgtr{p_}")
                P.emit("scalar", lambda s: s.activation(rowbuf[0:1, 0:16],
                                                        sb["pb2"][0:1, 0:16],
                                                        AF.Exp, accum_out=zrow[0:1, 0:1]),
                       reads=["pb2"], writes=["b_rowbuf", "b_zrow"], tag=f"bgex{p_}")
                P.emit("vector", lambda v: v.reciprocal(zrow[0:1, 0:1], zrow[0:1, 0:1]),
                       reads=["b_zrow"], writes=["b_zrow"], tag=f"bgrc{p_}")
                P.emit("vector", lambda v: v.tensor_scalar(rowbuf[0:1, 0:16],
                                                           rowbuf[0:1, 0:16],
                                                           zrow[0:1, 0:1], None, ALU.mult),
                       reads=["b_zrow", "b_rowbuf"], writes=["b_rowbuf"], tag=f"bgn{p_}")
                if p_ == 0:
                    P.emit("vector", lambda v, p_=p_: v.tensor_scalar(
                        hgrow[0:1, :], rowbuf[0:1, 0:16], polrow[0:1, p_:p_ + 1], None,
                        ALU.mult),
                        reads=["b_rowbuf", "b_polrow"], writes=["b_hgrow"], tag=f"bhga{p_}")
                else:
                    P.emit("vector", lambda v, p_=p_: v.scalar_tensor_tensor(
                        hgrow[0:1, :], rowbuf[0:1, 0:16], polrow[0:1, p_:p_ + 1],
                        hgrow[0:1, :], ALU.mult, ALU.add),
                        reads=["b_rowbuf", "b_polrow", "b_hgrow"], writes=["b_hgrow"],
                        tag=f"bhga{p_}")
            P.emit("vector", lambda v: v.tensor_scalar(hgrow[0:1, :], hgrow[0:1, :], 0.125,
                                                       None, ALU.mult),
                   reads=["b_hgrow"], writes=["b_hgrow"], tag="bhgs")
            P.emit("vector", lambda v: v.tensor_copy(hgrowb[0:1, :], hgrow[0:1, :]),
                   reads=["b_hgrow"], writes=["b_hgrowb"], tag="bhgc")
            P.emit("tensor", lambda t: t.matmul(sb["pb2"][0:16, 0:1], hgrowb[0:1, :],
                                                onesb[0:1, 0:1], start=True, stop=True),
                   reads=["b_hgrowb", "b_onesb"], writes=["pb2"], tag="bhgtr")
            P.emit("vector", lambda v: v.tensor_copy(hgcolb[:], sb["pb2"][0:16, 0:1]),
                   reads=["pb2"], writes=["b_hgcolb"], tag="bhgcc")

            def f_hgc(t):
                last = None
                for j in range(8):
                    last = t.matmul(sb["pb3"][:, j:j + 1], indT[:, j * 128:(j + 1) * 128],
                                    hgcolb[0:16, 0:1], start=True, stop=True)
                return last
            P.emit("tensor", f_hgc, reads=["b_hgcolb", "b_indT"], writes=["pb3"],
                   tag="bhgcm")
            P.emit("vector", lambda v: v.tensor_copy(cx.sb["hgc_g"][:], sb["pb3"][:, 0:8]),
                   reads=["pb3"], writes=["hgc_g"], tag="bhgce")
            P.barrier()
    _sec_b0()

    def _sec_b1a():
        with contextlib.ExitStack() as stk:
            sbuf = lambda nm, shp, dt: cx.sbuf(f"{pref}{nm}", shp, dt, stack=stk)
            wqs = sbuf("wqs", [128, 8 * 1024], BF16)
            wks = sbuf("wks", [128, 8 * 1024], BF16)
            wvs = sbuf("wvs", [128, 8 * 1024], BF16)
            bqc = sbuf("bqc", [128, 8], F32)
            bkc = sbuf("bkc", [128, 8], F32)
            bvr = sbuf("bvr", [1, 1024], BF16)
            xnc = [sbuf(f"xnc{i}", [128, 8 * TCH], BF16) for i in range(2)]
            qsb = sbuf("qsb", [128, 512], BF16)
            ksb = sbuf("ksb", [128, 512], BF16)
            vsb = sbuf("vsb", [128, 1024], BF16)
            onesbr = sbuf("onesbr", [1, 128], BF16)
            P.emit("vector", lambda v: v.memset(onesbr[:], 1.0), writes=["b_onesbr"],
                   tag="bones2")
            for nm, key in (("wqs", "bwq_t"), ("wks", "bwk_t"), ("wvs", "bwv_t")):
                P.emit("gpsimd", lambda e, nm=nm, key=key: e.dma_start(
                    out=cx.sb[f"{pref}{nm}"][:].rearrange("p (a n) -> p a n", a=8),
                    in_=cx.dram[key].rearrange("(a p) n -> p a n", p=128)),
                    writes=[f"b_{nm}"], dma=True, tag=f"bld{nm}")
            P.emit("sync", lambda e: e.dma_start(out=bqc[:], in_=cx.dram["b_bqc"]),
                   writes=["b_bqc"], dma=True, tag="bldbq")
            P.emit("sync", lambda e: e.dma_start(out=bkc[:], in_=cx.dram["b_bkc"]),
                   writes=["b_bkc"], dma=True, tag="bldbk")
            P.emit("gpsimd", lambda e: e.dma_start(out=bvr[:], in_=cx.dram["b_bvr"]),
                   writes=["b_bvr"], dma=True, tag="bldbv")
            for c in range(NTCH):
                bi = c % 2
                sl = np.s_[c * TCH:(c + 1) * TCH]
                kxn = f"b_xnc{bi}"
                P.emit("sync", lambda e, sl=sl, bi=bi: e.dma_start(
                    out=xnc[bi][:].rearrange("p (a n) -> p a n", a=8),
                    in_=xnd[:, sl].rearrange("(a p) n -> p a n", p=128)),
                    reads=[f"{xn_in}:{c}"], writes=[kxn], dma=True, tag=f"bpx{c}")
                for m in range(8):
                    def f_qm(t, m=m, bi=bi):
                        last = None
                        for k in range(8):
                            last = t.matmul(
                                sb["pb0"][:, 0:512],
                                wqs[:, k * 1024 + m * 128: k * 1024 + (m + 1) * 128],
                                xnc[bi][:, k * TCH:(k + 1) * TCH],
                                start=(k == 0), stop=(k == 7))
                        return last
                    P.emit("tensor", f_qm, reads=[kxn, "b_wqs"], writes=["pb0"],
                           tag=f"bqmm{c}_{m}")

                    def f_qe(v, m=m):
                        return v.scalar_tensor_tensor(
                            qsb[:], sb["pb0"][:, 0:512], bqc[:, m:m + 1],
                            cx.sb["hgc_g"][:, m:m + 1].to_broadcast([128, 512]),
                            ALU.add, ALU.mult)
                    P.emit("vector", f_qe, reads=["pb0", "b_bqc", "hgc_g"],
                           writes=["b_qsb"], tag=f"bqe{c}_{m}")
                    P.emit("sync", lambda e, m=m, sl=sl: e.dma_start(
                        out=qdd[m * 128:(m + 1) * 128, sl], in_=qsb[:]),
                        reads=["b_qsb"], writes=[f"qd:{c}_{m}"], dma=True,
                        tag=f"bqs{c}_{m}")

                    def f_km(t, m=m, bi=bi):
                        last = None
                        for k in range(8):
                            last = t.matmul(
                                sb["pb1"][:, 0:512],
                                wks[:, k * 1024 + m * 128: k * 1024 + (m + 1) * 128],
                                xnc[bi][:, k * TCH:(k + 1) * TCH],
                                start=(k == 0), stop=(k == 7))
                        return last
                    P.emit("tensor", f_km, reads=[kxn, "b_wks"], writes=["pb1"],
                           tag=f"bkmm{c}_{m}")
                    P.emit("vector", lambda v, m=m: v.tensor_scalar(
                        ksb[:], sb["pb1"][:, 0:512], bkc[:, m:m + 1], None, ALU.add),
                        reads=["pb1", "b_bkc"], writes=["b_ksb"], tag=f"bke{c}_{m}")
                    P.emit("sync", lambda e, m=m, sl=sl: e.dma_start(
                        out=kdd[m * 128:(m + 1) * 128, sl], in_=ksb[:]),
                        reads=["b_ksb"], writes=[f"kd:{c}_{m}"], dma=True,
                        tag=f"bks{c}_{m}")
                for tb in range(4):
                    def f_vm(t, tb=tb, bi=bi):
                        last = None
                        for nn in range(2):
                            dstp = sb["pb2"] if nn == 0 else sb["pb3"]
                            for k in range(8):
                                t.matmul(dstp[:, 0:512],
                                         xnc[bi][:, k * TCH + tb * 128: k * TCH + (tb + 1) * 128],
                                         wvs[:, k * 1024 + nn * 512: k * 1024 + (nn + 1) * 512],
                                         start=(k == 0), stop=False)
                            last = t.matmul(dstp[:, 0:512], onesbr[0:1, :],
                                            bvr[0:1, nn * 512:(nn + 1) * 512],
                                            start=False, stop=True)
                        return last
                    P.emit("tensor", f_vm, reads=[kxn, "b_wvs", "b_bvr", "b_onesbr"],
                           writes=["pb2", "pb3"], tag=f"bvmm{c}_{tb}")

                    def f_ve(v):
                        v.tensor_copy(vsb[:, 0:512], sb["pb2"][:, 0:512])
                        return v.tensor_copy(vsb[:, 512:1024], sb["pb3"][:, 0:512])
                    P.emit("vector", f_ve, reads=["pb2", "pb3"], writes=["b_vsb"],
                           tag=f"bve{c}_{tb}")
                    P.emit("sync", lambda e, c=c, tb=tb: e.dma_start(
                        out=vtd[c * 4 + tb].rearrange("p (h n) -> p h n", n=65)[:, :, 0:64],
                        in_=vsb[:].rearrange("p (h n) -> p h n", n=64)),
                        reads=["b_vsb"], writes=[f"vt:{c}_{tb}"], dma=True,
                        tag=f"bvs{c}_{tb}")
            P.barrier()
    _sec_b1a()

    def _sec_b1bc():
        with contextlib.ExitStack() as stk:
            sbuf = lambda nm, shp, dt: cx.sbuf(f"{pref}{nm}", shp, dt, stack=stk)
            kds = sbuf("kds", [128, 8 * 4096], BF16)
            vts = sbuf("vts", [128, 32 * 1040], BF16)
            wos = sbuf("wos", [128, 8 * 1024], BF16)
            boc = sbuf("boc", [128, 8], F32)
            mskb = sbuf("mskb", [128, 4 * 512], BF16)
            qg = sbuf("qg", [128, 8 * 512], BF16)
            pbuf = sbuf("pbuf", [128, 4 * 512], BF16)
            rrec2 = [sbuf("rrec0", [1, 512], F32), sbuf("rrec1", [1, 512], F32)]
            otmps = [sbuf("otmp0", [128, 512], F32), sbuf("otmp1", [128, 512], F32)]
            xck = sbuf("xck", [128, 8 * 256], F32)
            ln = LNHelper(cx, 256, "blna", stack=stk) if xn_out is not None else None
            P.emit("sync", lambda e: e.dma_start(
                out=kds[:].rearrange("p (a n) -> p a n", a=8),
                in_=kdd.rearrange("(a p) n -> p a n", p=128)),
                reads=[f"kd:{c}_{m}" for c in range(8) for m in range(8)],
                writes=["b_kds"], dma=True, tag="bldkds")
            P.emit("sync", lambda e: e.dma_start(
                out=vts[:].rearrange("p (a n) -> p a n", a=32),
                in_=vtd.rearrange("a p n -> p a n")),
                reads=[f"vt:{c}_{tb}" for c in range(8) for tb in range(4)],
                writes=["b_vts"], dma=True, tag="bldvts")
            P.emit("vector", lambda v: v.memset(
                vts[:].rearrange("p (a n) -> p a n", n=65)[:, :, 64:65]
                .rearrange("p a n -> p (a n)"), 1.0),
                writes=["b_vts"], tag="bvones")
            P.emit("gpsimd", lambda e: e.dma_start(
                out=wos[:].rearrange("p (a n) -> p a n", a=8),
                in_=cx.dram["bwo_t"].rearrange("(a p) n -> p a n", p=128)),
                writes=["b_wos"], dma=True, tag="bldwo")
            P.emit("sync", lambda e: e.dma_start(out=boc[:], in_=cx.dram["b_boc"]),
                   writes=["b_boc"], dma=True, tag="bldbo")
            P.emit("gpsimd", lambda e: e.dma_start(
                out=mskb[:].rearrange("p (a n) -> p a n", a=4),
                in_=cx.dram["c_att_mask"].rearrange("a p n -> p a n")),
                writes=["b_mskb"], dma=True, tag="bldmsk")
            xck2 = sbuf("xck2", [128, 8 * 256], F32)
            xck_ = [xck, xck2]
            for g in range(8):
                gsl = np.s_[g * 512:(g + 1) * 512]
                P.emit("sync", lambda e, gsl=gsl: e.dma_start(
                    out=qg[:].rearrange("p (a n) -> p a n", a=8),
                    in_=qdd[:, gsl].rearrange("(a p) n -> p a n", p=128)),
                    reads=[f"qd:{g}_{m}" for m in range(8)],
                    writes=[f"b_qg{t}_{pp}" for t in range(8) for pp in range(2)],
                    dma=True, tag=f"bqg{g}")
                for h in range(16):
                    tile = h // 2
                    r0 = 64 * (h % 2)
                    nkb = 4 * g + 4
                    pav = "pb2" if h % 2 == 0 else "pb3"
                    pex = "pb4" if h % 2 == 0 else "pb5"
                    for kq in range(0, nkb, 4):
                        kbs = list(range(kq, min(kq + 4, nkb)))
                        for kb in kbs:
                            pqk = ("pb0", "pb1", "pb6", "pb7")[kb % 4]

                            def f_qk(t, tile=tile, r0=r0, kb=kb, pqk=pqk):
                                return t.matmul(
                                    sb[pqk][:, 0:512],
                                    kds[r0:r0 + 64,
                                        tile * 4096 + kb * 128: tile * 4096 + (kb + 1) * 128],
                                    qg[r0:r0 + 64, tile * 512:(tile + 1) * 512],
                                    start=True, stop=True, tile_position=(r0, 0))
                            P.emit("tensor", f_qk,
                                   reads=["b_kds", f"b_qg{tile}_{h % 2}"], writes=[pqk],
                                   tag=f"bqk{g}_{h}_{kb}")
                            if kb >= 4 * g:
                                r = kb - 4 * g

                                def f_mask(v, r=r, pqk=pqk):
                                    return v.tensor_add(sb[pqk][:, 0:512], sb[pqk][:, 0:512],
                                                        mskb[:, r * 512:(r + 1) * 512])
                                P.emit("vector", f_mask, reads=[pqk, "b_mskb"], writes=[pqk],
                                       tag=f"bmk{g}_{h}_{kb}")

                            def f_exp(s, kb=kb, pqk=pqk):
                                return s.activation(pbuf[:, (kb % 4) * 512:(kb % 4) * 512 + 512],
                                                    sb[pqk][:, 0:512], AF.Exp)
                            P.emit("scalar", f_exp, reads=[pqk], writes=[f"b_pbuf{kb % 4}"],
                                   tag=f"bex{g}_{h}_{kb}")
                        def f_av(t, kbs=tuple(kbs), h=h, nkb=nkb, pav=pav):
                            last = None
                            for kb in kbs:
                                last = t.matmul(
                                    sb[pav][0:65, 0:512],
                                    vts[:, kb * 1040 + h * 65: kb * 1040 + (h + 1) * 65],
                                    pbuf[:, (kb % 4) * 512:(kb % 4) * 512 + 512],
                                    start=(kb == 0), stop=(kb == nkb - 1))
                            return last
                        P.emit("tensor", f_av,
                               reads=[f"b_pbuf{kb % 4}" for kb in kbs] + ["b_vts"],
                               writes=[pav], tag=f"bav{g}_{h}_{kq}")

                    P.emit("vector", lambda v, pav=pav, h=h: v.reciprocal(
                        rrec2[h % 2][0:1, 0:512], sb[pav][64:65, 0:512]),
                        reads=[pav], writes=[f"b_rrec{h % 2}"], tag=f"brc{g}_{h}")

                    def emit_fin(hf):
                        pavf = "pb2" if hf % 2 == 0 else "pb3"
                        pexf = "pb4" if hf % 2 == 0 else "pb5"
                        tf = hf // 2
                        r0f = 64 * (hf % 2)

                        def f_re(t, pexf=pexf, hf=hf):
                            return t.matmul(sb[pexf][0:64, 0:512],
                                            cx.sb["ones_row"][0:1, 0:64],
                                            rrec2[hf % 2][0:1, 0:512],
                                            start=True, stop=True)
                        P.emit("tensor", f_re, reads=[f"b_rrec{hf % 2}", "ones_row"],
                               writes=[pexf], tag=f"bre{g}_{hf}")

                        def f_oc(v, pavf=pavf, hf=hf):
                            return v.tensor_copy(otmps[hf % 2][0:64, 0:512],
                                                 sb[pavf][0:64, 0:512])
                        P.emit("vector", f_oc, reads=[pavf], writes=[f"b_otmp{hf % 2}"],
                               tag=f"boc{g}_{hf}")

                        def f_ev(v, pexf=pexf, tf=tf, r0f=r0f, hf=hf):
                            return v.tensor_tensor(
                                qg[r0f:r0f + 64, tf * 512:(tf + 1) * 512],
                                otmps[hf % 2][0:64, 0:512], sb[pexf][0:64, 0:512],
                                ALU.mult)
                        P.emit("vector", f_ev, reads=[f"b_otmp{hf % 2}", pexf],
                               writes=[f"b_qg{tf}_{hf % 2}"], tag=f"bev{g}_{hf}")
                    if h > 0:
                        emit_fin(h - 1)
                    if h == 15:
                        emit_fin(15)
                for hh in range(2):
                    csl = np.s_[g * 512 + hh * 256: g * 512 + (hh + 1) * 256]
                    cidx = g * 2 + hh
                    P.emit("sync", lambda e, csl=csl, hh=hh: e.dma_start(
                        out=xck_[hh][:].rearrange("p (a n) -> p a n", a=8),
                        in_=xind[:, csl].rearrange("(a p) n -> p a n", p=128)),
                        reads=[f"{x_in}:{cidx // 2}"],
                        writes=[f"b_xck{hh}"], dma=True, tag=f"blx{g}_{hh}")
                qgall = [f"b_qg{t}_{pp}" for t in range(8) for pp in range(2)]
                for m in range(8):
                    pwo = "pb6" if m % 2 == 0 else "pb7"

                    def f_wo(t, m=m, pwo=pwo):
                        last = None
                        for k in range(8):
                            last = t.matmul(
                                sb[pwo][:, 0:512],
                                wos[:, k * 1024 + m * 128: k * 1024 + (m + 1) * 128],
                                qg[:, k * 512:(k + 1) * 512],
                                start=(k == 0), stop=(k == 7))
                        return last
                    P.emit("tensor", f_wo, reads=qgall + ["b_wos"], writes=[pwo],
                           tag=f"bwo{g}_{m}")
                    for hh in range(2):
                        def f_woe(v, m=m, hh=hh, pwo=pwo):
                            return v.scalar_tensor_tensor(
                                xck_[hh][:, m * 256:(m + 1) * 256],
                                sb[pwo][:, hh * 256:(hh + 1) * 256],
                                boc[:, m:m + 1], xck_[hh][:, m * 256:(m + 1) * 256],
                                ALU.add, ALU.add)
                        P.emit("vector", f_woe, reads=[pwo, "b_boc", f"b_xck{hh}"],
                               writes=[f"b_xck{hh}"], tag=f"bwoe{g}_{m}_{hh}")
                for hh in range(2):
                    csl = np.s_[g * 512 + hh * 256: g * 512 + (hh + 1) * 256]
                    cidx = g * 2 + hh
                    P.emit("sync", lambda e, csl=csl, hh=hh: e.dma_start(
                        out=xoutd[:, csl].rearrange("(a p) n -> p a n", p=128),
                        in_=xck_[hh][:].rearrange("p (a n) -> p a n", a=8)),
                        reads=[f"b_xck{hh}"], writes=[f"{x_out}:{cidx // 2}"],
                        dma=True, tag=f"bso{g}_{hh}")
                    if xn_out is not None:
                        ln.emit([f"b_xck{hh}"], xck_[hh][:],
                                cx.dram[xn_out][:, csl].rearrange("(a p) n -> p a n", p=128),
                                [f"{xn_out}:{cidx // 2}"])
            P.barrier()
    _sec_b1bc()


def emit_copy_dram(cx, src, dst):
    P = cx.P
    s = cx.dram[src]
    dd = cx.dram[dst]
    with contextlib.ExitStack() as stk:
        buf = cx.sbuf("cpbuf", [128, 8 * TCH], F32, stack=stk)
        for c in range(T // TCH):
            sl = np.s_[c * TCH:(c + 1) * TCH]
            P.emit("sync", lambda e, sl=sl: e.dma_start(
                out=buf[:].rearrange("p (a n) -> p a n", a=8),
                in_=s[:, sl].rearrange("(a p) n -> p a n", p=128)),
                reads=[f"{src}:{c}"], writes=["cpbuf"], dma=True, tag="cp_in")
            P.emit("sync", lambda e, sl=sl: e.dma_start(
                out=dd[:, sl].rearrange("(a p) n -> p a n", p=128),
                in_=buf[:].rearrange("p (a n) -> p a n", a=8)),
                reads=["cpbuf"], writes=[f"{dst}:{c}"], dma=True, tag="cp_out")
        P.barrier()



def build_program(hp, phases=("ln0", "attn_a", "ffn1", "attn_b", "ffn2"), dbg=()):
    nc = bass.Bass("TRN2", target_bir_lowering=False)
    P = Prog()
    with contextlib.ExitStack() as stack:
        cx = Ctx(nc, P, stack)
        xin = nc.dram_tensor("x_t", [D, T], F32, kind="ExternalInput").ap()
        out = nc.dram_tensor("out_t", [D, T], F32, kind="ExternalOutput").ap()
        extra = {}
        for k, arr in hp.items():
            t = nc.dram_tensor(k, list(arr.shape), F32, kind="ExternalInput").ap()
            extra[k] = np.asarray(arr, dtype=np.float32)
            cx.dram[k] = t
        def scratch(name, shape, dtype):
            kind = "ExternalOutput" if name in dbg else "Internal"
            cx.dram[name] = nc.dram_tensor(name, shape, dtype, kind=kind).ap()
        cx.dram["xr1"] = nc.dram_tensor("xr1", [D, T], F32, kind="Internal").ap()
        scratch("xr2", [D, T], F32)
        scratch("xtmp", [D, T], F32)
        scratch("wtd", [32, 128, T], BF16)
        scratch("qd", [D, T], BF16)
        scratch("kdd", [D, T], BF16)
        scratch("vtd", [32, 128, 1040], BF16)
        scratch("xn0", [D, T], BF16)
        scratch("dbgt", [D, T], F32)
        if "dbgt" in dbg:
            cx.dram["DBGTAP"] = True
        scratch("xn1", [D, T], BF16)
        cx.dram["x_in"] = xin
        cx.dram["out"] = out

        PS = stack.enter_context(nc.psum_tensor("PS", [128, 4096], F32))
        cx.sb["PS"] = PS
        for i in range(8):
            cx.sb[f"pb{i}"] = PS[:, i * 512:(i + 1) * 512]

        cx.sbuf("onescol", [128, 1], BF16)
        P.emit("vector", lambda v: v.memset(cx.sb["onescol"][:], 1.0),
               writes=["onescol"], tag="onescol")
        cx.sbuf("ones_row", [1, 128], F32)
        P.emit("vector", lambda v: v.memset(cx.sb["ones_row"][:], 1.0),
               writes=["ones_row"], tag="onesrow")
        cx.sbuf("epscol", [128, 1], F32)
        P.emit("vector", lambda v: v.memset(cx.sb["epscol"][:], 1e-5),
               writes=["epscol"], tag="epscol")
        cx.sbuf("hgc_g", [128, 8], F32)
        cx.sbuf("ones_row_b", [1, 128], BF16)
        P.emit("vector", lambda v: v.memset(cx.sb["ones_row_b"][:], 1.0),
               writes=["ones_row_b"], tag="onesrowb")
        identb = cx.sbuf("identb", [128, 128], BF16)
        P.emit("gpsimd", lambda e: e.dma_start(out=identb[:], in_=cx.dram["c_ident"]),
               writes=["identb"], dma=True, tag="ld_ident")

        if "ln0" in phases:
            emit_ln_input(cx)
            P.barrier()
        resid, xn = "x_in", "xn0"
        if "attn_a" in phases:
            emit_attn_a(cx, x_in=resid, xn_in=xn, x_out="xr1", xn_out="xn1")
            resid, xn = "xr1", "xn1"
        if "ffn1" in phases:
            nxt_xn = None if phases[-1] == "ffn1" else "xn0"
            xo = "out" if phases[-1] == "ffn1" else "xr2"
            emit_ffn(cx, x_in=resid, xn_in=xn, x_out=xo, xn_out=nxt_xn, idx=1)
            resid, xn = xo, nxt_xn
        if "attn_b" in phases:
            emit_attn_b(cx, x_in=resid, xn_in=xn, x_out="xr1", xn_out="xn1")
            resid, xn = "xr1", "xn1"
        if "ffn2" in phases:
            emit_ffn(cx, x_in=resid, xn_in=xn, x_out="out", xn_out=None, idx=2)
            resid = "out"
        if resid != "out":
            emit_copy_dram(cx, resid, "out")
        P.emit("sync", lambda e: None,
               reads=[f"out:{c}" for c in range(T // TCH)], tag="final_wait")
        P.finalize_and_emit(nc)
    return nc, extra


def kernel(**inputs):
    hp = host_prep(inputs)
    nc, extra = build_program(hp)
    x = np.asarray(inputs["L0_attn_a_x"], dtype=np.float32)
    in_maps = []
    for b in range(NCORE):
        m = dict(extra)
        m["x_t"] = _cc(x[b].T)
        in_maps.append(m)
    res = run_bass_kernel_spmd(nc, in_maps, core_ids=list(range(NCORE)))
    outs = [np.asarray(res.results[b]["out_t"]).T for b in range(NCORE)]
    return np.stack(outs, axis=0).astype(np.float32)
